# version 03
# baseline (speedup 1.0000x reference)
import math
import numpy as np
import concourse.bass as bass
import concourse.mybir as mybir

F32 = mybir.dt.float32
BF16 = mybir.dt.bfloat16
AF = mybir.ActivationFunctionType
ALU = mybir.AluOpType
AX = mybir.AxisListType

N_CORES = 8
D = 1024
SEQ = 16384
TPC = SEQ // N_CORES
STRIP = 1024
N_STRIPS = SEQ // STRIP


class Sem:

    def __init__(self, nc, name):
        self.name = name
        self._cm = nc.semaphore(name)
        self.h = self._cm.__enter__()
        self.val = 0

    def inc(self, amount):
        self.val += amount
        return (self, amount)


class Prog:

    ENGINES = ("tensor", "vector", "scalar", "gpsimd", "sync")
    CHAINED = ("vector", "scalar", "gpsimd")

    def __init__(self, nc):
        self.nc = nc
        self.items = {e: [] for e in self.ENGINES}
        self.chain = {e: Sem(nc, f"chain_{e}") for e in self.CHAINED}
        self.last = {e: None for e in self.CHAINED}

    def emit(self, eng, fn, waits=(), inc=None, is_dma=False):
        waits = list(waits)
        if eng in self.CHAINED and not is_dma:
            if self.last[eng] is not None:
                waits.append(self.last[eng])
            if inc is None:
                inc = self.chain[eng].inc(1)
                sem_obj = self.chain[eng]
            else:
                sem_obj = inc[0]
            self.last[eng] = (sem_obj, sem_obj.val)
        self.items[eng].append((fn, waits, inc))

    def dma(self, eng, out, in_, sem, waits=(), **kw):
        inc = sem.inc(16)
        self.emit(eng, lambda e: e.dma_start(out=out, in_=in_, **kw), waits, inc, is_dma=True)
        return sem.val

    def barrier_event(self, eng):
        return self.last.get(eng)

    def run(self):
        nc = self.nc
        with nc.Block() as block:
            for eng_name in self.ENGINES:
                items = self.items[eng_name]
                if not items:
                    continue

                def mk(items):
                    def body(eng):
                        for fn, waits, inc in items:
                            for s, v in waits:
                                eng.wait_ge(s.h if isinstance(s, Sem) else s, v)
                            ins = fn(eng)
                            if inc is not None:
                                sem, amt = inc
                                ins.then_inc(sem.h if isinstance(sem, Sem) else sem, amt)
                    return body

                getattr(block, eng_name)(mk(items))


def ceil_div(a, b):
    return (a + b - 1) // b


class LoadRing:

    def __init__(self, nc, prog, name, k, queue="sync"):
        self.prog = prog
        self.k = k
        self.queue = queue
        self.ld = [Sem(nc, f"{name}_ld{i}") for i in range(k)]
        self.n = 0
        self.uses = [0] * k

    def load(self, out, in_, extra_waits=()):
        slot = self.n % self.k
        self.prog.dma(self.queue, out, in_, self.ld[slot], list(extra_waits))
        self.uses[slot] += 1
        self.n += 1
        return slot, 16 * self.uses[slot]

    def consumer_wait(self, slot_info):
        slot, v = slot_info
        return (self.ld[slot], v)



def l0_qtile_of(core, slot):
    return core + 8 * slot


def l0_core_rows(core):
    rows = []
    for k in range(16):
        t = l0_qtile_of(core, k)
        rows.extend(range(128 * t, 128 * (t + 1)))
    return np.array(rows)


def l0_masks(core):
    f = np.arange(512)[None, :]
    pp = np.arange(128)[:, None]
    if core < 4:
        off = 128 * core
        maskA = (f <= off + pp).astype(np.float32)
        maskB = np.zeros((128, 512), np.float32)
    else:
        off = 128 * (core - 4)
        maskA = np.ones((128, 512), np.float32)
        maskB = (f <= off + pp).astype(np.float32)
    import ml_dtypes
    return np.concatenate([maskA, maskB], axis=1).astype(ml_dtypes.bfloat16)



def emit_l0_proj(nc, prog, x_scat, x_cont, w_aps, b_aps, out_q, out_k, out_v, name="l0p"):
    st = nc.sbuf_tensor
    pt = nc.psum_tensor
    allocs = [
        st(name + "_wld", [128, 1024 * 2], F32),
        st(name + "_wbf", [128, 1024 * 2], BF16),
        st(name + "_wt", [128, 3 * 8192], BF16),
        st(name + "_xq", [128, 16384], BF16),
        st(name + "_xc", [128, 16384], BF16),
        st(name + "_idf", [128, 128], F32),
        st(name + "_idb", [128, 128], BF16),
        st(name + "_iota", [128, 256], F32),
        st(name + "_bias", [1, 3 * 1024], BF16),
        st(name + "_biasf", [1, 3 * 1024], F32),
        st(name + "_ones", [1, 512], BF16),
        st(name + "_ev", [128, 512 * 3], BF16),
        pt(name + "_ptrA", [128, 512], BF16),
        pt(name + "_ptrB", [128, 512], BF16),
        pt(name + "_pmm", [128, 512 * 3], F32),
    ]
    hs = [a.__enter__() for a in allocs]
    (wld, wbf, wt, xq, xc, idf, idb, iota, bias, biasf, ones, ev, ptrA, ptrB, pmm) = hs
    ptr = (ptrA, ptrB)

    s_misc = Sem(nc, name + "_misc")
    s_miscv = Sem(nc, name + "_miscv")
    s_bias = Sem(nc, name + "_bias")
    s_bld = Sem(nc, name + "_bld")
    s_cast = Sem(nc, name + "_cast")
    s_tr = Sem(nc, name + "_tr")
    s_trev = Sem(nc, name + "_trev")
    s_mm = Sem(nc, name + "_mm")
    s_evb = [Sem(nc, name + f"_evb{b}") for b in range(3)]
    s_outb = [Sem(nc, name + f"_outb{b}") for b in range(3)]
    ring = LoadRing(nc, prog, name + "_in", 2, "sync")

    prog.emit("gpsimd", lambda g: g.iota(iota[:, 0:128], pattern=[[1, 128]], base=0,
                                         channel_multiplier=0, allow_small_or_imprecise_dtypes=True),
              (), s_misc.inc(1))
    prog.emit("gpsimd", lambda g: g.iota(iota[:, 128:256], pattern=[[0, 128]], base=0,
                                         channel_multiplier=1, allow_small_or_imprecise_dtypes=True),
              (), s_misc.inc(1))
    v0 = s_misc.val
    prog.emit("vector", lambda v: v.tensor_tensor(idf[:], iota[:, 0:128], iota[:, 128:256], ALU.is_equal),
              [(s_misc, v0)], s_miscv.inc(1))
    prog.emit("vector", lambda v: v.tensor_copy(idb[:], idf[:]), (), s_miscv.inc(1))
    prog.emit("vector", lambda v: v.memset(ones[:], 1.0), (), s_miscv.inc(1))
    id_ready = 2
    ones_ready = 3

    for bi, b_ap in enumerate(b_aps):
        prog.dma("sync", biasf[0:1, 1024 * bi:1024 * (bi + 1)],
                 b_ap.rearrange("(o k) -> o k", o=1), s_bld)
    prog.emit("scalar", lambda s: s.copy(bias[:], biasf[:]), [(s_bld, s_bld.val)], s_bias.inc(1))

    tr_groups = 0

    def chunk_pipeline(in_dram, dst_fn):
        nonlocal tr_groups
        n = ring.n
        buf = n % 2
        lw = [(s_cast, n - 1)] if n >= 2 else []
        sl = ring.load(wld[:, 1024 * buf:1024 * (buf + 1)], in_dram, lw)
        cw = [ring.consumer_wait(sl)]
        if n >= 2:
            cw.append((s_tr, (n - 1) * 2))
        prog.emit("scalar", lambda s, buf=buf: s.copy(wbf[:, 1024 * buf:1024 * (buf + 1)],
                                                      wld[:, 1024 * buf:1024 * (buf + 1)]),
                  cw, s_cast.inc(1))
        castv = s_cast.val
        for grp in range(2):
            bank = tr_groups % 2
            waits = [(s_cast, castv), (s_miscv, id_ready)]
            if tr_groups >= 2:
                waits.append((s_trev, tr_groups - 1))
            for jj in range(4):
                j = grp * 4 + jj
                inc = s_tr.inc(1) if jj == 3 else None
                prog.emit("tensor",
                          lambda t, buf=buf, j=j, bank=bank, jj=jj:
                          t.transpose(ptr[bank][:, 128 * jj:128 * (jj + 1)],
                                      wbf[:, 1024 * buf + 128 * j:1024 * buf + 128 * (j + 1)],
                                      idb[:]),
                          waits if jj == 0 else (), inc)
            trv = s_tr.val
            prog.emit("vector",
                      lambda v, dst_fn=dst_fn, grp=grp, bank=bank:
                      v.tensor_copy(dst_fn(grp),
                                    ptr[bank][:].rearrange("p (j t) -> p j t", j=4)),
                      [(s_tr, trv)], s_trev.inc(1))
            tr_groups += 1

    wt3 = wt[:].rearrange("p (w j t) -> p w j t", w=3, j=8)
    xq3 = xq[:].rearrange("p (j t) -> p j t", j=8)
    xc3 = xc[:].rearrange("p (j t) -> p j t", j=8)
    for wi, w_ap in enumerate(w_aps):
        for i in range(8):
            chunk_pipeline(w_ap[128 * i:128 * (i + 1), :],
                           lambda grp, wi=wi, i=i: wt3[:, wi, grp * 4:(grp + 1) * 4, 128 * i:128 * (i + 1)])
    for ti in range(16):
        chunk_pipeline(x_scat[128 * ti:128 * (ti + 1), :],
                       lambda grp, ti=ti: xq3[:, grp * 4:(grp + 1) * 4, 128 * ti:128 * (ti + 1)])
    for ti in range(16):
        chunk_pipeline(x_cont[128 * ti:128 * (ti + 1), :],
                       lambda grp, ti=ti: xc3[:, grp * 4:(grp + 1) * 4, 128 * ti:128 * (ti + 1)])

    all_trev = tr_groups
    nout = 0

    def proj_group(mm_list, evac_dst_dram):
        nonlocal nout
        bank = nout % 3
        use = nout // 3
        waits = [(s_trev, all_trev), (s_bias, 1), (s_miscv, ones_ready)]
        if use >= 1:
            waits.append((s_evb[bank], use))
        n_mm = len(mm_list)
        for mi, mmfn in enumerate(mm_list):
            inc = s_mm.inc(1) if mi == n_mm - 1 else None
            prog.emit("tensor", lambda t, mmfn=mmfn, bank=bank: mmfn(t, bank),
                      waits if mi == 0 else (), inc)
        mmv = s_mm.val
        ew = [(s_mm, mmv)]
        if use >= 1:
            ew.append((s_outb[bank], 16 * use))
        eng = ("scalar", "vector", "scalar")[bank]
        fn = (lambda e, bank=bank: e.copy(ev[:, 512 * bank:512 * (bank + 1)],
                                          pmm[:, 512 * bank:512 * (bank + 1)])) if eng == "scalar" else \
             (lambda e, bank=bank: e.tensor_copy(ev[:, 512 * bank:512 * (bank + 1)],
                                                 pmm[:, 512 * bank:512 * (bank + 1)]))
        prog.emit(eng, fn, ew, s_evb[bank].inc(1))
        prog.dma("gpsimd", evac_dst_dram, ev[:, 512 * bank:512 * (bank + 1)], s_outb[bank],
                 [(s_evb[bank], use + 1)])
        nout += 1

    for wi, xsrc, out_dram in ((0, xq3, out_q), (1, xc3, out_k)):
        for i in range(8):
            for t4 in range(4):
                mms = [lambda t, bank, wi=wi, i=i:
                       t.matmul(pmm[:, 512 * bank:512 * (bank + 1)],
                                bias[0:1, 1024 * wi + 128 * i:1024 * wi + 128 * (i + 1)],
                                ones[0:1, :], start=True, stop=False)]
                for j in range(8):
                    mms.append(lambda t, bank, wi=wi, i=i, j=j, t4=t4, xsrc=xsrc:
                               t.matmul(pmm[:, 512 * bank:512 * (bank + 1)],
                                        wt3[:, wi, j, 128 * i:128 * (i + 1)],
                                        xsrc[:, j, 512 * t4:512 * (t4 + 1)],
                                        start=False, stop=(j == 7)))
                proj_group(mms, out_dram[128 * i:128 * (i + 1), 512 * t4:512 * (t4 + 1)])
    for si in range(16):
        for h in range(2):
            mms = [lambda t, bank, h=h:
                   t.matmul(pmm[:, 512 * bank:512 * (bank + 1)],
                            ones[0:1, 0:128],
                            bias[0:1, 2048 + 512 * h:2048 + 512 * (h + 1)],
                            start=True, stop=False)]
            for j in range(8):
                mms.append(lambda t, bank, si=si, h=h, j=j:
                           t.matmul(pmm[:, 512 * bank:512 * (bank + 1)],
                                    xc3[:, j, 128 * si:128 * (si + 1)],
                                    wt3[:, 2, j, 512 * h:512 * (h + 1)],
                                    start=False, stop=(j == 7)))
            proj_group(mms, out_v[128 * si:128 * (si + 1), 512 * h:512 * (h + 1)])

    for b in range(3):
        cnt = (nout + 2 - b) // 3
        prog.emit("gpsimd", lambda g: g.engine_nop(), [(s_outb[b], 16 * cnt)], None)
    return dict(allocs=allocs)



def emit_l0_attn(nc, prog, q_own, k_fm_g, v_tm_g, msk_in, x_res, out_x1, name="l0a", dbg_phase=99):
    SC = 1.0 / 32.0
    st = nc.sbuf_tensor
    pt = nc.psum_tensor
    allocs = [
        st(name + "_q", [128, 8 * 2048], BF16),
        st(name + "_k", [128, 2 * 4096], BF16),
        st(name + "_v", [128, 2 * 4096], BF16),
        st(name + "_acc", [128, 16 * 1024], F32),
        st(name + "_m", [128, 16 * 4], F32),
        st(name + "_P", [128, 2 * 512], BF16),
        st(name + "_PT", [128, 2 * 512], BF16),
        st(name + "_msk", [128, 1024], BF16),
        st(name + "_idb", [128, 128], BF16),
        st(name + "_iota", [128, 256], F32),
        st(name + "_idf", [128, 128], F32),
        st(name + "_xr", [128, 2 * 1024], F32),
        st(name + "_fin", [128, 4 * 1024], F32),
        st(name + "_o1", [128, 2 * 1024], F32),
        st(name + "_fs", [128, 16], F32),
        pt(name + "_pS0", [128, 512], F32),
        pt(name + "_pS1", [128, 512], F32),
        pt(name + "_pT0", [128, 512], BF16),
        pt(name + "_pT1", [128, 512], BF16),
        pt(name + "_pV0", [128, 512], F32),
        pt(name + "_pV1", [128, 512], F32),
    ]
    hs = [a.__enter__() for a in allocs]
    (q_sb, k_sb, v_sb, acc, msb, P_sb, PT_sb, msk, idb, iota, idf,
     xr, fin, o1, fs, pS0, pS1, pT0, pT1, pV0, pV1) = hs
    pS = (pS0, pS1)
    pT = (pT0, pT1)

    s_qld = Sem(nc, name + "_qld")
    s_mld = Sem(nc, name + "_mld")
    s_kld = [Sem(nc, name + f"_kld{i}") for i in range(2)]
    s_vld = [Sem(nc, name + f"_vld{i}") for i in range(2)]
    s_xld = [Sem(nc, name + f"_xld{i}") for i in range(2)]
    s_pair = Sem(nc, name + "_pair")
    s_smm = Sem(nc, name + "_smm")
    s_tr = Sem(nc, name + "_tr")
    s_exp = Sem(nc, name + "_exp")
    s_ptev = Sem(nc, name + "_ptev")
    s_stat = Sem(nc, name + "_stat")
    s_lup = Sem(nc, name + "_lup")
    s_accm = Sem(nc, name + "_accm")
    s_out = [Sem(nc, name + f"_out{i}") for i in range(2)]
    s_gmisc = Sem(nc, name + "_gmisc")
    s_vmisc = Sem(nc, name + "_vmisc")

    prog.emit("gpsimd", lambda g: g.iota(iota[:, 0:128], pattern=[[1, 128]], base=0,
                                         channel_multiplier=0, allow_small_or_imprecise_dtypes=True),
              (), s_gmisc.inc(1))
    prog.emit("gpsimd", lambda g: g.iota(iota[:, 128:256], pattern=[[0, 128]], base=0,
                                         channel_multiplier=1, allow_small_or_imprecise_dtypes=True),
              (), s_gmisc.inc(1))
    prog.emit("vector", lambda v: v.tensor_tensor(idf[:], iota[:, 0:128], iota[:, 128:256], ALU.is_equal),
              [(s_gmisc, 2)], s_vmisc.inc(1))
    prog.emit("vector", lambda v: v.tensor_copy(idb[:], idf[:]), (), s_vmisc.inc(1))
    id_ready = s_vmisc.val
    prog.emit("vector", lambda v: v.memset(msb[:], 0.0), (), s_vmisc.inc(1))
    prog.emit("vector", lambda v: v.tensor_scalar(msb[:].rearrange("p (q c) -> p q c", c=4)[:, :, 0:2],
                                                  msb[:].rearrange("p (q c) -> p q c", c=4)[:, :, 0:2],
                                                  0.0, -1e30, ALU.mult, ALU.add), (), s_vmisc.inc(1))
    prog.emit("vector", lambda v: v.memset(acc[:], 0.0), (), s_vmisc.inc(1))
    state_ready = s_vmisc.val

    prog.dma("sync", msk[:], msk_in[:], s_mld)
    for j in range(8):
        prog.dma("sync", q_sb[:, 2048 * j:2048 * (j + 1)],
                 q_own[128 * j:128 * (j + 1), :], s_qld)
    q_ready = s_qld.val

    np_pair = 0
    kv_last_pair = []
    qcount = [0] * 16
    total_pairs = sum(16 - (t // 2) for t in range(32))

    for t in range(32):
        slot = t % 2
        kw = []
        if t >= 2 and dbg_phase >= 5:
            kw.append((s_pair, kv_last_pair[t - 2]))
        r, off = t // 4, (t % 4) * 512
        prog.dma("sync",
                 k_sb[:, 4096 * slot:4096 * (slot + 1)].rearrange("p (j c) -> p j c", j=8),
                 k_fm_g[1024 * r:1024 * (r + 1), off:off + 512].rearrange("(j p) c -> p j c", p=128),
                 s_kld[slot], kw)
        kldv = s_kld[slot].val
        prog.dma("sync",
                 v_sb[:, 4096 * slot:4096 * (slot + 1)].rearrange("p (s d) -> p s d", s=4),
                 v_tm_g[512 * t:512 * (t + 1), :].rearrange("(s p) d -> p s d", p=128),
                 s_vld[slot], kw)
        vldv = s_vld[slot].val

        for k in range(t // 2, 16):
            if dbg_phase < 1:
                continue
            n = np_pair
            masked = (k == t // 2)
            mask_sel = t % 2
            k_ = qcount[k]
            qcount[k] += 1
            mold_c, mnew_c = 4 * k + (k_ % 2), 4 * k + ((k_ + 1) % 2)
            mc_c, l_c = 4 * k + 2, 4 * k + 3
            sbank = n % 2
            tbank = n % 2
            vpar = n % 2
            rs_col = 2 + (n % 2)

            w = [(s_qld, q_ready), (s_kld[slot], kldv)]
            if n >= 2:
                w.append((s_exp, n - 1))
            for j in range(8):
                prog.emit("tensor",
                          lambda tt, j=j, k=k, sbank=sbank, slot=slot:
                          tt.matmul(pS[sbank][:],
                                    q_sb[:, 2048 * j + 128 * k:2048 * j + 128 * (k + 1)],
                                    k_sb[:, 4096 * slot + 512 * j:4096 * slot + 512 * (j + 1)],
                                    start=(j == 0), stop=(j == 7)),
                          w if j == 0 else (), s_smm.inc(1) if j == 7 else None)
            smmv = s_smm.val

            if dbg_phase < 2:
                np_pair += 1
                continue
            rmw = [(s_smm, smmv), (s_vmisc, state_ready)]
            if n >= 1:
                rmw.append((s_exp, n))
            prog.emit("vector", lambda v, sbank=sbank, mc_c=mc_c:
                      v.reduce_max(msb[:, mc_c:mc_c + 1], pS[sbank][:], axis=AX.X),
                      rmw, None)
            prog.emit("vector", lambda v, mc_c=mc_c: v.tensor_scalar(
                          msb[:, mc_c:mc_c + 1], msb[:, mc_c:mc_c + 1], SC, None, ALU.mult),
                      (), None)
            prog.emit("vector", lambda v, mc_c=mc_c, mold_c=mold_c, mnew_c=mnew_c:
                      v.tensor_tensor(msb[:, mnew_c:mnew_c + 1], msb[:, mold_c:mold_c + 1],
                                      msb[:, mc_c:mc_c + 1], ALU.max),
                      (), None)
            prog.emit("vector", lambda v, mc_c=mc_c, mnew_c=mnew_c: v.tensor_scalar(
                          msb[:, mc_c:mc_c + 1], msb[:, mnew_c:mnew_c + 1], -1.0, None, ALU.mult),
                      (), s_stat.inc(1))
            statv = s_stat.val
            prog.emit("scalar", lambda s, mold_c=mold_c, mc_c=mc_c:
                      s.activation(msb[:, mold_c:mold_c + 1], msb[:, mold_c:mold_c + 1],
                                   AF.Exp, bias=msb[:, mc_c:mc_c + 1], scale=1.0),
                      [(s_stat, statv)], None)

            if dbg_phase < 3:
                np_pair += 1
                continue
            pw = []
            if n >= 2:
                pw.append((s_tr, n - 1))
                pw.append((s_lup, n - 1))
            if masked:
                prog.emit("scalar", lambda s, sbank=sbank, vpar=vpar, mc_c=mc_c:
                          s.activation(P_sb[:, 512 * vpar:512 * (vpar + 1)], pS[sbank][:],
                                       AF.Exp, bias=msb[:, mc_c:mc_c + 1], scale=SC),
                          pw, s_exp.inc(1))
                expv = s_exp.val
                prog.emit("vector", lambda v, vpar=vpar, mask_sel=mask_sel:
                          v.tensor_tensor(P_sb[:, 512 * vpar:512 * (vpar + 1)],
                                          P_sb[:, 512 * vpar:512 * (vpar + 1)],
                                          msk[:, 512 * mask_sel:512 * (mask_sel + 1)], ALU.mult),
                          [(s_exp, expv), (s_mld, 16)], None)
                prog.emit("vector", lambda v, vpar=vpar, rs_col=rs_col:
                          v.reduce_sum(fs[:, rs_col:rs_col + 1],
                                       P_sb[:, 512 * vpar:512 * (vpar + 1)], axis=AX.X),
                          (), s_stat.inc(1))
            else:
                prog.emit("scalar", lambda s, sbank=sbank, vpar=vpar, mc_c=mc_c, rs_col=rs_col:
                          s.activation(P_sb[:, 512 * vpar:512 * (vpar + 1)], pS[sbank][:],
                                       AF.Exp, bias=msb[:, mc_c:mc_c + 1], scale=SC,
                                       accum_out=fs[:, rs_col:rs_col + 1]),
                          pw, s_exp.inc(1))
                prog.emit("vector", lambda v: v.engine_nop(), [(s_exp, s_exp.val)], s_stat.inc(1))
            rsv = s_stat.val

            prog.emit("vector", lambda v, l_c=l_c, mold_c=mold_c, rs_col=rs_col:
                      v.scalar_tensor_tensor(msb[:, l_c:l_c + 1], msb[:, l_c:l_c + 1],
                                             msb[:, mold_c:mold_c + 1], fs[:, rs_col:rs_col + 1],
                                             ALU.mult, ALU.add),
                      [(s_exp, n + 1)], s_lup.inc(1))

            if dbg_phase < 4:
                np_pair += 1
                continue
            tw = [(s_stat, rsv), (s_vmisc, id_ready)]
            if n >= 2:
                tw.append((s_ptev, n - 1))
            for k4 in range(4):
                prog.emit("tensor",
                          lambda tt, k4=k4, vpar=vpar, tbank=tbank:
                          tt.transpose(pT[tbank][:, 128 * k4:128 * (k4 + 1)],
                                       P_sb[:, 512 * vpar + 128 * k4:512 * vpar + 128 * (k4 + 1)],
                                       idb[:]),
                          tw if k4 == 0 else (), s_tr.inc(1) if k4 == 3 else None)
            trv = s_tr.val
            prog.emit("scalar", lambda s, tbank=tbank, vpar=vpar:
                      s.copy(PT_sb[:, 512 * vpar:512 * (vpar + 1)], pT[tbank][:]),
                      [(s_tr, trv)], s_ptev.inc(1))
            ptevv = s_ptev.val

            if dbg_phase < 5:
                np_pair += 1
                continue
            vw = [(s_ptev, ptevv), (s_vld[slot], vldv)]
            if n >= 1:
                vw.append((s_accm, 2 * n))
            for h in range(2):
                for k4 in range(4):
                    pv = (pV0, pV1)[h]
                    prog.emit("tensor",
                              lambda tt, h=h, k4=k4, vpar=vpar, slot=slot, pv=pv:
                              tt.matmul(pv[:],
                                        PT_sb[:, 512 * vpar + 128 * k4:512 * vpar + 128 * (k4 + 1)],
                                        v_sb[:, 4096 * slot + 1024 * k4 + 512 * h:4096 * slot + 1024 * k4 + 512 * (h + 1)],
                                        start=(k4 == 0), stop=(k4 == 3)),
                              vw if (h == 0 and k4 == 0) else (),
                              s_pair.inc(1) if (h == 1 and k4 == 3) else None)
            pairv = s_pair.val

            for h in range(2):
                pv = (pV0, pV1)[h]
                prog.emit("vector",
                          lambda v, k=k, h=h, mold_c=mold_c, pv=pv:
                          v.scalar_tensor_tensor(acc[:, 1024 * k + 512 * h:1024 * k + 512 * (h + 1)],
                                                 acc[:, 1024 * k + 512 * h:1024 * k + 512 * (h + 1)],
                                                 msb[:, mold_c:mold_c + 1], pv[:],
                                                 ALU.mult, ALU.add),
                          [(s_pair, pairv)] if h == 0 else (), s_accm.inc(1))
            np_pair += 1
        kv_last_pair.append(np_pair)

    if dbg_phase < 6:
        sdone = Sem(nc, name + "_dbgdone")
        for k in range(16):
            prog.dma("sync", xr[:, 0:1024], x_res[128 * k:128 * (k + 1), :], s_xld[0],
                     [(s_out[0], 16 * k)] if k >= 1 else [])
            prog.dma("gpsimd", out_x1[128 * k:128 * (k + 1), :], xr[:, 0:1024], s_out[0],
                     [(s_xld[0], 16 * (k + 1))])
        if dbg_phase >= 1:
            prog.emit("gpsimd", lambda g: g.engine_nop(), [(s_pair, np_pair)] if dbg_phase >= 5 else [], None)
        prog.emit("gpsimd", lambda g: g.engine_nop(), [(s_out[0], 16 * 16)], None)
        return dict(allocs=allocs)

    fin_ev = []
    n_fin = 0
    for k in range(16):
        l_c = 4 * k + 3
        par = n_fin % 2
        row0 = 128 * k
        xw = []
        if n_fin >= 2:
            xw.append(fin_ev[n_fin - 2])
        prog.dma("sync", xr[:, 1024 * par:1024 * (par + 1)], x_res[row0:row0 + 128, :],
                 s_xld[par], xw)
        xldv = s_xld[par].val
        prog.emit("vector", lambda v, l_c=l_c: v.reciprocal(fs[:, 6:7], msb[:, l_c:l_c + 1]),
                  [(s_accm, 2 * total_pairs)], None)
        prog.emit("vector", lambda v, k=k: v.tensor_scalar(
                      acc[:, 1024 * k:1024 * (k + 1)], acc[:, 1024 * k:1024 * (k + 1)],
                      fs[:, 6:7], None, ALU.mult), (), None)
        prog.emit("vector", lambda v, k=k, par=par: v.tensor_tensor(
                      fin[:, 2048 * par:2048 * par + 1024], acc[:, 1024 * k:1024 * (k + 1)],
                      xr[:, 1024 * par:1024 * (par + 1)], ALU.add),
                  [(s_xld[par], xldv)], None)
        prog.emit("vector", lambda v, par=par: v.reduce_sum(
                      fs[:, 0:1], fin[:, 2048 * par:2048 * par + 1024], axis=AX.X), (), None)
        prog.emit("vector", lambda v: v.tensor_scalar(fs[:, 0:1], fs[:, 0:1], 1.0 / 1024, None, ALU.mult),
                  (), None)
        prog.emit("vector", lambda v, par=par: v.tensor_tensor(
                      fin[:, 2048 * par + 1024:2048 * par + 2048],
                      fin[:, 2048 * par:2048 * par + 1024],
                      fin[:, 2048 * par:2048 * par + 1024], ALU.mult), (), None)
        prog.emit("vector", lambda v, par=par: v.reduce_sum(
                      fs[:, 1:2], fin[:, 2048 * par + 1024:2048 * par + 2048], axis=AX.X), (), None)
        prog.emit("vector", lambda v: v.tensor_tensor(fs[:, 2:3], fs[:, 0:1], fs[:, 0:1], ALU.mult), (), None)
        prog.emit("vector", lambda v: v.scalar_tensor_tensor(fs[:, 1:2], fs[:, 1:2], 1.0 / 1024,
                                                             fs[:, 2:3], ALU.mult, ALU.subtract),
                  (), None)
        prog.emit("vector", lambda v: v.tensor_scalar(fs[:, 1:2], fs[:, 1:2], 1e-5, None, ALU.add),
                  (), s_stat.inc(1))
        vstat = s_stat.val
        prog.emit("scalar", lambda s: s.activation(fs[:, 3:4], fs[:, 1:2], AF.Ln),
                  [(s_stat, vstat)], None)
        prog.emit("scalar", lambda s: s.activation(fs[:, 4:5], fs[:, 3:4], AF.Exp, scale=-0.5),
                  (), s_exp.inc(1))
        rstdv = s_exp.val
        ow = [(s_exp, rstdv)]
        if n_fin >= 2:
            ow.append((s_out[par], 16 * (n_fin // 2)))
        prog.emit("vector", lambda v, par=par: v.tensor_scalar(
                      o1[:, 1024 * par:1024 * (par + 1)], fin[:, 2048 * par:2048 * par + 1024],
                      fs[:, 0:1], fs[:, 4:5], ALU.subtract, ALU.mult),
                  ow, s_accm.inc(1))
        fin_ev.append((s_accm, s_accm.val))
        prog.dma("gpsimd", out_x1[row0:row0 + 128, :], o1[:, 1024 * par:1024 * (par + 1)],
                 s_out[par], [fin_ev[-1]])
        n_fin += 1

    prog.emit("gpsimd", lambda g: g.engine_nop(), [(s_out[0], 16 * 8), (s_out[1], 16 * 8)], None)
    return dict(allocs=allocs)



def emit_l0_ffn(nc, prog, x1_ap, wg1, bg1, wg2, w_in, w_out, out_x2, out_x2fm, name="l0f"):
    st = nc.sbuf_tensor
    pt = nc.psum_tensor
    allocs = [
        st(name + "_x1fm", [128, 16384], BF16),
        st(name + "_yacc", [128, 16 * 1024], F32),
        st(name + "_hh", [128, 8 * 2048], BF16),
        st(name + "_wi", [128, 8 * 2048], BF16),
        st(name + "_wo", [128, 16 * 1024], BF16),
        st(name + "_stg", [128, 3072], F32),
        st(name + "_gate", [128, 16 * 8], F32),
        st(name + "_ones", [1, 512], BF16),
        st(name + "_idb", [128, 128], BF16),
        st(name + "_idf", [128, 128], F32),
        st(name + "_iota", [128, 256], F32),
        st(name + "_fs", [128, 16], F32),
        pt(name + "_pA", [128, 512 * 8], F32),
    ]
    hs = [a.__enter__() for a in allocs]
    (x1fm, yacc, hh, wi, wo, stg, gate, ones,
     idb, idf, iota, fs, pA) = hs
    ptrA = pA[:, 512 * 4:512 * 4 + 256].bitcast(BF16)
    ptrB = pA[:, 512 * 5:512 * 5 + 256].bitcast(BF16)
    smf = stg
    gfm = wo[:, 0:4096]
    wg1s = wo[:, 4096:6144]
    wg2s = wo[:, 6144:6160]
    bg = wo[0:1, 6160:6416]
    gsc = yacc[:, 0:512]
    ptr = (ptrA, ptrB)

    s_ld = [Sem(nc, name + f"_ld{i}") for i in range(2)]
    s_cast = Sem(nc, name + "_cast")
    s_gcast = Sem(nc, name + "_gcast")
    s_tr = Sem(nc, name + "_tr")
    s_trev = Sem(nc, name + "_trev")
    s_mm = Sem(nc, name + "_mm")
    s_hev = Sem(nc, name + "_hev")
    s_mrg = Sem(nc, name + "_mrg")
    s_gst = Sem(nc, name + "_gst")
    s_sexp = Sem(nc, name + "_sexp")
    s_bld = Sem(nc, name + "_bld")
    s_xld = [Sem(nc, name + f"_xld{i}") for i in range(2)]
    s_out = [Sem(nc, name + f"_out{i}") for i in range(2)]
    s_outf = [Sem(nc, name + f"_outf{i}") for i in range(2)]
    s_gmisc = Sem(nc, name + "_gmisc")
    s_vmisc = Sem(nc, name + "_vmisc")

    prog.emit("gpsimd", lambda g: g.iota(iota[:, 0:128], pattern=[[1, 128]], base=0,
                                         channel_multiplier=0, allow_small_or_imprecise_dtypes=True),
              (), s_gmisc.inc(1))
    prog.emit("gpsimd", lambda g: g.iota(iota[:, 128:256], pattern=[[0, 128]], base=0,
                                         channel_multiplier=1, allow_small_or_imprecise_dtypes=True),
              (), s_gmisc.inc(1))
    prog.emit("vector", lambda v: v.tensor_tensor(idf[:], iota[:, 0:128], iota[:, 128:256], ALU.is_equal),
              [(s_gmisc, 2)], s_vmisc.inc(1))
    prog.emit("vector", lambda v: v.tensor_copy(idb[:], idf[:]), (), s_vmisc.inc(1))
    id_ready = s_vmisc.val
    prog.emit("vector", lambda v: v.memset(ones[:], 1.0), (), s_vmisc.inc(1))
    ones_ready = s_vmisc.val

    for j in range(8):
        prog.dma("sync", smf[:, 256 * j:256 * (j + 1)], wg1[128 * j:128 * (j + 1), :], s_bld)
    for j2 in range(2):
        prog.dma("sync", smf[:, 2048 + 8 * j2:2048 + 8 * (j2 + 1)],
                 wg2[128 * j2:128 * (j2 + 1), :], s_bld)
    prog.dma("sync", smf[0:1, 2064:2064 + 256], bg1.rearrange("(o k) -> o k", o=1), s_bld)
    bldv = s_bld.val
    prog.emit("scalar", lambda s: s.copy(wg1s[:], smf[:, 0:2048]), [(s_bld, bldv)], s_cast.inc(1))
    prog.emit("scalar", lambda s: s.copy(wg2s[:], smf[:, 2048:2064]), (), s_cast.inc(1))
    prog.emit("scalar", lambda s: s.copy(bg[:], smf[0:1, 2064:2064 + 256]), (), s_cast.inc(1))
    gatew_ready = s_cast.val

    tr_groups = 0
    x1fm3 = x1fm[:].rearrange("p (j t) -> p j t", j=8)
    for ti in range(16):
        par = ti % 2
        lw = [(s_gcast, ti - 1)] if ti >= 2 else [(s_cast, gatew_ready)]
        prog.dma("sync", stg[:, 1024 * par:1024 * (par + 1)], x1_ap[128 * ti:128 * (ti + 1), :],
                 s_ld[par], lw)
        ldv = s_ld[par].val
        cw = [(s_ld[par], ldv)]
        if ti >= 2:
            cw.append((s_tr, (ti - 1) * 2))
        prog.emit("gpsimd", lambda g, par=par: g.tensor_copy(hh[:, 1024 * par:1024 * (par + 1)],
                                                             stg[:, 1024 * par:1024 * (par + 1)]),
                  cw, s_gcast.inc(1))
        castv = s_gcast.val
        for grp in range(2):
            bank = tr_groups % 2
            waits = [(s_gcast, castv), (s_vmisc, id_ready)]
            if tr_groups >= 2:
                waits.append((s_trev, tr_groups - 1))
            for jj in range(4):
                j = grp * 4 + jj
                prog.emit("tensor",
                          lambda t, par=par, j=j, bank=bank, jj=jj:
                          t.transpose(ptr[bank][:, 128 * jj:128 * (jj + 1)],
                                      hh[:, 1024 * par + 128 * j:1024 * par + 128 * (j + 1)],
                                      idb[:]),
                          waits if jj == 0 else (), s_tr.inc(1) if jj == 3 else None)
            prog.emit("vector",
                      lambda v, ti=ti, grp=grp, bank=bank:
                      v.tensor_copy(x1fm3[:, grp * 4:(grp + 1) * 4, 128 * ti:128 * (ti + 1)],
                                    ptr[bank][:].rearrange("p (j t) -> p j t", j=4)),
                      [(s_tr, s_tr.val)], s_trev.inc(1))
            tr_groups += 1
    x1fm_ready = tr_groups

    nmm = 0
    wg1s3 = wg1s[:].rearrange("p (j c) -> p j c", j=8)
    for i2 in range(2):
        for t4 in range(4):
            bank = nmm % 2
            w0 = [(s_trev, x1fm_ready), (s_cast, gatew_ready), (s_vmisc, ones_ready)]
            if nmm >= 2:
                w0.append((s_hev, nmm - 1))
            prog.emit("tensor", lambda t, i2=i2, bank=bank:
                      t.matmul(pA[:, 512 * bank:512 * (bank + 1)],
                               bg[0:1, 128 * i2:128 * (i2 + 1)], ones[0:1, :],
                               start=True, stop=False), w0, None)
            for j in range(8):
                prog.emit("tensor", lambda t, i2=i2, j=j, t4=t4, bank=bank:
                          t.matmul(pA[:, 512 * bank:512 * (bank + 1)],
                                   wg1s3[:, j, 128 * i2:128 * (i2 + 1)],
                                   x1fm3[:, j, 512 * t4:512 * (t4 + 1)],
                                   start=False, stop=(j == 7)),
                          (), s_mm.inc(1) if j == 7 else None)
            prog.emit("scalar", lambda s, i2=i2, t4=t4, bank=bank:
                      s.activation(gfm[:, 2048 * i2 + 512 * t4:2048 * i2 + 512 * (t4 + 1)],
                                   pA[:, 512 * bank:512 * (bank + 1)], AF.Gelu),
                      [(s_mm, s_mm.val)], s_hev.inc(1))
            nmm += 1
    gfm_ready = s_hev.val

    nlg = 0
    for sub in range(16):
        bank = 2 + (nlg % 2)
        G = gsc[:, 32 * sub:32 * (sub + 1)]
        w0 = [(s_hev, gfm_ready)]
        if nlg >= 2:
            w0.append((s_sexp, nlg - 1))
        for j2 in range(2):
            prog.emit("tensor", lambda t, sub=sub, j2=j2, bank=bank:
                      t.matmul(pA[:, 512 * bank:512 * bank + 8],
                               gfm[:, 2048 * j2 + 128 * sub:2048 * j2 + 128 * (sub + 1)],
                               wg2s[:, 8 * j2:8 * (j2 + 1)],
                               start=(j2 == 0), stop=(j2 == 1)),
                      w0 if j2 == 0 else (), s_mm.inc(1) if j2 == 1 else None)
        mmv = s_mm.val
        pslice = pA[:, 512 * bank:512 * bank + 8]
        prog.emit("vector", lambda v, pslice=pslice, G=G: v.reduce_max(G[:, 0:1], pslice, axis=AX.X),
                  [(s_mm, mmv)], None)
        prog.emit("vector", lambda v, G=G: v.tensor_scalar(G[:, 1:2], G[:, 0:1], -1.0, None, ALU.mult),
                  (), s_gst.inc(1))
        prog.emit("scalar", lambda s, pslice=pslice, G=G:
                  s.activation(G[:, 6:14], pslice, AF.Exp, bias=G[:, 1:2], accum_out=G[:, 2:3]),
                  [(s_gst, s_gst.val)], s_sexp.inc(1))
        prog.emit("vector", lambda v, G=G: v.reciprocal(G[:, 3:4], G[:, 2:3]),
                  [(s_sexp, s_sexp.val)], None)
        prog.emit("vector", lambda v, G=G: v.tensor_scalar(G[:, 6:14], G[:, 6:14], G[:, 3:4], None, ALU.mult),
                  (), None)
        prog.emit("vector", lambda v, G=G: v.reduce_max(G[:, 4:5], G[:, 6:14], axis=AX.X), (), None)
        prog.emit("vector", lambda v, G=G: v.tensor_scalar(G[:, 14:22], G[:, 6:14], G[:, 4:5], None, ALU.is_ge),
                  (), None)
        prog.emit("vector", lambda v, G=G: v.scalar_tensor_tensor(G[:, 22:30], G[:, 14:22], -1e9,
                                                                  G[:, 6:14], ALU.mult, ALU.add), (), None)
        prog.emit("vector", lambda v, G=G: v.reduce_max(G[:, 5:6], G[:, 22:30], axis=AX.X), (), None)
        prog.emit("vector", lambda v, G=G: v.tensor_scalar(G[:, 22:30], G[:, 22:30], G[:, 5:6], None, ALU.is_ge),
                  (), None)
        prog.emit("vector", lambda v, G=G: v.tensor_add(G[:, 14:22], G[:, 14:22], G[:, 22:30]), (), None)
        prog.emit("vector", lambda v, G=G, sub=sub: v.tensor_mul(gate[:, 8 * sub:8 * (sub + 1)],
                                                                 G[:, 6:14], G[:, 14:22]),
                  (), s_gst.inc(1))
        nlg += 1
    gates_ready = s_gst.val

    s_wild = [Sem(nc, name + f"_wild{i}") for i in range(2)]
    s_wold = [Sem(nc, name + f"_wold{i}") for i in range(2)]
    s_wic = Sem(nc, name + "_wic")
    s_woc = Sem(nc, name + "_woc")
    n_h = 0
    n_o = 0
    wi_casts = 0
    wo_casts = 0
    hh3 = hh[:].rearrange("p (io t) -> p io t", io=8)

    def load_wi(e):
        nonlocal wi_casts
        for jh in range(16):
            j, hhf = jh // 2, jh % 2
            par = jh % 2
            lw = []
            if e == 0 and jh < 2:
                lw.append((s_gcast, 16))
            if e * 16 + jh >= 2:
                lw.append((s_wic, wi_casts - 1))
            if e > 0:
                lw.append((s_mm, hmm_done[e - 1]))
            prog.dma("sync", stg[:, 1024 * par:1024 * (par + 1)],
                     w_in[e].rearrange("(c p) h -> c p h", p=128)[j][:, 1024 * hhf:1024 * (hhf + 1)],
                     s_wild[par], lw)
            prog.emit("scalar", lambda s, par=par, j=j, hhf=hhf:
                      s.copy(wi[:, 2048 * j + 1024 * hhf:2048 * j + 1024 * (hhf + 1)],
                             stg[:, 1024 * par:1024 * (par + 1)]),
                      [(s_wild[par], 16 * ((e * 16 + jh) // 2 + 1))], s_wic.inc(1))
            wi_casts += 1

    def load_wo(e, logits_done):
        nonlocal wo_casts
        for i in range(16):
            lw = []
            if e * 16 + i >= 1:
                lw.append((s_woc, wo_casts))
            if e > 0:
                lw.append((s_mm, omm_done[e - 1]))
            prog.dma("gpsimd", stg[:, 2048:3072],
                     w_out[e].rearrange("(c p) h -> c p h", p=128)[i], s_wold[0], lw)
            cw = [(s_wold[0], 16 * (e * 16 + i + 1))]
            if e == 0 and i <= 6:
                cw.append((s_mm, logits_done))
            prog.emit("gpsimd", lambda g, i=i:
                      g.tensor_copy(wo[:, 1024 * i:1024 * (i + 1)], stg[:, 2048:3072]),
                      cw, s_woc.inc(1))
            wo_casts += 1

    hmm_done = {}
    omm_done = {}
    hev_count = s_hev.val
    mrg_count = s_mrg.val

    logits_done = s_mm.val
    for e in range(8):
        load_wi(e)
        load_wo(e, logits_done)
        wic_ready = wi_casts
        woc_ready = wo_casts
        for hf in range(2):
            for io in range(8):
                i_abs = 8 * hf + io
                for t4 in range(4):
                    bank = 6 + (n_h % 2)
                    w0 = [(s_wic, wic_ready), (s_trev, x1fm_ready)]
                    if n_h >= 2:
                        w0.append((s_hev, hev_count + n_h - 1))
                    if n_o >= 1:
                        w0.append((s_mrg, mrg_count + 8 * n_o))
                    for j in range(8):
                        prog.emit("tensor",
                                  lambda t, j=j, i_abs=i_abs, t4=t4, bank=bank:
                                  t.matmul(pA[:, 512 * bank:512 * (bank + 1)],
                                           wi[:, 2048 * j + 128 * i_abs:2048 * j + 128 * (i_abs + 1)],
                                           x1fm3[:, j, 512 * t4:512 * (t4 + 1)],
                                           start=(j == 0), stop=(j == 7)),
                                  w0 if j == 0 else (), s_mm.inc(1) if j == 7 else None)
                    prog.emit("scalar", lambda s, io=io, t4=t4, bank=bank:
                              s.activation(hh3[:, io, 512 * t4:512 * (t4 + 1)],
                                           pA[:, 512 * bank:512 * (bank + 1)], AF.Gelu),
                              [(s_mm, s_mm.val)], s_hev.inc(1))
                    n_h += 1
            hphase_done = s_hev.val
            for t4 in range(4):
                grp_banks = []
                w0 = [(s_hev, hphase_done), (s_woc, woc_ready), (s_gst, gates_ready)]
                if n_o >= 1:
                    w0.append((s_mrg, mrg_count + 8 * n_o))
                first = True
                for io in range(8):
                    i_abs = 8 * hf + io
                    for sub in range(4):
                        for h2 in range(2):
                            bank = sub * 2 + h2
                            prog.emit("tensor",
                                      lambda t, io=io, i_abs=i_abs, t4=t4, sub=sub, h2=h2, bank=bank:
                                      t.matmul(pA[:, 512 * bank:512 * (bank + 1)],
                                               hh3[:, io, 512 * t4 + 128 * sub:512 * t4 + 128 * (sub + 1)],
                                               wo[:, 1024 * i_abs + 512 * h2:1024 * i_abs + 512 * (h2 + 1)],
                                               start=(io == 0), stop=(io == 7)),
                                      w0 if first else (),
                                      s_mm.inc(1) if (io == 7 and sub == 3 and h2 == 1) else None)
                            first = False
                mmv = s_mm.val
                for sub in range(4):
                    for h2 in range(2):
                        bank = sub * 2 + h2
                        s16 = 4 * t4 + sub
                        dst = yacc[:, 1024 * s16 + 512 * h2:1024 * s16 + 512 * (h2 + 1)]
                        gap = gate[:, 8 * s16 + e:8 * s16 + e + 1]
                        if e == 0 and hf == 0:
                            prog.emit("vector", lambda v, dst=dst, gap=gap, bank=bank:
                                      v.tensor_scalar(dst, pA[:, 512 * bank:512 * (bank + 1)],
                                                      gap, None, ALU.mult),
                                      [(s_mm, mmv)] if (sub == 0 and h2 == 0) else (), s_mrg.inc(1))
                        else:
                            prog.emit("vector", lambda v, dst=dst, gap=gap, bank=bank:
                                      v.scalar_tensor_tensor(dst, pA[:, 512 * bank:512 * (bank + 1)],
                                                             gap, dst, ALU.mult, ALU.add),
                                      [(s_mm, mmv)] if (sub == 0 and h2 == 0) else (), s_mrg.inc(1))
                n_o += 1
        hmm_done[e] = s_mm.val
        omm_done[e] = s_mm.val
    experts_done_mrg = s_mrg.val

    s_fin = Sem(nc, name + "_fin")
    fin_ev = []
    n_fin = 0
    ntr = tr_groups
    wic_all = wi_casts
    woc_all = wo_casts
    for s16 in range(16):
        par = n_fin % 2
        row0 = 128 * s16
        xw = [(s_wic, wic_all)]
        if n_fin >= 2:
            xw.append((s_fin, n_fin - 1))
        prog.dma("sync", stg[:, 1024 * par:1024 * (par + 1)], x1_ap[row0:row0 + 128, :],
                 s_xld[par], xw)
        xldv = s_xld[par].val
        ys = yacc[:, 1024 * s16:1024 * (s16 + 1)]
        prog.emit("vector", lambda v, ys=ys, par=par: v.tensor_tensor(
                      ys, ys, stg[:, 1024 * par:1024 * (par + 1)], ALU.add),
                  [(s_xld[par], xldv), (s_mrg, experts_done_mrg)], s_fin.inc(1))
        prog.emit("vector", lambda v, ys=ys: v.reduce_sum(fs[:, 0:1], ys, axis=AX.X), (), None)
        prog.emit("vector", lambda v: v.tensor_scalar(fs[:, 0:1], fs[:, 0:1], 1.0 / 1024, None, ALU.mult),
                  (), s_gst.inc(1))
        mean_v = s_gst.val
        prog.emit("scalar", lambda s, ys=ys: s.activation(stg[:, 2048:3072], ys, AF.Square,
                                                          accum_out=fs[:, 1:2]),
                  [(s_gst, mean_v), (s_woc, woc_all)], s_sexp.inc(1))
        sqv = s_sexp.val
        prog.emit("vector", lambda v: v.tensor_tensor(fs[:, 2:3], fs[:, 0:1], fs[:, 0:1], ALU.mult),
                  [(s_sexp, sqv)], None)
        prog.emit("vector", lambda v: v.scalar_tensor_tensor(fs[:, 1:2], fs[:, 1:2], 1.0 / 1024,
                                                             fs[:, 2:3], ALU.mult, ALU.subtract), (), None)
        prog.emit("vector", lambda v: v.tensor_scalar(fs[:, 1:2], fs[:, 1:2], 1e-5, None, ALU.add),
                  (), s_gst.inc(1))
        vstat = s_gst.val
        prog.emit("scalar", lambda s: s.activation(fs[:, 3:4], fs[:, 1:2], AF.Ln),
                  [(s_gst, vstat)], None)
        prog.emit("scalar", lambda s: s.activation(fs[:, 4:5], fs[:, 3:4], AF.Exp, scale=-0.5),
                  (), s_sexp.inc(1))
        rstdv = s_sexp.val
        ow = [(s_sexp, rstdv)]
        if n_fin >= 2:
            ow.append((s_out[par], 16 * (n_fin // 2)))
        prog.emit("vector", lambda v, ys=ys: v.tensor_scalar(
                      ys, ys, fs[:, 0:1], fs[:, 4:5], ALU.subtract, ALU.mult),
                  ow, s_mrg.inc(1))
        fin_ev.append((s_mrg, s_mrg.val))
        prog.dma("gpsimd", out_x2[row0:row0 + 128, :], ys, s_out[par], [fin_ev[-1]])
        cw = [fin_ev[-1], (s_mm, omm_done[7])]
        if n_fin >= 2:
            cw.append((s_tr, ntr))
        prog.emit("vector", lambda v, ys=ys, par=par: v.tensor_copy(
                      hh[:, 1024 * par:1024 * (par + 1)], ys), cw, s_mrg.inc(1))
        castev = (s_mrg, s_mrg.val)
        for grp in range(2):
            bank = ntr % 2
            tw2 = [castev] if grp == 0 else []
            if ntr >= 2:
                tw2.append((s_trev, ntr - 1))
            for jj in range(4):
                j = grp * 4 + jj
                prog.emit("tensor",
                          lambda t, par=par, j=j, bank=bank, jj=jj:
                          t.transpose(ptr[bank][:, 128 * jj:128 * (jj + 1)],
                                      hh[:, 1024 * par + 128 * j:1024 * par + 128 * (j + 1)],
                                      idb[:]),
                          tw2 if jj == 0 else (), s_tr.inc(1) if jj == 3 else None)
            ew = [(s_tr, s_tr.val)]
            if n_fin >= 2:
                ew.append((s_outf[par], 16 * (n_fin // 2)))
            prog.emit("vector",
                      lambda v, par=par, grp=grp, bank=bank:
                      v.tensor_copy(hh[:, 2048 + 1024 * par + 512 * grp:2048 + 1024 * par + 512 * (grp + 1)],
                                    ptr[bank][:].rearrange("p (j t) -> p j t", j=4)),
                      ew, s_trev.inc(1))
            ntr += 1
        prog.dma("gpsimd",
                 out_x2fm[:, 128 * s16:128 * (s16 + 1)].rearrange("(j p) c -> p j c", p=128),
                 hh[:, 2048 + 1024 * par:2048 + 1024 * (par + 1)].rearrange("p (j c) -> p j c", j=8),
                 s_outf[par], [(s_trev, ntr)])
        n_fin += 1

    prog.emit("gpsimd", lambda g: g.engine_nop(),
              [(s_out[0], 16 * 8), (s_out[1], 16 * 8), (s_outf[0], 16 * 8), (s_outf[1], 16 * 8)], None)
    return dict(allocs=allocs)



def l1_const_block():
    pos = np.arange(128) % 64
    inv_freq = 1.0 / (10000.0 ** (np.arange(0, 64, 2, dtype=np.float64) / 64))
    emb = pos[:, None] * inv_freq[None, :]
    cos = np.cos(np.concatenate([emb, emb], 1))
    sin = np.sin(np.concatenate([emb, emb], 1))
    cos2 = np.tile(cos, (1, 4)).astype(np.float32)
    sin_half = np.concatenate([-sin[:, 32:64], sin[:, 0:32]], 1)
    sinS = np.tile(np.concatenate([-sin[:, 0:32], sin[:, 32:64]], 1), (1, 4)).astype(np.float32)
    maskT = (np.arange(128)[:, None] % 64 <= np.arange(64)[None, :]).astype(np.float32)
    return cos2, sinS, maskT


def l1_make_inputs(core, Wq, Wk, Wv, lr_w, lr_b, g4, b4, W1, b1, W2, b2):
    h0, h1 = 2 * core, 2 * core + 1
    projT = np.concatenate([Wq[128 * core:128 * (core + 1)].T,
                            Wk[128 * core:128 * (core + 1)].T,
                            Wv[128 * core:128 * (core + 1)].T], axis=1)
    lr_exp = np.zeros((1024, 128), np.float32)
    lr_exp[:, 0] = lr_w[h0]
    lr_exp[:, 64] = lr_w[h1]
    bias_row = np.zeros((1, 128), np.float32)
    bias_row[0, 0] = lr_b[h0]
    bias_row[0, 64] = lr_b[h1]
    cos2, sinS, maskT = l1_const_block()
    g4t = np.concatenate([np.tile(g4[h0], (128, 1)), np.tile(g4[h1], (128, 1))], 1)
    b4t = np.concatenate([np.tile(b4[h0], (128, 1)), np.tile(b4[h1], (128, 1))], 1)
    g4s = np.concatenate([np.tile(g4[h0], (64, 1)), np.tile(g4[h1], (64, 1))], 0)
    b4s = np.concatenate([np.tile(b4[h0], (64, 1)), np.tile(b4[h1], (64, 1))], 0)
    l1c = np.concatenate([cos2, sinS, g4t, b4t, g4s, b4s, maskT], axis=1).astype(np.float32)
    W1_0 = np.concatenate([W1[h0], W1[h1]], 0).astype(np.float32)
    b1_0 = np.concatenate([b1[h0, 0], b1[h1, 0]])[None, :].astype(np.float32)
    W2_0 = np.zeros((128, 256), np.float32)
    W2_0[:, 0:64] = W2[h0][0:128]
    W2_0[:, 64:128] = W2[h0][128:256]
    W2_0[:, 128:192] = W2[h1][0:128]
    W2_0[:, 192:256] = W2[h1][128:256]
    W2T_0 = np.concatenate([W2[h0].T, W2[h1].T], 1).astype(np.float32)
    b2_0 = np.concatenate([b2[h0, 0], b2[h1, 0]])[None, :].astype(np.float32)
    return dict(projT=np.ascontiguousarray(projT), lr_exp=lr_exp, bias_row=bias_row,
                l1c=l1c, W1_0=W1_0, b1_0=b1_0, W2_0=np.ascontiguousarray(W2_0),
                W2T_0=np.ascontiguousarray(W2T_0), b2_0=b2_0)


def emit_l1_attn(nc, prog, x2fm_g, projT_d, lr_exp_d, bias_row_d, l1c_d,
                 W1_d, b1_d, W2_d, W2T_d, b2_d, out_yfm, out_stats, name="l1a",
                 n_tiles=128, n_chunks=256, dbg="full", dbg_cut=9999):
    GC = math.sqrt(2.0 / math.pi)
    A_ = 0.044715
    st_ = nc.sbuf_tensor
    pt = nc.psum_tensor
    allocs = [
        st_(name + "_xqf", [128, 16384], BF16),
        st_(name + "_xkf", [128, 16384], BF16),
        st_(name + "_tgt", [128, 16384], BF16),
        st_(name + "_a1m", [128, 16384], BF16),
        st_(name + "_yfm", [128, 16384], BF16),
        st_(name + "_eta", [128, 256], F32),
        st_(name + "_sts", [128, 512], F32),
        st_(name + "_l1c", [128, 960], BF16),
        st_(name + "_pjT", [128, 8 * 384], BF16),
        st_(name + "_lre", [128, 8 * 128], BF16),
        st_(name + "_brw", [1, 128], BF16),
        st_(name + "_x2t", [128, 2 * 1024], BF16),
        st_(name + "_stg", [128, 960], F32),
        st_(name + "_prj", [128, 2 * 384], BF16),
        st_(name + "_qkn", [128, 256], F32),
        st_(name + "_qkr", [128, 256], F32),
        st_(name + "_qkb", [128, 256], BF16),
        st_(name + "_tgtb", [128, 128], BF16),
        st_(name + "_sc", [128, 1024], F32),
        st_(name + "_sc2", [128, 384], F32),
        st_(name + "_lns", [128, 32], F32),
        st_(name + "_etf", [128, 256], BF16),
        st_(name + "_idb", [128, 128], BF16),
        st_(name + "_id2", [128, 64], BF16),
        st_(name + "_W1f", [128, 256], F32),
        st_(name + "_W1zb", [128, 512], BF16),
        st_(name + "_W2b", [128, 256], BF16),
        st_(name + "_WTf", [64, 512], F32),
        st_(name + "_WTb", [64, 512], BF16),
        st_(name + "_b1f", [1, 512], F32),
        st_(name + "_b2f", [1, 128], F32),
        st_(name + "_b1b2", [1, 512], BF16),
        st_(name + "_b2b2", [1, 128], BF16),
        st_(name + "_gz1z", [128, 512], BF16),
        st_(name + "_gz2z", [128, 128], BF16),
        st_(name + "_o1z", [128, 2], BF16),
        st_(name + "_xqfz", [128, 128], BF16),
        st_(name + "_X2t", [128, 256], BF16),
        st_(name + "_X2f", [128, 256], BF16),
        st_(name + "_X2bt", [128, 256], BF16),
        st_(name + "_X2bf", [128, 256], BF16),
        st_(name + "_gb", [128, 256], F32),
        st_(name + "_gz2", [128, 64], BF16),
        st_(name + "_gz2f", [64, 128], BF16),
        st_(name + "_gz1", [128, 256], BF16),
        st_(name + "_a2m", [128, 64], BF16),
        st_(name + "_xqt", [128, 64], BF16),
        st_(name + "_xkt", [128, 64], BF16),
        st_(name + "_yc", [128, 64], F32),
        st_(name + "_ons", [128, 128], BF16),
        st_(name + "_o11", [128, 1], BF16),
        pt(name + "_p0", [128, 512], F32),
        pt(name + "_p1", [128, 512], F32),
        pt(name + "_p2", [128, 512], F32),
        pt(name + "_p3", [128, 512], F32),
        pt(name + "_p4", [128, 512], F32),
        pt(name + "_p5", [128, 512], F32),
        pt(name + "_p6", [128, 512], F32),
        pt(name + "_p7", [128, 512], F32),
    ]
    hs = [a.__enter__() for a in allocs]
    (xqf, xkf, tgt, a1m, yfm, eta, sts, l1c, pjT, lre, brw, x2t, stg, prj, qkn, qkr, qkb, tgtb,
     sc, sc2, lns, etf, idb, id2, W1f, W1zb, W2b, WTf, WTb, b1f, b2f, b1b2, b2b2, gz1z, gz2z, o1z, xqfz,
     X2t, X2f, X2bt, X2bf, gb, gz2, gz2f, gz1, a2m, xqt, xkt, yc, ons, o11,
     p0, p1, p2, p3, p4, p5, p6, p7) = hs
    pT0b = p6[:, 0:256].bitcast(BF16)
    pT1b = p7[:, 0:256].bitcast(BF16)
    pTb = (pT0b, pT1b)

    cos2 = l1c[:, 0:256]
    sinS = l1c[:, 256:512]
    g4t = l1c[:, 512:640]
    b4t = l1c[:, 640:768]
    g4s = l1c[:, 768:832]
    b4s = l1c[:, 832:896]
    mskT = l1c[:, 896:960]

    s_ld = [Sem(nc, name + f"_ld{i}") for i in range(2)]
    s_wld = Sem(nc, name + "_wld")
    s_cast = Sem(nc, name + "_cast")
    s_mm = Sem(nc, name + "_mm")
    s_vp = Sem(nc, name + "_vp")
    s_sp = Sem(nc, name + "_sp")
    s_gp = Sem(nc, name + "_gp")
    s_tr = Sem(nc, name + "_tr")
    s_out = Sem(nc, name + "_out")

    prog.emit("gpsimd", lambda g: g.iota(sc[:, 0:128], pattern=[[1, 128]], base=0,
                                         channel_multiplier=0, allow_small_or_imprecise_dtypes=True),
              (), s_gp.inc(1))
    prog.emit("gpsimd", lambda g: g.iota(sc[:, 128:256], pattern=[[0, 128]], base=0,
                                         channel_multiplier=1, allow_small_or_imprecise_dtypes=True),
              (), s_gp.inc(1))
    prog.emit("vector", lambda v: v.tensor_tensor(sc[:, 256:384], sc[:, 0:128], sc[:, 128:256], ALU.is_equal),
              [(s_gp, 2)], s_vp.inc(1))
    prog.emit("vector", lambda v: v.tensor_copy(idb[:], sc[:, 256:384]), (), s_vp.inc(1))
    prog.emit("vector", lambda v: v.tensor_copy(id2[0:64, :], idb[0:64, 0:64]), (), s_vp.inc(1))
    prog.emit("vector", lambda v: v.tensor_copy(id2[64:128, :], idb[64:128, 64:128]), (), s_vp.inc(1))

    init_loads = []
    init_loads.append((l1c_d[:], 960, "vector", lambda: l1c[:]))
    for j in range(8):
        init_loads.append((projT_d[128 * j:128 * (j + 1), :], 384, "scalar",
                           lambda j=j: pjT[:, 384 * j:384 * (j + 1)]))
        init_loads.append((lr_exp_d[128 * j:128 * (j + 1), :], 128, "scalar",
                           lambda j=j: lre[:, 128 * j:128 * (j + 1)]))
    init_loads.append((W1_d[:], 256, "vector", lambda: W1f[:]))
    init_loads.append((b1_d[:], 512, "vector", lambda: b1f[:]))
    init_loads.append((W2T_d[:], 512, "vector", lambda: WTf[:]))
    init_loads.append((b2_d[:], 128, "vector", lambda: b2f[:]))
    prev_ev = None
    for (src_ap, wid, eng, dstfn) in init_loads:
        psz = src_ap.shape[0]
        prog.dma("sync", stg[0:psz, 0:wid], src_ap, s_wld, [prev_ev] if prev_ev else [])
        wv = s_wld.val
        if eng == "vector":
            prog.emit("vector", lambda v, dstfn=dstfn, wid=wid, psz=psz:
                      v.tensor_copy(dstfn(), stg[0:psz, 0:wid]), [(s_wld, wv)], s_vp.inc(1))
            prev_ev = (s_vp, s_vp.val)
        else:
            prog.emit("scalar", lambda s, dstfn=dstfn, wid=wid, psz=psz:
                      s.copy(dstfn(), stg[0:psz, 0:wid]), [(s_wld, wv)], s_cast.inc(1))
            prev_ev = (s_cast, s_cast.val)
    prog.dma("sync", stg[0:1, 0:128], bias_row_d[:], s_wld, [prev_ev])
    prog.emit("scalar", lambda s: s.copy(brw[:], stg[0:1, 0:128]),
              [(s_wld, s_wld.val)], s_cast.inc(1))
    prog.emit("vector", lambda v: v.memset(W1zb[:], 0.0), (), s_vp.inc(1))
    prog.emit("vector", lambda v: v.tensor_copy(W1zb[0:64, 0:256], W1f[0:64, :]), (), s_vp.inc(1))
    prog.emit("vector", lambda v: v.tensor_copy(W1zb[64:128, 256:512], W1f[64:128, :]), (), s_vp.inc(1))
    prog.emit("vector", lambda v: v.tensor_copy(WTb[:], WTf[:]), (), s_vp.inc(1))
    prog.emit("vector", lambda v: v.tensor_copy(b1b2[:], b1f[:]), (), s_vp.inc(1))
    prog.emit("vector", lambda v: v.tensor_copy(b2b2[:], b2f[:]), (), s_vp.inc(1))
    prog.emit("vector", lambda v: v.memset(gz1z[:], 0.0), (), s_vp.inc(1))
    prog.emit("vector", lambda v: v.memset(gz2z[:], 0.0), (), s_vp.inc(1))
    prog.emit("vector", lambda v: v.memset(xqfz[:], 0.0), (), s_vp.inc(1))
    prog.emit("vector", lambda v: v.memset(o1z[:], 0.0), (), s_vp.inc(1))
    prog.emit("vector", lambda v: v.tensor_scalar(o1z[0:64, 0:1], o1z[0:64, 0:1], 0.0, 1.0,
                                                  ALU.mult, ALU.add), (), s_vp.inc(1))
    prog.emit("vector", lambda v: v.tensor_scalar(o1z[64:128, 1:2], o1z[64:128, 1:2], 0.0, 1.0,
                                                  ALU.mult, ALU.add), (), s_vp.inc(1))
    prog.emit("vector", lambda v: v.memset(ons[:], 1.0), (), s_vp.inc(1))
    prog.emit("vector", lambda v: v.memset(o11[:], 1.0), (), s_vp.inc(1))
    prog.emit("vector", lambda v: v.memset(sts[:], 0.0), (), s_vp.inc(1))
    prog.emit("vector", lambda v: v.memset(yfm[:], 0.0), (), s_vp.inc(1))
    init_v = s_vp.val
    cast_ok = s_cast.val

    pjT3 = pjT[:].rearrange("p (j c) -> p j c", j=8)
    lre3 = lre[:].rearrange("p (j c) -> p j c", j=8)
    x2t3 = [x2t[:, 1024 * par:1024 * (par + 1)].rearrange("p (j c) -> p j c", j=8) for par in range(2)]
    n_tr = 0
    tile_mm_done = []
    tile_prj_free = []
    tile_psum_free = []
    tile_trev = []
    trev_vals = []
    eta_mm_evac = []

    for T in range(n_tiles):
        par = T % 2
        r, coff = T % 8, 128 * (T // 8)
        lw = [(s_cast, cast_ok)]
        if T >= 2:
            lw.append((s_mm, tile_mm_done[T - 2]))
        prog.dma("sync", x2t3[par],
                 x2fm_g[1024 * r:1024 * (r + 1), coff:coff + 128].rearrange("(j p) c -> p j c", p=128),
                 s_ld[par], lw)
        ldv = s_ld[par].val
        pP = (p0, p1)[par]
        pL = (p2, p3)[par]
        w0 = [(s_ld[par], ldv), (s_cast, cast_ok), (s_vp, init_v)]
        if T >= 2:
            w0.append((s_sp, tile_psum_free[T - 2]))
        for j in range(8):
            prog.emit("tensor", lambda t, j=j, par=par, pP=pP:
                      t.matmul(pP[:, 0:384], x2t3[par][:, j, :], pjT3[:, j, :],
                               start=(j == 0), stop=(j == 7)),
                      w0 if j == 0 else (), None)
        for j in range(8):
            prog.emit("tensor", lambda t, j=j, par=par, pL=pL:
                      t.matmul(pL[:, 0:128], lre3[:, j, :], x2t3[par][:, j, :],
                               start=(j == 0), stop=False), (), None)
        prog.emit("tensor", lambda t, pL=pL: t.matmul(pL[:, 0:128], brw[0:1, :], ons[0:1, 0:128],
                                                      start=False, stop=True), (), s_mm.inc(1))
        tile_mm_done.append(s_mm.val)
        ew = [(s_mm, s_mm.val)]
        if T >= 2:
            ew.append((s_vp, tile_prj_free[T - 2]))
        prog.emit("scalar", lambda s, par=par, pP=pP: s.copy(prj[:, 384 * par:384 * (par + 1)], pP[:, 0:384]),
                  ew, s_sp.inc(1))
        sgw = [eta_mm_evac[-1]] if eta_mm_evac else []
        prog.emit("scalar", lambda s, pL=pL:
                  s.activation(etf[:, 0:128], pL[:, 0:128], AF.Exp, scale=-1.0),
                  sgw, s_sp.inc(1))
        tile_psum_free.append(s_sp.val)
        spv = s_sp.val
        P = prj[:, 384 * par:384 * (par + 1)]
        EF = etf[:]
        QKB = qkb[:]
        vw = [(s_sp, spv)]
        prog.emit("vector", lambda v, P=P: v.tensor_tensor(sc[:, 0:256], P[:, 0:256], P[:, 0:256], ALU.mult),
                  vw, None)
        prog.emit("vector", lambda v: v.reduce_sum(lns[:, 0:4].rearrange("p (g o) -> p g o", o=1),
                                                   sc[:, 0:256].rearrange("p (g f) -> p g f", g=4),
                                                   axis=AX.X), (), s_vp.inc(1))
        ssv = s_vp.val
        prog.emit("scalar", lambda s: s.activation(lns[:, 4:8], lns[:, 0:4], AF.Ln),
                  [(s_vp, ssv)], None)
        prog.emit("scalar", lambda s: s.activation(lns[:, 8:12], lns[:, 4:8], AF.Exp, scale=-0.5),
                  (), s_sp.inc(1))
        rnv = s_sp.val
        prog.emit("vector", lambda v: v.tensor_scalar(lns[:, 8:12], lns[:, 8:12], 1e12, None, ALU.min),
                  [(s_sp, rnv)], None)
        for g_ in range(4):
            prog.emit("vector", lambda v, P=P, g_=g_:
                      v.tensor_scalar(qkn[:, 64 * g_:64 * (g_ + 1)], P[:, 64 * g_:64 * (g_ + 1)],
                                      lns[:, 8 + g_:9 + g_], None, ALU.mult), (), None)
        qkn4 = qkn[:].rearrange("p (g h f) -> p g h f", g=4, f=32)
        prog.emit("vector", lambda v: v.tensor_tensor(qkr[:], qkn[:], cos2, ALU.mult), (), None)
        prog.emit("vector", lambda v, qkn4=qkn4: v.tensor_tensor(
                      sc[:, 256:512].rearrange("p (g h f) -> p g h f", g=4, f=32),
                      qkn4[:, :, ::-1, :],
                      sinS.rearrange("p (g h f) -> p g h f", g=4, f=32), ALU.mult), (), None)
        prog.emit("vector", lambda v: v.tensor_tensor(qkr[:], qkr[:], sc[:, 256:512], ALU.add),
                  (), None)
        cw = []
        if T >= 1:
            cw.append((s_tr, 3 * T))
        prog.emit("vector", lambda v, QKB=QKB: v.tensor_copy(QKB, qkr[:]), cw, s_vp.inc(1))
        qkbv = s_vp.val
        prog.emit("vector", lambda v, P=P: v.tensor_tensor(sc2[:, 0:128], P[:, 256:384],
                                                           qkr[:, 128:256], ALU.subtract), (), None)
        prog.emit("vector", lambda v: v.reduce_sum(lns[:, 12:14].rearrange("p (g o) -> p g o", o=1),
                                                   sc2[:, 0:128].rearrange("p (g f) -> p g f", g=2),
                                                   axis=AX.X), (), None)
        prog.emit("vector", lambda v: v.tensor_scalar(lns[:, 12:14], lns[:, 12:14], 1.0 / 64, None, ALU.mult),
                  (), None)
        for h in range(2):
            prog.emit("vector", lambda v, h=h: v.tensor_scalar(
                          sc2[:, 64 * h:64 * (h + 1)], sc2[:, 64 * h:64 * (h + 1)],
                          lns[:, 12 + h:13 + h], None, ALU.subtract), (), None)
        prog.emit("vector", lambda v: v.tensor_tensor(sc2[:, 128:256], sc2[:, 0:128], sc2[:, 0:128], ALU.mult),
                  (), None)
        prog.emit("vector", lambda v: v.reduce_sum(lns[:, 14:16].rearrange("p (g o) -> p g o", o=1),
                                                   sc2[:, 128:256].rearrange("p (g f) -> p g f", g=2),
                                                   axis=AX.X), (), None)
        prog.emit("vector", lambda v: v.tensor_scalar(lns[:, 14:16], lns[:, 14:16], 1.0 / 63, None, ALU.mult),
                  (), s_vp.inc(1))
        varv = s_vp.val
        prog.emit("scalar", lambda s: s.activation(lns[:, 16:18], lns[:, 14:16], AF.Ln),
                  [(s_vp, varv)], None)
        prog.emit("scalar", lambda s: s.activation(lns[:, 18:20], lns[:, 16:18], AF.Exp, scale=0.5),
                  (), s_sp.inc(1))
        sdv = s_sp.val
        prog.emit("vector", lambda v: v.tensor_scalar(lns[:, 18:20], lns[:, 18:20], 1e-8, None, ALU.add),
                  [(s_sp, sdv)], None)
        prog.emit("vector", lambda v: v.reciprocal(lns[:, 20:22], lns[:, 18:20]), (), None)
        for h in range(2):
            prog.emit("vector", lambda v, h=h: v.tensor_scalar(
                          sc2[:, 64 * h:64 * (h + 1)], sc2[:, 64 * h:64 * (h + 1)],
                          lns[:, 20 + h:21 + h], None, ALU.mult), (), None)
        prog.emit("vector", lambda v: v.tensor_tensor(sc2[:, 0:128], sc2[:, 0:128], g4t, ALU.mult), (), None)
        prog.emit("vector", lambda v: v.tensor_tensor(sc2[:, 0:128], sc2[:, 0:128], b4t, ALU.add), (), None)
        tgw = []
        if T >= 1:
            tgw.append((s_tr, 3 * T))
        prog.emit("vector", lambda v: v.tensor_copy(tgtb[:], sc2[:, 0:128]), tgw, s_vp.inc(1))
        tile_prj_free.append(s_vp.val)
        prog.emit("vector", lambda v, EF=EF: v.tensor_scalar(sc2[:, 0:128], EF[:, 0:128], 1.0, None, ALU.add),
                  (), None)
        prog.emit("vector", lambda v: v.reciprocal(sc2[:, 128:256], sc2[:, 0:128]), (), None)
        prog.emit("vector", lambda v, EF=EF: v.tensor_scalar(EF[:, 128:256], sc2[:, 128:256], 1.0 / 64,
                                                             None, ALU.mult), (), s_vp.inc(1))
        etav = s_vp.val
        tile_tgtv = s_vp.val
        for dst, srcap, gate in ((xqf, QKB[:, 0:128], qkbv), (xkf, QKB[:, 128:256], qkbv),
                                 (tgt, tgtb[:], tile_tgtv)):
            bank = n_tr % 2
            tw = [(s_vp, gate)]
            if n_tr >= 2:
                tw.append((s_vp, trev_vals[n_tr - 2]))
            prog.emit("tensor", lambda t, srcap=srcap, bank=bank:
                      t.transpose(pTb[bank][:, 0:128], srcap, idb[:]),
                      tw, s_tr.inc(1))
            prog.emit("vector", lambda v, dst=dst, T=T, bank=bank:
                      v.tensor_copy(dst[:, 128 * T:128 * (T + 1)], pTb[bank][:, 0:128]),
                      [(s_tr, s_tr.val)], s_vp.inc(1))
            trev_vals.append(s_vp.val)
            n_tr += 1
        tile_trev.append(s_vp.val)
        for ci in range(2):
            i_ = 2 * T + ci
            emw = [(s_vp, etav)]
            if eta_mm_evac:
                emw.append(eta_mm_evac[-1])
            for h in range(2):
                prog.emit("tensor", lambda t, EF=EF, ci=ci, h=h:
                          t.matmul(p4[64 * h:64 * (h + 1), 0:1],
                                   EF[64 * h:64 * h + 1, 128 + 64 * ci:128 + 64 * (ci + 1)],
                                   o11[64 * h:64 * h + 1, 0:1],
                                   start=True, stop=True),
                          emw if h == 0 else (), s_mm.inc(1) if h == 1 else None)
            prog.emit("vector", lambda v, i_=i_: v.tensor_copy(eta[:, i_:i_ + 1], p4[:, 0:1]),
                      [(s_mm, s_mm.val)], s_vp.inc(1))
            eta_mm_evac.append((s_vp, s_vp.val))
    pre_v = s_vp.val

    if dbg == "pre":
        prog.dma("gpsimd", out_yfm[:], yfm[:], s_out, [(s_vp, s_vp.val)])
        prog.dma("gpsimd", out_stats[:], sts[:], s_out, [(s_vp, s_vp.val)])
        prog.emit("gpsimd", lambda g: g.engine_nop(), [(s_out, 32)], None)
        return dict(allocs=allocs)

    a1_evac = []
    for i in range(n_chunks):
        bank = (p4, p5)[i % 2]
        cpw = [(s_vp, pre_v)] if i == 0 else [(s_vp, a1_evac[i - 1])]
        prog.emit("vector", lambda v, i=i: v.tensor_copy(xqfz[0:64, 0:64],
                                                         xqf[0:64, 64 * i:64 * (i + 1)]), cpw, None)
        prog.emit("vector", lambda v, i=i: v.tensor_copy(xqfz[64:128, 64:128],
                                                         xqf[64:128, 64 * i:64 * (i + 1)]),
                  (), s_vp.inc(1))
        xqzv = s_vp.val
        w0 = [(s_vp, xqzv)]
        if i >= 2:
            w0.append((s_vp, a1_evac[i - 2]))
        for h in range(2):
            prog.emit("tensor", lambda t, i=i, h=h, bank=bank:
                      t.matmul(bank[64 * h:64 * (h + 1), 0:64],
                               xkf[:, 64 * i:64 * (i + 1)],
                               xqfz[:, 64 * h:64 * (h + 1)],
                               start=True, stop=True),
                      w0 if h == 0 else (), s_mm.inc(1) if h == 1 else None)
        prog.emit("vector", lambda v, i=i, bank=bank:
                  v.scalar_tensor_tensor(a1m[:, 64 * i:64 * (i + 1)], bank[:, 0:64], 1.0,
                                         mskT, ALU.add, ALU.mult),
                  [(s_mm, s_mm.val)], s_vp.inc(1))
        a1_evac.append(s_vp.val)
    a1_done = s_vp.val

    if dbg == "a1":
        prog.dma("gpsimd", out_yfm[:], a1m[:], s_out, [(s_vp, a1_done)])
        prog.dma("gpsimd", out_stats[:], sts[:], s_out, [(s_vp, a1_done)])
        prog.emit("gpsimd", lambda g: g.engine_nop(), [(s_out, 32)], None)
        return dict(allocs=allocs)

    ybf = nc.sbuf_tensor(name + "_ybf", [128, 64], BF16).__enter__()
    tgc = nc.sbuf_tensor(name + "_tgc", [128, 64], BF16).__enter__()
    s_sv = Sem(nc, name + "_sv")
    s_ss = Sem(nc, name + "_ss")
    for k4, (h, c) in enumerate(((0, 0), (0, 1), (1, 0), (1, 1))):
        prog.emit("tensor", lambda t, h=h, c=c, k4=k4:
                  t.transpose(pTb[0][:, 64 * k4:64 * (k4 + 1)],
                              WTb[0:64, 256 * h + 128 * c:256 * h + 128 * (c + 1)],
                              idb[0:64, 0:64]),
                  [(s_vp, a1_done)] if k4 == 0 else (), s_tr.inc(1) if k4 == 3 else None)
    prog.emit("vector", lambda v: v.tensor_copy(W2b[:], pTb[0][:, 0:256]),
              [(s_tr, s_tr.val)], s_sv.inc(1))
    carry_ok0 = (s_sv, s_sv.val)

    n_str = 0
    str_evac = []

    def tr_unit(tr_ops, evac_fn, eng, waits):
        nonlocal n_str
        bank = n_str % 2
        for oi, mk in enumerate(tr_ops):
            out_sl, in_ap, ident = mk(pTb[bank])
            tw = list(waits) if oi == 0 else []
            if oi == 0 and n_str >= 2:
                tw.append(str_evac[n_str - 2])
            prog.emit("tensor", lambda t, out_sl=out_sl, in_ap=in_ap, ident=ident:
                      t.transpose(out_sl, in_ap, ident),
                      tw, s_tr.inc(1) if oi == len(tr_ops) - 1 else None)
        trv = s_tr.val
        if eng == "scalar":
            prog.emit("scalar", lambda s, bank=bank: evac_fn(s, pTb[bank]), [(s_tr, trv)], s_ss.inc(1))
            ev = (s_ss, s_ss.val)
        else:
            prog.emit("vector", lambda v, bank=bank: evac_fn(v, pTb[bank]), [(s_tr, trv)], s_sv.inc(1))
            ev = (s_sv, s_sv.val)
        str_evac.append(ev)
        n_str += 1
        return ev

    x2gelu_v = {}
    x2bgelu_v = {}
    gm_v = {}
    gz2v_v = {}
    z1ev_v = {}
    updmm_v = {}
    carry_v = {}
    z2evac_v = {}
    a2mp_v = {}
    ycopy_v = {}

    for i in range(n_chunks):
        ii = 64 * i
        w0 = [(s_vp, a1_done), carry_ok0] if i == 0 else []

        def mk_tr2(srcfm, h):
            return lambda PB: (PB[64 * h:64 * (h + 1), 0:64],
                               srcfm[64 * h:64 * (h + 1), ii:ii + 64],
                               id2[64 * h:64 * (h + 1), :])
        uwq = list(w0)
        if i >= 1:
            uwq.append((s_sv, ycopy_v[i - 1]))
        ev_xqt = tr_unit([mk_tr2(xqf, 0), mk_tr2(xqf, 1)],
                         lambda e, PB: e.copy(xqt[:], PB[:, 0:64]), "scalar", uwq)
        uwk = []
        if i >= 1:
            uwk.append((s_mm, updmm_v[i - 1]))
        ev_xkt = tr_unit([mk_tr2(xkf, 0), mk_tr2(xkf, 1)],
                         lambda e, PB: e.copy(xkt[:], PB[:, 0:64]), "scalar", uwk)
        uwt = []
        if i >= 1:
            uwt.append((s_sv, gz2v_v[i - 1]))
        ev_tgc = tr_unit([mk_tr2(tgt, 0), mk_tr2(tgt, 1)],
                         lambda e, PB: e.copy(tgc[:], PB[:, 0:64]), "scalar", uwt)

        zw = list(w0)
        if i >= 1:
            zw.append((s_ss, z1ev_v[i - 1]))
            zw.append(carry_v[i - 1])
        for h in range(2):
            prog.emit("tensor", lambda t, h=h, ii=ii:
                      t.matmul(p0[64 * h:64 * (h + 1), 0:256],
                               xkf[:, ii:ii + 64],
                               W1zb[:, 256 * h:256 * (h + 1)], start=True, stop=False),
                      zw if h == 0 else (), None)
            prog.emit("tensor", lambda t, h=h:
                      t.matmul(p0[64 * h:64 * (h + 1), 0:256],
                               ons[0:1, 0:64],
                               b1b2[0:1, 256 * h:256 * (h + 1)], start=False, stop=True),
                      (), s_mm.inc(1) if h == 1 else None)
        z1v = s_mm.val

        gw = [(s_mm, z1v)]
        if i >= 1:
            gw.append((s_mm, updmm_v[i - 1]))
            gw.append((s_gp, gm_v[i - 1]))
        prog.emit("scalar", lambda s: s.activation(X2t[:], p0[:, 0:256], AF.Gelu_apprx_tanh),
                  gw, s_ss.inc(1))
        x2gelu_v[i] = s_ss.val
        prog.emit("scalar", lambda s: s.activation(sc[:, 0:256], p0[:, 0:256], AF.Square), (), None)
        prog.emit("scalar", lambda s: s.copy(sc[:, 256:512], p0[:, 0:256]), (), s_ss.inc(1))
        z1ev_v[i] = s_ss.val

        prog.emit("vector", lambda v: v.tensor_scalar(sc[:, 512:768], sc[:, 0:256], A_, 1.0,
                                                      ALU.mult, ALU.add), [(s_ss, z1ev_v[i])], None)
        prog.emit("vector", lambda v: v.tensor_tensor(sc[:, 512:768], sc[:, 512:768],
                                                      sc[:, 256:512], ALU.mult), (), s_sv.inc(1))
        uv = s_sv.val
        prog.emit("scalar", lambda s: s.activation(sc[:, 512:768], sc[:, 512:768], AF.Tanh, scale=GC),
                  [(s_sv, uv)], s_ss.inc(1))
        thv = s_ss.val
        prog.emit("vector", lambda v: v.tensor_scalar(sc[:, 0:256], sc[:, 0:256], 3 * A_ * GC, GC,
                                                      ALU.mult, ALU.add), [(s_ss, thv)], s_sv.inc(1))
        bcoefv = s_sv.val
        prog.emit("vector", lambda v: v.tensor_scalar(sc[:, 768:1024], sc[:, 512:768], -1.0, 1.0,
                                                      ALU.mult, ALU.add), (), s_sv.inc(1))
        prog.emit("gpsimd", lambda g: g.tensor_tensor(sc[:, 768:1024], sc[:, 768:1024], X2t[:], ALU.mult),
                  [(s_ss, x2gelu_v[i]), (s_sv, s_sv.val)], s_gp.inc(1))
        prog.emit("gpsimd", lambda g: g.tensor_tensor(sc[:, 768:1024], sc[:, 768:1024], sc[:, 0:256],
                                                      ALU.mult), [(s_sv, bcoefv)], s_gp.inc(1))
        gm2 = s_gp.val
        gm_v[i] = gm2
        prog.emit("vector", lambda v: v.tensor_scalar(gb[:], sc[:, 512:768], 0.5, 0.5, ALU.mult, ALU.add),
                  (), None)
        prog.emit("vector", lambda v: v.tensor_tensor(gb[:], gb[:], sc[:, 768:1024], ALU.add),
                  [(s_gp, gm2)], s_sv.inc(1))
        gbv = s_sv.val

        ev_x2f = None
        for c in range(2):
            ev_x2f = tr_unit([lambda PB, c=c: (PB[:, 0:128], X2t[:, 128 * c:128 * (c + 1)], idb[:])],
                             (lambda e, PB, c=c: e.copy(X2f[:, 128 * c:128 * (c + 1)], PB[:, 0:128])),
                             "scalar", [(s_ss, x2gelu_v[i])] if c == 0 else [])

        zw2 = [ev_x2f]
        if i >= 1:
            zw2.append(carry_v[i - 1])
        first = True
        for h in range(2):
            for c in range(2):
                prog.emit("tensor", lambda t, h=h, c=c:
                          t.matmul(p1[64 * h:64 * (h + 1), 0:64],
                                   X2f[:, 128 * c + 64 * h:128 * c + 64 * (h + 1)],
                                   W2b[:, 64 * (2 * h + c):64 * (2 * h + c + 1)],
                                   start=(c == 0), stop=False),
                          zw2 if first else (), None)
                first = False
            prog.emit("tensor", lambda t, h=h:
                      t.matmul(p1[64 * h:64 * (h + 1), 0:64],
                               ons[0:1, 0:64],
                               b2b2[0:1, 64 * h:64 * (h + 1)], start=False, stop=True),
                      (), s_mm.inc(1) if h == 1 else None)
        z2v = s_mm.val

        prog.emit("vector", lambda v: v.tensor_copy(sc2[:, 0:64], p1[:, 0:64]),
                  [(s_mm, z2v)], s_sv.inc(1))
        z2evac_v[i] = s_sv.val
        prog.emit("vector", lambda v: v.reduce_sum(lns[:, 0:1], sc2[:, 0:64], axis=AX.X), (), None)
        prog.emit("vector", lambda v: v.tensor_scalar(lns[:, 0:1], lns[:, 0:1], 1.0 / 64, None, ALU.mult),
                  (), None)
        prog.emit("vector", lambda v: v.tensor_scalar(sc2[:, 0:64], sc2[:, 0:64], lns[:, 0:1], None,
                                                      ALU.subtract), (), None)
        prog.emit("vector", lambda v: v.tensor_tensor(sc2[:, 64:128], sc2[:, 0:64], sc2[:, 0:64], ALU.mult),
                  (), None)
        prog.emit("vector", lambda v: v.reduce_sum(lns[:, 1:2], sc2[:, 64:128], axis=AX.X), (), None)
        prog.emit("vector", lambda v: v.tensor_scalar(lns[:, 1:2], lns[:, 1:2], 1.0 / 64, 1e-6,
                                                      ALU.mult, ALU.add), (), s_sv.inc(1))
        varv = s_sv.val
        lni = lns[:, 2:3].bitcast(mybir.dt.int32)
        lnv = lns[:, 1:2].bitcast(mybir.dt.int32)
        prog.emit("vector", lambda v: v.tensor_scalar(lni, lnv, 1, None, ALU.arith_shift_right),
                  [(s_sv, varv)], None)
        prog.emit("vector", lambda v: v.tensor_scalar(lni, lni, -1, 0x5f3759df, ALU.mult, ALU.add),
                  (), None)
        for _ in range(2):
            prog.emit("vector", lambda v: v.tensor_tensor(lns[:, 3:4], lns[:, 2:3], lns[:, 2:3], ALU.mult),
                      (), None)
            prog.emit("vector", lambda v: v.tensor_tensor(lns[:, 3:4], lns[:, 3:4], lns[:, 1:2], ALU.mult),
                      (), None)
            prog.emit("vector", lambda v: v.tensor_scalar(lns[:, 3:4], lns[:, 3:4], -0.5, 1.5,
                                                          ALU.mult, ALU.add), (), None)
            prog.emit("vector", lambda v: v.tensor_tensor(lns[:, 2:3], lns[:, 2:3], lns[:, 3:4], ALU.mult),
                      (), None)
        prog.emit("vector", lambda v: v.tensor_scalar(sc2[:, 0:64], sc2[:, 0:64], lns[:, 2:3], None,
                                                      ALU.mult), (), None)
        prog.emit("vector", lambda v: v.tensor_tensor(sc2[:, 64:128], sc2[:, 0:64], g4s, ALU.mult), (), None)
        prog.emit("vector", lambda v: v.tensor_tensor(sc2[:, 128:192], b4s, tgc[:], ALU.subtract),
                  [ev_tgc], None)
        prog.emit("vector", lambda v: v.tensor_tensor(sc2[:, 64:128], sc2[:, 64:128], sc2[:, 128:192],
                                                      ALU.add), (), None)
        prog.emit("vector", lambda v: v.tensor_tensor(sc2[:, 64:128], sc2[:, 64:128], g4s, ALU.mult),
                  (), None)
        prog.emit("vector", lambda v: v.reduce_sum(lns[:, 4:5], sc2[:, 64:128], axis=AX.X), (), None)
        prog.emit("vector", lambda v: v.tensor_tensor(sc2[:, 128:192], sc2[:, 64:128], sc2[:, 0:64],
                                                      ALU.mult), (), None)
        prog.emit("vector", lambda v: v.reduce_sum(lns[:, 5:6], sc2[:, 128:192], axis=AX.X), (), None)
        prog.emit("vector", lambda v: v.tensor_scalar(lns[:, 4:6], lns[:, 4:6], 1.0 / 64, None, ALU.mult),
                  (), None)
        prog.emit("vector", lambda v: v.tensor_scalar(sc2[:, 128:192], sc2[:, 0:64], lns[:, 5:6], None,
                                                      ALU.mult), (), None)
        prog.emit("vector", lambda v: v.tensor_tensor(sc2[:, 128:192], sc2[:, 64:128], sc2[:, 128:192],
                                                      ALU.subtract), (), None)
        prog.emit("vector", lambda v, i=i: v.tensor_tensor(lns[:, 6:7], lns[:, 2:3], eta[:, i:i + 1],
                                                           ALU.mult), (), None)
        prog.emit("vector", lambda v: v.tensor_scalar(lns[:, 6:7], lns[:, 6:7], -1.0, None, ALU.mult),
                  (), None)
        prog.emit("vector", lambda v: v.tensor_scalar(sc2[:, 128:192], sc2[:, 128:192], lns[:, 4:5], None,
                                                      ALU.subtract), (), None)
        gz2w = [(s_mm, updmm_v[i - 1])] if i >= 1 else []
        prog.emit("vector", lambda v: v.tensor_scalar(gz2[:], sc2[:, 128:192], lns[:, 6:7], None,
                                                      ALU.mult), gz2w, s_sv.inc(1))
        gz2v_v[i] = s_sv.val
        prog.emit("vector", lambda v: v.tensor_copy(gz2z[0:64, 0:64], gz2[0:64, :]), (), None)
        prog.emit("vector", lambda v: v.tensor_copy(gz2z[64:128, 64:128], gz2[64:128, :]),
                  (), s_sv.inc(1))
        gz2zv = s_sv.val

        ev_gz2f = tr_unit([lambda PB: (PB[0:64, 0:128], gz2[:], idb[:])],
                          (lambda e, PB: e.copy(gz2f[0:64, 0:128], PB[0:64, 0:128])),
                          "scalar", [(s_sv, gz2v_v[i])])

        g1w = [ev_gz2f]
        if i >= 1:
            g1w.append(carry_v[i - 1])
        for h in range(2):
            prog.emit("tensor", lambda t, h=h:
                      t.matmul(p2[64 * h:64 * (h + 1), 0:256],
                               gz2f[0:64, 64 * h:64 * (h + 1)],
                               WTb[0:64, 256 * h:256 * (h + 1)], start=True, stop=True),
                      g1w if h == 0 else (), s_mm.inc(1) if h == 1 else None)
        gz1mmv = s_mm.val
        gz1w = [(s_mm, gz1mmv), (s_sv, gbv)]
        if i >= 1:
            gz1w.append((s_mm, updmm_v[i - 1]))
        prog.emit("vector", lambda v: v.tensor_tensor(gz1[:], p2[:, 0:256], gb[:], ALU.mult),
                  gz1w, s_sv.inc(1))
        gz1v = s_sv.val
        prog.emit("vector", lambda v: v.tensor_copy(gz1z[0:64, 0:256], gz1[0:64, :]), (), None)
        prog.emit("vector", lambda v: v.tensor_copy(gz1z[64:128, 256:512], gz1[64:128, :]),
                  (), s_sv.inc(1))
        gz1zv = s_sv.val

        zbw = [(s_sv, gz1zv)]
        if i >= 1:
            zbw.append(carry_v[i - 1])
        for h in range(2):
            prog.emit("tensor", lambda t, h=h, ii=ii:
                      t.matmul(p3[64 * h:64 * (h + 1), 0:256],
                               xqf[:, ii:ii + 64],
                               W1zb[:, 256 * h:256 * (h + 1)], start=True, stop=False),
                      zbw if h == 0 else (), None)
            prog.emit("tensor", lambda t, h=h, ii=ii:
                      t.matmul(p3[64 * h:64 * (h + 1), 0:256],
                               a1m[:, ii:ii + 64],
                               gz1z[:, 256 * h:256 * (h + 1)], start=False, stop=False),
                      (), None)
            prog.emit("tensor", lambda t, h=h:
                      t.matmul(p3[64 * h:64 * (h + 1), 0:256],
                               ons[0:1, 0:64],
                               b1b2[0:1, 256 * h:256 * (h + 1)], start=False, stop=True),
                      (), s_mm.inc(1) if h == 1 else None)
        z1bv = s_mm.val
        x2bw = [(s_mm, z1bv)]
        if i >= 1:
            x2bw.append((s_mm, updmm_v[i - 1]))
        prog.emit("scalar", lambda s: s.activation(X2bt[:], p3[:, 0:256], AF.Gelu_apprx_tanh),
                  x2bw, s_ss.inc(1))
        x2bgelu_v[i] = s_ss.val

        ev_x2bf = None
        for c in range(2):
            ev_x2bf = tr_unit([lambda PB, c=c: (PB[:, 0:128], X2bt[:, 128 * c:128 * (c + 1)], idb[:])],
                              (lambda e, PB, c=c: e.copy(X2bf[:, 128 * c:128 * (c + 1)], PB[:, 0:128])),
                              "scalar", [(s_ss, x2bgelu_v[i])] if c == 0 else [])

        a2w = [ev_x2bf]
        if i >= 1:
            a2w.append(carry_v[i - 1])
        first = True
        for h in range(2):
            for c in range(2):
                prog.emit("tensor", lambda t, h=h, c=c:
                          t.matmul(p5[64 * h:64 * (h + 1), 0:64],
                                   X2f[:, 128 * c + 64 * h:128 * c + 64 * (h + 1)],
                                   X2bf[:, 128 * c + 64 * h:128 * c + 64 * (h + 1)],
                                   start=(c == 0), stop=(c == 1)),
                          a2w if first else (), s_mm.inc(1) if (h == 1 and c == 1) else None)
                first = False
        a2v = s_mm.val
        prog.emit("vector", lambda v: v.scalar_tensor_tensor(a2m[:], p5[:, 0:64], 1.0, mskT,
                                                             ALU.add, ALU.mult),
                  [(s_mm, a2v)], s_sv.inc(1))
        a2mp_v[i] = s_sv.val

        zb2w = [(s_sv, a2mp_v[i]), (s_sv, z2evac_v[i]), (s_sv, gz2zv)]
        first = True
        for h in range(2):
            for c in range(2):
                prog.emit("tensor", lambda t, h=h, c=c:
                          t.matmul(p1[64 * h:64 * (h + 1), 0:64],
                                   X2bf[:, 128 * c + 64 * h:128 * c + 64 * (h + 1)],
                                   W2b[:, 64 * (2 * h + c):64 * (2 * h + c + 1)],
                                   start=(c == 0), stop=False),
                          zb2w if first else (), None)
                first = False
            prog.emit("tensor", lambda t, h=h:
                      t.matmul(p1[64 * h:64 * (h + 1), 0:64],
                               a2m[:, 0:64] if False else a2m[:],
                               gz2z[:, 64 * h:64 * (h + 1)], start=False, stop=False),
                      (), None)
            prog.emit("tensor", lambda t, h=h:
                      t.matmul(p1[64 * h:64 * (h + 1), 0:64],
                               ons[0:1, 0:64],
                               b2b2[0:1, 64 * h:64 * (h + 1)], start=False, stop=True),
                      (), s_mm.inc(1) if h == 1 else None)
        z2bv = s_mm.val

        prog.emit("vector", lambda v: v.tensor_copy(sc2[:, 192:256], p1[:, 0:64]),
                  [(s_mm, z2bv)], s_sv.inc(1))
        prog.emit("vector", lambda v: v.reduce_sum(lns[:, 8:9], sc2[:, 192:256], axis=AX.X), (), None)
        prog.emit("vector", lambda v: v.tensor_scalar(lns[:, 8:9], lns[:, 8:9], 1.0 / 64, None, ALU.mult),
                  (), None)
        prog.emit("vector", lambda v: v.tensor_scalar(sc2[:, 192:256], sc2[:, 192:256], lns[:, 8:9], None,
                                                      ALU.subtract), (), None)
        prog.emit("vector", lambda v: v.tensor_tensor(sc2[:, 320:384], sc2[:, 192:256], sc2[:, 192:256],
                                                      ALU.mult), (), None)
        prog.emit("vector", lambda v: v.reduce_sum(lns[:, 9:10], sc2[:, 320:384], axis=AX.X), (), None)
        prog.emit("vector", lambda v: v.tensor_scalar(lns[:, 9:10], lns[:, 9:10], 1.0 / 64, 1e-6,
                                                      ALU.mult, ALU.add), (), s_sv.inc(1))
        var2v = s_sv.val
        ln2i = lns[:, 10:11].bitcast(mybir.dt.int32)
        ln2v = lns[:, 9:10].bitcast(mybir.dt.int32)
        prog.emit("vector", lambda v: v.tensor_scalar(ln2i, ln2v, 1, None, ALU.arith_shift_right),
                  [(s_sv, var2v)], None)
        prog.emit("vector", lambda v: v.tensor_scalar(ln2i, ln2i, -1, 0x5f3759df, ALU.mult, ALU.add),
                  (), None)
        for _ in range(2):
            prog.emit("vector", lambda v: v.tensor_tensor(lns[:, 11:12], lns[:, 10:11], lns[:, 10:11],
                                                          ALU.mult), (), None)
            prog.emit("vector", lambda v: v.tensor_tensor(lns[:, 11:12], lns[:, 11:12], lns[:, 9:10],
                                                          ALU.mult), (), None)
            prog.emit("vector", lambda v: v.tensor_scalar(lns[:, 11:12], lns[:, 11:12], -0.5, 1.5,
                                                          ALU.mult, ALU.add), (), None)
            prog.emit("vector", lambda v: v.tensor_tensor(lns[:, 10:11], lns[:, 10:11], lns[:, 11:12],
                                                          ALU.mult), (), None)
        prog.emit("vector", lambda v: v.tensor_scalar(sc2[:, 192:256], sc2[:, 192:256], lns[:, 10:11],
                                                      None, ALU.mult), (), None)
        prog.emit("vector", lambda v: v.tensor_tensor(sc2[:, 192:256], sc2[:, 192:256], g4s, ALU.mult),
                  (), None)
        prog.emit("vector", lambda v: v.tensor_tensor(sc2[:, 192:256], sc2[:, 192:256], b4s, ALU.add),
                  (), None)
        prog.emit("vector", lambda v: v.tensor_copy(sc2[:, 256:320], xqt[:]), [ev_xqt], s_sv.inc(1))
        ycopy_v[i] = s_sv.val
        prog.emit("vector", lambda v: v.tensor_tensor(yc[:], sc2[:, 192:256], sc2[:, 256:320], ALU.add),
                  (), None)
        prog.emit("vector", lambda v, i=i: v.reduce_sum(sts[:, 2 * i:2 * i + 1], yc[:], axis=AX.X),
                  (), None)
        prog.emit("vector", lambda v: v.tensor_tensor(sc2[:, 320:384], yc[:], yc[:], ALU.mult), (), None)
        prog.emit("vector", lambda v, i=i: v.reduce_sum(sts[:, 2 * i + 1:2 * i + 2], sc2[:, 320:384],
                                                        axis=AX.X), (), None)
        prog.emit("vector", lambda v: v.tensor_copy(ybf[:], yc[:]), (), s_sv.inc(1))
        ybfv = s_sv.val
        tr_unit([(lambda PB, h_=h_: (PB[64 * h_:64 * (h_ + 1), 0:64], ybf[64 * h_:64 * (h_ + 1), :],
                                     id2[64 * h_:64 * (h_ + 1), :]))
                 for h_ in range(2)],
                (lambda e, PB, ii=ii: e.tensor_copy(yfm[:, ii:ii + 64], PB[:, 0:64])),
                "vector", [(s_sv, ybfv)])

        uw1 = [(s_sv, gz1zv), ev_xkt]
        if i >= 1:
            uw1.append(carry_v[i - 1])
        for h in range(2):
            prog.emit("tensor", lambda t, h=h:
                      t.matmul(p4[64 * h:64 * (h + 1), 0:256],
                               xkt[:, 0:64],
                               gz1z[:, 256 * h:256 * (h + 1)], start=True, stop=True),
                      uw1 if h == 0 else (), None)
        prog.emit("tensor", lambda t: t.matmul(p2[0:1, 256:512], o1z[:, 0:1], gz1[:, :],
                                               start=True, stop=True), (), None)
        prog.emit("tensor", lambda t: t.matmul(p3[0:1, 256:512], o1z[:, 1:2], gz1[:, :],
                                               start=True, stop=True),
                  [(s_ss, x2bgelu_v[i])], None)
        wtdw = [(s_ss, z1ev_v[i]), (s_sv, a2mp_v[i]), (s_ss, x2gelu_v[i])]
        for h in range(2):
            prog.emit("tensor", lambda t, h=h:
                      t.matmul(p5[0:64, 256 * h:256 * (h + 1)],
                               gz2z[:, 64 * h:64 * (h + 1)],
                               X2t[:, 0:256], start=True, stop=True),
                      wtdw if h == 0 else (), None)
        for h in range(2):
            prog.emit("tensor", lambda t, h=h:
                      t.matmul(p4[0:1, 256 + 64 * h:320 + 64 * h],
                               o1z[:, h:h + 1],
                               gz2[:, :], start=True, stop=True),
                      (), s_mm.inc(1) if h == 1 else None)
        updmm_v[i] = s_mm.val

        cw = [(s_mm, updmm_v[i])]
        prog.emit("vector", lambda v: v.tensor_tensor(W1f[:], W1f[:], p4[:, 0:256], ALU.add), cw, None)
        prog.emit("vector", lambda v: v.tensor_copy(W1zb[0:64, 0:256], W1f[0:64, :]), (), None)
        prog.emit("vector", lambda v: v.tensor_copy(W1zb[64:128, 256:512], W1f[64:128, :]), (), None)
        prog.emit("vector", lambda v: v.tensor_tensor(WTf[:], WTf[:], p5[0:64, 0:512], ALU.add), (), None)
        prog.emit("vector", lambda v: v.tensor_copy(WTb[:], WTf[:]), (), s_sv.inc(1))
        wtb_ev = (s_sv, s_sv.val)
        prog.emit("vector", lambda v: v.tensor_tensor(b1f[0:1, 0:256], b1f[0:1, 0:256],
                                                      p2[0:1, 256:512], ALU.add), (), None)
        prog.emit("vector", lambda v: v.tensor_tensor(b1f[0:1, 256:512], b1f[0:1, 256:512],
                                                      p3[0:1, 256:512], ALU.add), (), None)
        prog.emit("vector", lambda v: v.tensor_tensor(b2f[0:1, :], b2f[0:1, :],
                                                      p4[0:1, 256:384], ALU.add), (), None)
        prog.emit("vector", lambda v: v.tensor_copy(b1b2[:], b1f[:]), (), None)
        prog.emit("vector", lambda v: v.tensor_copy(b2b2[:], b2f[:]), (), s_sv.inc(1))
        ev_w2b = tr_unit([(lambda PB, hc=hc: (PB[:, 64 * hc[2]:64 * (hc[2] + 1)],
                                              WTb[0:64, 256 * hc[0] + 128 * hc[1]:
                                                  256 * hc[0] + 128 * (hc[1] + 1)],
                                              idb[0:64, 0:64]))
                          for hc in ((0, 0, 0), (0, 1, 1), (1, 0, 2), (1, 1, 3))],
                         (lambda e, PB: e.tensor_copy(W2b[:], PB[:, 0:256])),
                         "vector", [wtb_ev])
        carry_v[i] = ev_w2b

    scan_done_sv = s_sv.val
    prog.dma("gpsimd", out_yfm[:], yfm[:], s_out, [(s_sv, scan_done_sv)])
    prog.dma("gpsimd", out_stats[:], sts[:], s_out, [(s_sv, scan_done_sv)])
    prog.emit("gpsimd", lambda g: g.engine_nop(), [(s_out, 32)], None)
    return dict(allocs=allocs)



def emit_l1_post(nc, prog, y_tm, pnw_rep, pnb_rep, WoT, x2_own, rw,
                 out_x3, out_x3fm, out_emask, out_gate, name="l1p"):
    st = nc.sbuf_tensor
    pt = nc.psum_tensor
    allocs = [
        st(name + "_ylnf", [128, 16384], BF16),
        st(name + "_wot", [128, 8 * 1024], BF16),
        st(name + "_pn", [128, 2048], F32),
        st(name + "_rw", [128, 64], BF16),
        st(name + "_stg", [128, 2 * 1024], F32),
        st(name + "_yld", [128, 2 * 1024], BF16),
        st(name + "_yln", [128, 2 * 1024], F32),
        st(name + "_ylb", [128, 2 * 1024], BF16),
        st(name + "_xr", [128, 2 * 1024], F32),
        st(name + "_o3", [128, 2 * 1024], F32),
        st(name + "_o3b", [128, 2 * 1024], BF16),
        st(name + "_o3f", [128, 2 * 1024], BF16),
        st(name + "_sq", [128, 1024], F32),
        st(name + "_eout", [128, 32], F32),
        st(name + "_gsc", [128, 32], F32),
        st(name + "_fs", [128, 16], F32),
        st(name + "_idb", [128, 128], BF16),
        st(name + "_iota", [128, 384], F32),
        pt(name + "_pA", [128, 512 * 6], F32),
        pt(name + "_pT0", [128, 512], F32),
        pt(name + "_pT1", [128, 512], F32),
    ]
    hs = [a.__enter__() for a in allocs]
    (ylnf, wot, pn, rwsb, stg, yld, yln, ylb, xr, o3, o3b, o3f, sq, eout, gsc, fs, idb, iota,
     pA, pT0, pT1) = hs
    pTb = (pT0[:, 0:256].bitcast(BF16), pT1[:, 0:256].bitcast(BF16))
    rw3 = rwsb[:].rearrange("p (j c) -> p j c", j=8)
    ylnf3 = ylnf[:].rearrange("p (j t) -> p j t", j=8)
    o3f3 = [o3f[:, 1024 * par:1024 * (par + 1)].rearrange("p (j c) -> p j c", j=8) for par in range(2)]

    s_ld = [Sem(nc, name + f"_ld{i}") for i in range(2)]
    s_wld = Sem(nc, name + "_wld")
    s_cast = Sem(nc, name + "_cast")
    s_v = Sem(nc, name + "_v")
    s_s = Sem(nc, name + "_s")
    s_g = Sem(nc, name + "_g")
    s_tr = Sem(nc, name + "_tr")
    s_mm = Sem(nc, name + "_mm")
    s_out = [Sem(nc, name + f"_o{i}") for i in range(4)]

    prog.emit("gpsimd", lambda g: g.iota(iota[:, 0:128], pattern=[[1, 128]], base=0,
                                         channel_multiplier=0, allow_small_or_imprecise_dtypes=True),
              (), s_g.inc(1))
    prog.emit("gpsimd", lambda g: g.iota(iota[:, 128:256], pattern=[[0, 128]], base=0,
                                         channel_multiplier=1, allow_small_or_imprecise_dtypes=True),
              (), s_g.inc(1))
    prog.emit("vector", lambda v: v.tensor_tensor(iota[:, 256:384], iota[:, 0:128], iota[:, 128:256],
                                                  ALU.is_equal), [(s_g, 2)], s_v.inc(1))
    prog.emit("vector", lambda v: v.tensor_copy(idb[:], iota[:, 256:384]), (), s_v.inc(1))
    id_ok = s_v.val
    prog.dma("sync", pn[:, 0:1024], pnw_rep[:], s_wld)
    prog.dma("sync", pn[:, 1024:2048], pnb_rep[:], s_wld)
    for j in range(8):
        prog.dma("sync", stg[:, 0:1024], WoT[128 * j:128 * (j + 1), :], s_wld,
                 [(s_cast, j)] if j >= 1 else [])
        prog.emit("scalar", lambda s, j=j: s.copy(wot[:, 1024 * j:1024 * (j + 1)], stg[:, 0:1024]),
                  [(s_wld, s_wld.val)], s_cast.inc(1))
    prog.dma("sync", stg[:, 1024:1088].rearrange("p (j c) -> p j c", j=8),
             rw.rearrange("(j p) c -> p j c", p=128), s_wld)
    prog.emit("scalar", lambda s: s.copy(rwsb[:], stg[:, 1024:1088]),
              [(s_wld, s_wld.val)], s_cast.inc(1))
    w_ok = s_cast.val

    ntr = 0
    trev = []
    lnv_t = []
    for T in range(16):
        par = T % 2
        lw = [(s_cast, w_ok)] if T < 2 else [(s_v, lnv_t[T - 2])]
        prog.dma("sync", yld[:, 1024 * par:1024 * (par + 1)], y_tm[128 * T:128 * (T + 1), :],
                 s_ld[par], lw)
        ldv = s_ld[par].val
        Y = yld[:, 1024 * par:1024 * (par + 1)]
        L = yln[:, 1024 * par:1024 * (par + 1)]
        B = ylb[:, 1024 * par:1024 * (par + 1)]
        prog.emit("vector", lambda v, Y=Y, L=L: v.tensor_copy(L, Y), [(s_ld[par], ldv)], None)
        prog.emit("vector", lambda v, L=L: v.reduce_sum(fs[:, 0:1], L, axis=AX.X), (), None)
        prog.emit("vector", lambda v: v.tensor_scalar(fs[:, 0:1], fs[:, 0:1], 1.0 / 1024, None, ALU.mult),
                  (), None)
        prog.emit("vector", lambda v, L=L: v.tensor_scalar(L, L, fs[:, 0:1], None, ALU.subtract), (), None)
        prog.emit("vector", lambda v, L=L: v.tensor_tensor(sq[:], L, L, ALU.mult), (), None)
        prog.emit("vector", lambda v: v.reduce_sum(fs[:, 1:2], sq[:], axis=AX.X), (), None)
        prog.emit("vector", lambda v: v.tensor_scalar(fs[:, 1:2], fs[:, 1:2], 1.0 / 1024, 1e-6,
                                                      ALU.mult, ALU.add), (), s_v.inc(1))
        vv = s_v.val
        prog.emit("scalar", lambda s: s.activation(fs[:, 2:3], fs[:, 1:2], AF.Ln), [(s_v, vv)], None)
        prog.emit("scalar", lambda s: s.activation(fs[:, 3:4], fs[:, 2:3], AF.Exp, scale=-0.5),
                  (), s_s.inc(1))
        sv = s_s.val
        prog.emit("vector", lambda v, L=L: v.tensor_scalar(L, L, fs[:, 3:4], None, ALU.mult),
                  [(s_s, sv)], None)
        prog.emit("vector", lambda v, L=L: v.tensor_tensor(L, L, pn[:, 0:1024], ALU.mult), (), None)
        prog.emit("vector", lambda v, L=L: v.tensor_tensor(L, L, pn[:, 1024:2048], ALU.add), (), None)
        prog.emit("vector", lambda v, L=L, B=B: v.tensor_copy(B, L), (), s_v.inc(1))
        bv = s_v.val
        for grp in range(2):
            bank = ntr % 2
            tw = [(s_v, bv), (s_v, id_ok)]
            if ntr >= 2:
                tw.append((s_v, trev[ntr - 2]))
            for jj in range(4):
                j = grp * 4 + jj
                prog.emit("tensor", lambda t, B=B, j=j, bank=bank, jj=jj:
                          t.transpose(pTb[bank][:, 128 * jj:128 * (jj + 1)],
                                      B[:, 128 * j:128 * (j + 1)], idb[:]),
                          tw if jj == 0 else (), s_tr.inc(1) if jj == 3 else None)
            prog.emit("vector", lambda v, T=T, grp=grp, bank=bank:
                      v.tensor_copy(ylnf3[:, grp * 4:(grp + 1) * 4, 128 * T:128 * (T + 1)],
                                    pTb[bank][:].rearrange("p (j t) -> p j t", j=4)),
                      [(s_tr, s_tr.val)], s_v.inc(1))
            trev.append(s_v.val)
            ntr += 1
        lnv_t.append(s_v.val)
    ylnf_ok = s_v.val

    fin_ev = []
    evac_prev = []
    emask_done = []
    for T in range(16):
        par = T % 2
        xw = [(s_v, ylnf_ok)]
        if T >= 2:
            xw.append(fin_ev[T - 2])
        prog.dma("sync", xr[:, 1024 * par:1024 * (par + 1)], x2_own[128 * T:128 * (T + 1), :],
                 s_ld[par], xw)
        xldv = s_ld[par].val
        po = 1024 + 1024 * par
        mw = [(s_cast, w_ok), (s_v, ylnf_ok)]
        if T >= 2:
            mw.append((s_v, evac_prev[T - 2]))
        for hh in range(2):
            for j in range(8):
                prog.emit("tensor", lambda t, T=T, hh=hh, j=j, po=po:
                          t.matmul(pA[:, po + 512 * hh:po + 512 * (hh + 1)],
                                   ylnf3[:, j, 128 * T:128 * (T + 1)],
                                   wot[:, 1024 * j + 512 * hh:1024 * j + 512 * (hh + 1)],
                                   start=(j == 0), stop=(j == 7)),
                          mw if (hh == 0 and j == 0) else (),
                          s_mm.inc(1) if (hh == 1 and j == 7) else None)
        mmv = s_mm.val
        O = o3[:, 1024 * par:1024 * (par + 1)]
        prog.emit("vector", lambda v, O=O, po=po: v.tensor_copy(O, pA[:, po:po + 1024]),
                  [(s_mm, mmv)], s_v.inc(1))
        evac_prev.append(s_v.val)
        prog.emit("vector", lambda v, O=O, par=par: v.tensor_tensor(
                      O, O, xr[:, 1024 * par:1024 * (par + 1)], ALU.add),
                  [(s_ld[par], xldv)], None)
        prog.emit("vector", lambda v, O=O: v.reduce_sum(fs[:, 4:5], O, axis=AX.X), (), None)
        prog.emit("vector", lambda v: v.tensor_scalar(fs[:, 4:5], fs[:, 4:5], 1.0 / 1024, None, ALU.mult),
                  (), None)
        prog.emit("vector", lambda v, O=O: v.tensor_scalar(O, O, fs[:, 4:5], None, ALU.subtract), (), None)
        prog.emit("vector", lambda v, O=O: v.tensor_tensor(sq[:], O, O, ALU.mult), (), None)
        prog.emit("vector", lambda v: v.reduce_sum(fs[:, 5:6], sq[:], axis=AX.X), (), None)
        prog.emit("vector", lambda v: v.tensor_scalar(fs[:, 5:6], fs[:, 5:6], 1.0 / 1024, 1e-5,
                                                      ALU.mult, ALU.add), (), s_v.inc(1))
        v2 = s_v.val
        prog.emit("scalar", lambda s: s.activation(fs[:, 6:7], fs[:, 5:6], AF.Ln), [(s_v, v2)], None)
        prog.emit("scalar", lambda s: s.activation(fs[:, 7:8], fs[:, 6:7], AF.Exp, scale=-0.5),
                  (), s_s.inc(1))
        r2 = s_s.val
        ow = [(s_s, r2)]
        if T >= 2:
            ow.append((s_out[0], 16 * (T - 1)))
        prog.emit("vector", lambda v, O=O: v.tensor_scalar(O, O, fs[:, 7:8], None, ALU.mult), ow, None)
        prog.emit("vector", lambda v, O=O, par=par: v.tensor_copy(o3b[:, 1024 * par:1024 * (par + 1)], O),
                  (), s_v.inc(1))
        bv2 = s_v.val
        fin_ev.append((s_v, bv2))
        prog.dma("gpsimd", out_x3[128 * T:128 * (T + 1), :], O, s_out[0], [(s_v, bv2)])
        for grp in range(2):
            bank = ntr % 2
            tw2 = [(s_v, bv2)]
            if ntr >= 2:
                tw2.append((s_v, trev[ntr - 2]))
            for jj in range(4):
                j = grp * 4 + jj
                prog.emit("tensor", lambda t, par=par, j=j, bank=bank, jj=jj:
                          t.transpose(pTb[bank][:, 128 * jj:128 * (jj + 1)],
                                      o3b[:, 1024 * par + 128 * j:1024 * par + 128 * (j + 1)], idb[:]),
                          tw2 if jj == 0 else (), s_tr.inc(1) if jj == 3 else None)
            prog.emit("vector", lambda v, par=par, grp=grp, bank=bank:
                      v.tensor_copy(o3f3[par][:, grp * 4:(grp + 1) * 4, :].rearrange("p j c -> p (j c)")
                                    if False else
                                    o3f[:, 1024 * par + 512 * grp:1024 * par + 512 * (grp + 1)],
                                    pTb[bank][:]),
                      [(s_tr, s_tr.val)], s_v.inc(1))
            trev.append(s_v.val)
            ntr += 1
        o3f_ok = (s_v, trev[-1])
        prog.dma("gpsimd",
                 out_x3fm[:, 128 * T:128 * (T + 1)].rearrange("(j p) c -> p j c", p=128),
                 o3f[:, 1024 * par:1024 * (par + 1)].rearrange("p (j c) -> p j c", j=8),
                 s_out[1], [o3f_ok])
        rmw = [o3f_ok]
        if T >= 2:
            rmw.append((s_v, emask_done[T - 2]))
        for j in range(8):
            prog.emit("tensor", lambda t, par=par, j=j:
                      t.matmul(pA[:, 512 * par:512 * par + 8],
                               o3f3[par][:, j, :], rw3[:, j, :],
                               start=(j == 0), stop=(j == 7)),
                      rmw if j == 0 else (), s_mm.inc(1) if j == 7 else None)
        rmmv = s_mm.val
        G = gsc
        prog.emit("vector", lambda v, par=par: v.reduce_max(G[:, 0:1], pA[:, 512 * par:512 * par + 8],
                                                            axis=AX.X), [(s_mm, rmmv)], None)
        prog.emit("vector", lambda v: v.tensor_scalar(G[:, 1:2], G[:, 0:1], -1.0, None, ALU.mult),
                  (), None)
        eow = [(s_out[2], 16 * (T - 1)), (s_out[3], 16 * (T - 1))] if T >= 2 else []
        prog.emit("vector", lambda v, par=par: v.tensor_scalar(eout[:, 16 * par:16 * par + 8],
                                                               pA[:, 512 * par:512 * par + 8],
                                                               G[:, 0:1], None, ALU.is_ge),
                  eow, s_v.inc(1))
        gsv = s_v.val
        prog.emit("scalar", lambda s, par=par: s.activation(G[:, 8:16], pA[:, 512 * par:512 * par + 8],
                                                            AF.Exp, bias=G[:, 1:2], accum_out=G[:, 2:3]),
                  [(s_v, gsv)], s_s.inc(1))
        esv = s_s.val
        prog.emit("vector", lambda v, par=par: v.reciprocal(eout[:, 16 * par + 8:16 * par + 9],
                                                            G[:, 2:3]), [(s_s, esv)], s_v.inc(1))
        emv = s_v.val
        emask_done.append(emv)
        prog.dma("gpsimd", out_emask[128 * T:128 * (T + 1), :], eout[:, 16 * par:16 * par + 8],
                 s_out[2], [(s_v, emv)])
        prog.dma("gpsimd", out_gate[128 * T:128 * (T + 1), :], eout[:, 16 * par + 8:16 * par + 9],
                 s_out[3], [(s_v, emv)])
    prog.emit("gpsimd", lambda g: g.engine_nop(),
              [(s_out[0], 16 * 16), (s_out[1], 16 * 16), (s_out[2], 16 * 16), (s_out[3], 16 * 16)], None)
    return dict(allocs=allocs)



def emit_cumsum(nc, prog, emask_all, out_pos, name="csx"):
    st = nc.sbuf_tensor
    pt = nc.psum_tensor
    allocs = [
        st(name + "_em", [128, 2 * 8], BF16),
        st(name + "_emf", [128, 2 * 8], F32),
        st(name + "_offf", [1, 8], F32),
        st(name + "_pos", [128, 2 * 8], F32),
        st(name + "_tri", [128, 128], BF16),
        st(name + "_ons", [128, 128], BF16),
        st(name + "_onf", [1, 128], F32),
        pt(name + "_pA", [128, 512 * 6], F32),
    ]
    hs = [a.__enter__() for a in allocs]
    (em, emf, offf, pos, tri, ons, onf, pA) = hs
    s_eld = [Sem(nc, name + f"_eld{i}") for i in range(2)]
    s_v = Sem(nc, name + "_v")
    s_g = Sem(nc, name + "_g")
    s_mm = Sem(nc, name + "_mm")
    s_out = [Sem(nc, name + f"_o{i}") for i in range(2)]

    prog.emit("vector", lambda v: v.memset(ons[:], 1.0), (), s_v.inc(1))
    prog.emit("gpsimd", lambda g: g.affine_select(tri[:], ons[:], pattern=[[1, 128]],
                                                  compare_op=ALU.is_ge, fill=0.0, base=0,
                                                  channel_multiplier=-1),
              [(s_v, 1)], s_g.inc(1))
    tri_ok = s_g.val
    prog.emit("vector", lambda v: v.memset(offf[:], 0.0), (), s_v.inc(1))
    prog.emit("vector", lambda v: v.memset(onf[:], 1.0), (), s_v.inc(1))
    init_ok = s_v.val
    csum_done = {}
    off_upd = {}
    mcvs = {}
    for T in range(128):
        par = T % 2
        lw = [(s_v, init_ok)]
        if T >= 2:
            lw.append((s_v, csum_done[T - 2]))
        prog.dma("sync", emf[:, 8 * par:8 * (par + 1)], emask_all[128 * T:128 * (T + 1), :],
                 s_eld[par], lw)
        eldv = s_eld[par].val
        cpw = [(s_eld[par], eldv)]
        if T >= 2:
            cpw.append((s_mm, mcvs[T - 2]))
        prog.emit("vector", lambda v, par=par: v.tensor_copy(em[:, 8 * par:8 * (par + 1)],
                                                             emf[:, 8 * par:8 * (par + 1)]),
                  cpw, s_v.inc(1))
        cpv = s_v.val
        mmw = [(s_v, cpv), (s_g, tri_ok), (s_v, init_ok)]
        if T >= 2:
            mmw.append((s_v, csum_done[T - 2]))
        prog.emit("tensor", lambda t, par=par: t.matmul(pA[:, 512 * par:512 * par + 8], tri[:],
                                                        em[:, 8 * par:8 * (par + 1)],
                                                        start=True, stop=True), mmw, s_mm.inc(1))
        mav = s_mm.val
        bw = []
        if T >= 1:
            bw.append((s_v, off_upd[T - 1]))
        prog.emit("tensor", lambda t, par=par: t.matmul(pA[:, 1024 + 512 * par:1024 + 512 * par + 8],
                                                        onf[0:1, :], offf[0:1, 0:8],
                                                        start=True, stop=True), bw, s_mm.inc(1))
        mbv = s_mm.val
        prog.emit("tensor", lambda t, par=par: t.matmul(pA[0:1, 2048 + 512 * par:2048 + 512 * par + 8],
                                                        ons[:, 0:1], em[:, 8 * par:8 * (par + 1)],
                                                        start=True, stop=True), (), s_mm.inc(1))
        mcv = s_mm.val
        mcvs[T] = mcv
        pw = [(s_mm, mav)]
        if T >= 2:
            pw.append((s_out[par], 16 * (T // 2)))
        prog.emit("vector", lambda v, par=par:
                  v.scalar_tensor_tensor(pos[:, 8 * par:8 * (par + 1)],
                                         emf[:, 8 * par:8 * (par + 1)], -1.0,
                                         pA[:, 512 * par:512 * par + 8], ALU.mult, ALU.add),
                  pw, None)
        prog.emit("vector", lambda v, par=par: v.tensor_tensor(
                      pos[:, 8 * par:8 * (par + 1)], pos[:, 8 * par:8 * (par + 1)],
                      pA[:, 1024 + 512 * par:1024 + 512 * par + 8], ALU.add),
                  [(s_mm, mbv)], s_v.inc(1))
        csum_done[T] = s_v.val
        prog.dma("gpsimd", out_pos[128 * T:128 * (T + 1), :], pos[:, 8 * par:8 * (par + 1)],
                 s_out[par], [(s_v, csum_done[T])])
        prog.emit("vector", lambda v, par=par: v.tensor_tensor(
                      offf[0:1, :], offf[0:1, :],
                      pA[0:1, 2048 + 512 * par:2048 + 512 * par + 8], ALU.add),
                  [(s_mm, mcv)], s_v.inc(1))
        off_upd[T] = s_v.val
    prog.emit("gpsimd", lambda g: g.engine_nop(), [(s_out[0], 16 * 64), (s_out[1], 16 * 64)], None)
    return dict(allocs=allocs)


def emit_l1_ffn(nc, prog, emask_own, pos_own, x3_own, x3fm_own, gate_own, w1, b1, w2, b2,
                out, name="l1f"):
    st = nc.sbuf_tensor
    pt = nc.psum_tensor
    allocs = [
        st(name + "_x3fm", [128, 16384], BF16),
        st(name + "_yacc", [128, 16 * 1024], F32),
        st(name + "_hq", [128, 8 * 2048], BF16),
        st(name + "_w1q", [128, 8 * 1024], BF16),
        st(name + "_w2q", [128, 8 * 1024], BF16),
        st(name + "_stg", [128, 2 * 2048], F32),
        st(name + "_b1r", [1, 4096], BF16),
        st(name + "_b2r", [1, 1024], BF16),
        st(name + "_gate", [128, 16 * 8], F32),
        st(name + "_gv", [128, 16], F32),
        st(name + "_em", [128, 2 * 8], F32),
        st(name + "_pos", [128, 16 * 8], F32),
        st(name + "_ons", [128, 512], BF16),
        st(name + "_xr", [128, 2 * 1024], F32),
        st(name + "_sq", [128, 1024], F32),
        st(name + "_fs", [128, 16], F32),
        pt(name + "_pA", [128, 512 * 8], F32),
    ]
    hs = [a.__enter__() for a in allocs]
    (x3fm, yacc, hq, w1q, w2q, stg, b1r, b2r, gate, gv, em, pos, ons,
     xr, sq, fs, pA) = hs
    x3fm3 = x3fm[:].rearrange("p (j c) -> p j c", j=8)
    hq3 = hq[:].rearrange("p (i c) -> p i c", i=8)
    w1q3 = w1q[:].rearrange("p (j c) -> p j c", j=8)
    w2q3 = w2q[:].rearrange("p (i c) -> p i c", i=8)

    s_wld = [Sem(nc, name + f"_wld{i}") for i in range(2)]
    s_eld = [Sem(nc, name + f"_eld{i}") for i in range(2)]
    s_xld = [Sem(nc, name + f"_xld{i}") for i in range(2)]
    s_cast = Sem(nc, name + "_cast")
    s_gcast = Sem(nc, name + "_gcast")
    s_v = Sem(nc, name + "_v")
    s_s = Sem(nc, name + "_s")
    s_g = Sem(nc, name + "_g")
    s_mm = Sem(nc, name + "_mm")
    s_hev = Sem(nc, name + "_hev")
    s_mrg = Sem(nc, name + "_mrg")
    s_out = [Sem(nc, name + f"_o{i}") for i in range(2)]

    prog.emit("vector", lambda v: v.memset(ons[:], 1.0), (), s_v.inc(1))
    prog.dma("sync", x3fm[:].rearrange("p (j c) -> p j c", j=8),
             x3fm_own.rearrange("(j p) c -> p j c", p=128), s_wld[0])
    prog.dma("sync", gv[:], gate_own.rearrange("(t p) c -> p (t c)", p=128), s_wld[0])
    base_ld = s_wld[0].val

    gdone = {}
    for ti in range(16):
        par = ti % 2
        gw = [(s_wld[0], base_ld)]
        if ti >= 2:
            gw.append((s_v, gdone[ti - 2]))
        prog.dma("sync", em[:, 8 * par:8 * (par + 1)],
                 emask_own[128 * ti:128 * (ti + 1), :], s_eld[par], gw)
        prog.dma("sync", pos[:, 8 * ti:8 * (ti + 1)],
                 pos_own[128 * ti:128 * (ti + 1), :], s_eld[par])
        eldv = s_eld[par].val
        prog.emit("vector", lambda v, ti=ti: v.tensor_scalar(gate[:, 8 * ti:8 * (ti + 1)],
                                                             pos[:, 8 * ti:8 * (ti + 1)], 2048.0,
                                                             None, ALU.is_lt),
                  [(s_eld[par], eldv), (s_wld[0], base_ld)], None)
        prog.emit("vector", lambda v, ti=ti, par=par: v.tensor_tensor(
                      gate[:, 8 * ti:8 * (ti + 1)], gate[:, 8 * ti:8 * (ti + 1)],
                      em[:, 8 * par:8 * (par + 1)], ALU.mult), (), None)
        prog.emit("vector", lambda v, ti=ti: v.tensor_scalar(
                      gate[:, 8 * ti:8 * (ti + 1)], gate[:, 8 * ti:8 * (ti + 1)],
                      gv[:, ti:ti + 1], None, ALU.mult), (), s_v.inc(1))
        gdone[ti] = s_v.val
    gates_ok = s_v.val

    s_w2c = Sem(nc, name + "_w2c")
    n_h = 0
    n_mrg = 0
    w1c_n = 0
    w2c_n = 0
    prev_h_done = 0
    prev_o_done = 0
    prev_b_free = 0
    for e in range(8):
        bw = [(s_wld[0], base_ld)] if e == 0 else [(s_mm, prev_o_done)]
        prog.dma("sync", stg[0:1, 0:2048], b1[e].rearrange("(a c) -> a c", a=1)[:, 0:2048],
                 s_wld[1], bw)
        prog.dma("sync", stg[0:1, 2048:4096], b1[e].rearrange("(a c) -> a c", a=1)[:, 2048:4096],
                 s_wld[1])
        bldv = s_wld[1].val
        prog.emit("scalar", lambda s: s.copy(b1r[0:1, :], stg[0:1, 0:4096]),
                  [(s_wld[1], bldv)], s_cast.inc(1))
        b1cv = s_cast.val
        prog.dma("sync", stg[0:1, 0:1024], b2[e].rearrange("(a c) -> a c", a=1), s_wld[1],
                 [(s_cast, b1cv)])
        b2ldv = s_wld[1].val
        prog.emit("scalar", lambda s: s.copy(b2r[0:1, :], stg[0:1, 0:1024]),
                  [(s_wld[1], b2ldv)], s_cast.inc(1))
        b_ok = s_cast.val
        for q in range(4):
            for j in range(8):
                par = j % 2
                lw = [(s_cast, b_ok)]
                if w1c_n >= 2:
                    lw.append((s_gcast, w1c_n - 1))
                prog.dma("sync", stg[:, 2048 * par:2048 * par + 1024],
                         w1[e].rearrange("(c p) f -> c p f", p=128)[j][:, 1024 * q:1024 * (q + 1)],
                         s_wld[0], lw)
                wldv = s_wld[0].val
                cw = [(s_wld[0], wldv), (s_mm, prev_h_done)]
                prog.emit("gpsimd", lambda g, j=j:
                          g.tensor_copy(w1q3[:, j, :], stg[:, 2048 * (j % 2):2048 * (j % 2) + 1024]),
                          cw, s_gcast.inc(1))
                w1c_n += 1
            w1q_ok = s_gcast.val
            for i8 in range(8):
                for t4 in range(4):
                    bank = 6 + (n_h % 2)
                    hw = [(s_gcast, w1q_ok), (s_wld[0], base_ld)]
                    if n_h >= 2:
                        hw.append((s_hev, n_h - 1))
                    if i8 == 0 and t4 == 0 and n_mrg > 0:
                        hw.append((s_mrg, n_mrg))
                    Habs = 1024 * q + 128 * i8
                    prog.emit("tensor", lambda t, Habs=Habs, bank=bank:
                              t.matmul(pA[:, 512 * bank:512 * (bank + 1)],
                                       b1r[0:1, Habs:Habs + 128], ons[0:1, 0:512],
                                       start=True, stop=False),
                              hw, None)
                    for j in range(8):
                        prog.emit("tensor", lambda t, j=j, i8=i8, t4=t4, bank=bank:
                                  t.matmul(pA[:, 512 * bank:512 * (bank + 1)],
                                           w1q3[:, j, 128 * i8:128 * (i8 + 1)],
                                           x3fm3[:, j, 512 * t4:512 * (t4 + 1)],
                                           start=False, stop=(j == 7)),
                                  (), s_mm.inc(1) if j == 7 else None)
                    prog.emit("scalar", lambda s, i8=i8, t4=t4, bank=bank:
                              s.activation(hq3[:, i8, 512 * t4:512 * (t4 + 1)],
                                           pA[:, 512 * bank:512 * (bank + 1)], AF.Relu),
                              [(s_mm, s_mm.val)], s_hev.inc(1))
                    n_h += 1
            prev_h_done = s_mm.val
            h_ok = s_hev.val
            for i8 in range(8):
                par = i8 % 2
                lw2 = [(s_cast, b_ok)]
                if w2c_n >= 2:
                    lw2.append((s_w2c, w2c_n - 1))
                prog.dma("sync", stg[:, 2048 * par + 1024:2048 * par + 2048],
                         w2[e].rearrange("(c p) f -> c p f", p=128)[8 * q + i8],
                         s_wld[1], lw2)
                w2ldv = s_wld[1].val
                cw2 = [(s_wld[1], w2ldv), (s_mm, prev_o_done)]
                prog.emit("scalar", lambda s, par=par, i8=i8:
                          s.copy(w2q3[:, i8, :], stg[:, 2048 * par + 1024:2048 * par + 2048]),
                          cw2, s_w2c.inc(1))
                w2c_n += 1
            w2q_ok = s_w2c.val
            for t4 in range(4):
                ow = [(s_hev, h_ok), (s_w2c, w2q_ok), (s_v, gates_ok)]
                if n_mrg >= 8:
                    ow.append((s_mrg, n_mrg))
                first = True
                for i8 in range(8):
                    for sub in range(4):
                        for h2 in range(2):
                            bank = sub * 2 + h2
                            if q == 0 and i8 == 0:
                                prog.emit("tensor", lambda t, h2=h2, bank=bank:
                                          t.matmul(pA[:, 512 * bank:512 * (bank + 1)],
                                                   ons[0:1, 0:128], b2r[0:1, 512 * h2:512 * (h2 + 1)],
                                                   start=True, stop=False),
                                          ow if first else (), None)
                                first = False
                            prog.emit("tensor", lambda t, i8=i8, t4=t4, sub=sub, h2=h2, bank=bank, q=q:
                                      t.matmul(pA[:, 512 * bank:512 * (bank + 1)],
                                               hq3[:, i8, 512 * t4 + 128 * sub:512 * t4 + 128 * (sub + 1)],
                                               w2q3[:, i8, 512 * h2:512 * (h2 + 1)],
                                               start=(i8 == 0 and q != 0), stop=(i8 == 7)),
                                      ow if first else (),
                                      s_mm.inc(1) if (i8 == 7 and sub == 3 and h2 == 1) else None)
                            first = False
                omv = s_mm.val
                for sub in range(4):
                    for h2 in range(2):
                        bank = sub * 2 + h2
                        s16 = 4 * t4 + sub
                        dst = yacc[:, 1024 * s16 + 512 * h2:1024 * s16 + 512 * (h2 + 1)]
                        gap = gate[:, 8 * s16 + e:8 * s16 + e + 1]
                        if e == 0 and q == 0:
                            prog.emit("vector", lambda v, dst=dst, gap=gap, bank=bank:
                                      v.tensor_scalar(dst, pA[:, 512 * bank:512 * (bank + 1)],
                                                      gap, None, ALU.mult),
                                      [(s_mm, omv)] if (sub == 0 and h2 == 0) else (), s_mrg.inc(1))
                        else:
                            prog.emit("vector", lambda v, dst=dst, gap=gap, bank=bank:
                                      v.scalar_tensor_tensor(dst, pA[:, 512 * bank:512 * (bank + 1)],
                                                             gap, dst, ALU.mult, ALU.add),
                                      [(s_mm, omv)] if (sub == 0 and h2 == 0) else (), s_mrg.inc(1))
                        n_mrg += 1
            prev_o_done = s_mm.val
    mrg_all = s_mrg.val

    fin_ev = []
    for s16 in range(16):
        par = s16 % 2
        xw = [(s_mrg, mrg_all)]
        if s16 >= 2:
            xw.append(fin_ev[s16 - 2])
        prog.dma("sync", xr[:, 1024 * par:1024 * (par + 1)], x3_own[128 * s16:128 * (s16 + 1), :],
                 s_xld[par], xw)
        xldv = s_xld[par].val
        ys = yacc[:, 1024 * s16:1024 * (s16 + 1)]
        prog.emit("vector", lambda v, ys=ys, par=par: v.tensor_tensor(
                      ys, ys, xr[:, 1024 * par:1024 * (par + 1)], ALU.add),
                  [(s_xld[par], xldv), (s_mrg, mrg_all)], None)
        prog.emit("vector", lambda v, ys=ys: v.reduce_sum(fs[:, 0:1], ys, axis=AX.X), (), None)
        prog.emit("vector", lambda v: v.tensor_scalar(fs[:, 0:1], fs[:, 0:1], 1.0 / 1024, None, ALU.mult),
                  (), None)
        prog.emit("vector", lambda v, ys=ys: v.tensor_scalar(ys, ys, fs[:, 0:1], None, ALU.subtract),
                  (), None)
        prog.emit("vector", lambda v, ys=ys: v.tensor_tensor(sq[:], ys, ys, ALU.mult), (), None)
        prog.emit("vector", lambda v: v.reduce_sum(fs[:, 1:2], sq[:], axis=AX.X), (), None)
        prog.emit("vector", lambda v: v.tensor_scalar(fs[:, 1:2], fs[:, 1:2], 1.0 / 1024, 1e-5,
                                                      ALU.mult, ALU.add), (), s_v.inc(1))
        vv = s_v.val
        prog.emit("scalar", lambda s: s.activation(fs[:, 2:3], fs[:, 1:2], AF.Ln), [(s_v, vv)], None)
        prog.emit("scalar", lambda s: s.activation(fs[:, 3:4], fs[:, 2:3], AF.Exp, scale=-0.5),
                  (), s_s.inc(1))
        rv = s_s.val
        ow2 = [(s_s, rv)]
        if s16 >= 2:
            ow2.append((s_out[0], 16 * (s16 - 1)))
        prog.emit("vector", lambda v, ys=ys: v.tensor_scalar(ys, ys, fs[:, 3:4], None, ALU.mult),
                  ow2, s_v.inc(1))
        fin_ev.append((s_v, s_v.val))
        prog.dma("gpsimd", out[128 * s16:128 * (s16 + 1), :], ys, s_out[0], [fin_ev[-1]])
    prog.emit("gpsimd", lambda g: g.engine_nop(), [(s_out[0], 16 * 16)], None)
    return dict(allocs=allocs)


from concourse.bass_utils import run_bass_kernel_spmd


def kernel(**inputs):
    import sys as _sys, time as _time
    _t0 = _time.time()
    def _log(msg):
        print(f"[kernel +{_time.time()-_t0:7.1f}s] {msg}", file=_sys.stderr, flush=True)

    X = np.ascontiguousarray(inputs["L0_attn_x"].reshape(16384, 1024))
    cores = list(range(8))

    nc = bass.Bass("TRN2", target_bir_lowering=False)
    xs = nc.dram_tensor("x_scat", [2048, 1024], F32, kind="ExternalInput").ap()
    xc = nc.dram_tensor("x_cont", [2048, 1024], F32, kind="ExternalInput").ap()
    w = [nc.dram_tensor(n, [1024, 1024], F32, kind="ExternalInput").ap() for n in ("Wq", "Wk", "Wv")]
    b = [nc.dram_tensor(n, [1024], F32, kind="ExternalInput").ap() for n in ("bq", "bk", "bv")]
    q = nc.dram_tensor("q_fm", [1024, 2048], BF16, kind="ExternalOutput").ap()
    k = nc.dram_tensor("k_fm", [1024, 2048], BF16, kind="ExternalOutput").ap()
    v = nc.dram_tensor("v_tm", [2048, 1024], BF16, kind="ExternalOutput").ap()
    prog = Prog(nc)
    emit_l0_proj(nc, prog, xs, xc, w, b, q, k, v)
    prog.run()
    in_maps = []
    for c in cores:
        rows = l0_core_rows(c)
        in_maps.append({"x_scat": X[rows], "x_cont": X[2048 * c:2048 * (c + 1)],
                        "Wq": inputs["L0_attn_Wq"], "Wk": inputs["L0_attn_Wk"],
                        "Wv": inputs["L0_attn_Wv"], "bq": inputs["L0_attn_bq"],
                        "bk": inputs["L0_attn_bk"], "bv": inputs["L0_attn_bv"]})
    _log("start launch1 l0_proj")
    res = run_bass_kernel_spmd(nc, in_maps, core_ids=cores)
    q_own = [res.results[c]["q_fm"] for c in cores]
    k_g = np.concatenate([res.results[c]["k_fm"] for c in cores], axis=0)
    v_g = np.concatenate([res.results[c]["v_tm"] for c in cores], axis=0)

    nc = bass.Bass("TRN2", target_bir_lowering=False)
    qi = nc.dram_tensor("q_own", [1024, 2048], BF16, kind="ExternalInput").ap()
    ki = nc.dram_tensor("k_fm_g", [8192, 2048], BF16, kind="ExternalInput").ap()
    vi = nc.dram_tensor("v_tm_g", [16384, 1024], BF16, kind="ExternalInput").ap()
    mi = nc.dram_tensor("msk", [128, 1024], BF16, kind="ExternalInput").ap()
    xri = nc.dram_tensor("x_res", [2048, 1024], F32, kind="ExternalInput").ap()
    x1o = nc.dram_tensor("x1", [2048, 1024], F32, kind="ExternalOutput").ap()
    prog = Prog(nc)
    emit_l0_attn(nc, prog, qi, ki, vi, mi, xri, x1o)
    prog.run()
    in_maps = []
    for c in cores:
        rows = l0_core_rows(c)
        in_maps.append({"q_own": q_own[c], "k_fm_g": k_g, "v_tm_g": v_g,
                        "msk": l0_masks(c), "x_res": X[rows]})
    _log("start launch2 l0_attn")
    res = run_bass_kernel_spmd(nc, in_maps, core_ids=cores)
    x1_own = [res.results[c]["x1"] for c in cores]

    nc = bass.Bass("TRN2", target_bir_lowering=False)
    x1i = nc.dram_tensor("x1", [2048, 1024], F32, kind="ExternalInput").ap()
    wg1 = nc.dram_tensor("wg1", [1024, 256], F32, kind="ExternalInput").ap()
    bg1 = nc.dram_tensor("bg1", [256], F32, kind="ExternalInput").ap()
    wg2 = nc.dram_tensor("wg2", [256, 8], F32, kind="ExternalInput").ap()
    win = nc.dram_tensor("w_in", [8, 1024, 2048], F32, kind="ExternalInput").ap()
    wout = nc.dram_tensor("w_out", [8, 2048, 1024], F32, kind="ExternalInput").ap()
    x2o = nc.dram_tensor("x2", [2048, 1024], F32, kind="ExternalOutput").ap()
    x2fo = nc.dram_tensor("x2fm", [1024, 2048], BF16, kind="ExternalOutput").ap()
    prog = Prog(nc)
    emit_l0_ffn(nc, prog, x1i, wg1, bg1, wg2, win, wout, x2o, x2fo)
    prog.run()
    in_maps = []
    for c in cores:
        in_maps.append({"x1": x1_own[c], "wg1": inputs["L0_ffn_wg1"], "bg1": inputs["L0_ffn_bg1"],
                        "wg2": inputs["L0_ffn_wg2"], "w_in": inputs["L0_ffn_w_in"],
                        "w_out": inputs["L0_ffn_w_out"]})
    _log("start launch3 l0_ffn")
    res = run_bass_kernel_spmd(nc, in_maps, core_ids=cores)
    x2fm_g = np.concatenate([res.results[c]["x2fm"] for c in cores], axis=0)
    X2_seq = np.empty((16384, 1024), np.float32)
    for c in cores:
        X2_seq[l0_core_rows(c)] = res.results[c]["x2"]

    nc = bass.Bass("TRN2", target_bir_lowering=False)
    d = {}
    d["x2fm_g"] = nc.dram_tensor("x2fm_g", [8192, 2048], BF16, kind="ExternalInput").ap()
    d["projT"] = nc.dram_tensor("projT", [1024, 384], F32, kind="ExternalInput").ap()
    d["lr_exp"] = nc.dram_tensor("lr_exp", [1024, 128], F32, kind="ExternalInput").ap()
    d["bias_row"] = nc.dram_tensor("bias_row", [1, 128], F32, kind="ExternalInput").ap()
    d["l1c"] = nc.dram_tensor("l1c", [128, 960], F32, kind="ExternalInput").ap()
    d["W1_0"] = nc.dram_tensor("W1_0", [128, 256], F32, kind="ExternalInput").ap()
    d["b1_0"] = nc.dram_tensor("b1_0", [1, 512], F32, kind="ExternalInput").ap()
    d["W2_0"] = nc.dram_tensor("W2_0", [128, 256], F32, kind="ExternalInput").ap()
    d["W2T_0"] = nc.dram_tensor("W2T_0", [64, 512], F32, kind="ExternalInput").ap()
    d["b2_0"] = nc.dram_tensor("b2_0", [1, 128], F32, kind="ExternalInput").ap()
    yfm_o = nc.dram_tensor("yfm", [128, 16384], BF16, kind="ExternalOutput").ap()
    sts_o = nc.dram_tensor("stats", [128, 512], F32, kind="ExternalOutput").ap()
    prog = Prog(nc)
    emit_l1_attn(nc, prog, d["x2fm_g"], d["projT"], d["lr_exp"], d["bias_row"], d["l1c"],
                 d["W1_0"], d["b1_0"], d["W2_0"], d["W2T_0"], d["b2_0"], yfm_o, sts_o)
    prog.run()
    in_maps = []
    for c in cores:
        ins = l1_make_inputs(c, inputs["L1_attn_Wq"], inputs["L1_attn_Wk"], inputs["L1_attn_Wv"],
                             inputs["L1_attn_lr_weight"], inputs["L1_attn_lr_bias"],
                             inputs["L1_attn_ttt_norm_weight"], inputs["L1_attn_ttt_norm_bias"],
                             inputs["L1_attn_W1"], inputs["L1_attn_b1"],
                             inputs["L1_attn_W2"], inputs["L1_attn_b2"])
        in_maps.append({"x2fm_g": x2fm_g, "projT": ins["projT"], "lr_exp": ins["lr_exp"],
                        "bias_row": ins["bias_row"], "l1c": ins["l1c"], "W1_0": ins["W1_0"],
                        "b1_0": ins["b1_0"], "W2_0": ins["W2_0"], "W2T_0": ins["W2T_0"],
                        "b2_0": ins["b2_0"]})
    _log("start launch4 l1_attn")
    res = run_bass_kernel_spmd(nc, in_maps, core_ids=cores)
    y_fm = np.concatenate([res.results[c]["yfm"] for c in cores], axis=0)
    y_tm = np.ascontiguousarray(y_fm.T)

    nc = bass.Bass("TRN2", target_bir_lowering=False)
    yti = nc.dram_tensor("y_tm", [2048, 1024], BF16, kind="ExternalInput").ap()
    pnw = nc.dram_tensor("pnw", [128, 1024], F32, kind="ExternalInput").ap()
    pnb = nc.dram_tensor("pnb", [128, 1024], F32, kind="ExternalInput").ap()
    woi = nc.dram_tensor("WoT", [1024, 1024], F32, kind="ExternalInput").ap()
    x2i = nc.dram_tensor("x2_own", [2048, 1024], F32, kind="ExternalInput").ap()
    rwi = nc.dram_tensor("rw", [1024, 8], F32, kind="ExternalInput").ap()
    x3o = nc.dram_tensor("x3", [2048, 1024], F32, kind="ExternalOutput").ap()
    x3fo = nc.dram_tensor("x3fm", [1024, 2048], BF16, kind="ExternalOutput").ap()
    emo = nc.dram_tensor("emask", [2048, 8], F32, kind="ExternalOutput").ap()
    gto = nc.dram_tensor("gate", [2048, 1], F32, kind="ExternalOutput").ap()
    prog = Prog(nc)
    emit_l1_post(nc, prog, yti, pnw, pnb, woi, x2i, rwi, x3o, x3fo, emo, gto)
    prog.run()
    pnw_rep = np.ascontiguousarray(np.broadcast_to(inputs["L1_attn_post_norm_weight"][None, :], (128, 1024)))
    pnb_rep = np.ascontiguousarray(np.broadcast_to(inputs["L1_attn_post_norm_bias"][None, :], (128, 1024)))
    WoT = np.ascontiguousarray(inputs["L1_attn_Wo"].T)
    in_maps = []
    for c in cores:
        in_maps.append({"y_tm": y_tm[2048 * c:2048 * (c + 1)], "pnw": pnw_rep, "pnb": pnb_rep,
                        "WoT": WoT, "x2_own": X2_seq[2048 * c:2048 * (c + 1)],
                        "rw": inputs["L1_ffn_router_w"]})
    _log("start launch5 l1_post")
    res = run_bass_kernel_spmd(nc, in_maps, core_ids=cores)
    x3_own = [res.results[c]["x3"] for c in cores]
    x3fm_own = [res.results[c]["x3fm"] for c in cores]
    gate_own = [res.results[c]["gate"] for c in cores]
    emask_own = [res.results[c]["emask"] for c in cores]
    emask_all = np.concatenate(emask_own, axis=0)

    nc = bass.Bass("TRN2", target_bir_lowering=False)
    emi = nc.dram_tensor("emask_all", [16384, 8], F32, kind="ExternalInput").ap()
    pso = nc.dram_tensor("pos_all", [16384, 8], F32, kind="ExternalOutput").ap()
    prog = Prog(nc)
    emit_cumsum(nc, prog, emi, pso)
    prog.run()
    _log("start launch6 cumsum")
    res = run_bass_kernel_spmd(nc, [{"emask_all": emask_all}], core_ids=[0])
    pos_all = res.results[0]["pos_all"]

    nc = bass.Bass("TRN2", target_bir_lowering=False)
    emi = nc.dram_tensor("emask_own", [2048, 8], F32, kind="ExternalInput").ap()
    psi = nc.dram_tensor("pos_own", [2048, 8], F32, kind="ExternalInput").ap()
    x3i = nc.dram_tensor("x3_own", [2048, 1024], F32, kind="ExternalInput").ap()
    x3fi = nc.dram_tensor("x3fm_own", [1024, 2048], BF16, kind="ExternalInput").ap()
    gti = nc.dram_tensor("gate_own", [2048, 1], F32, kind="ExternalInput").ap()
    w1i = nc.dram_tensor("w1", [8, 1024, 4096], F32, kind="ExternalInput").ap()
    b1i = nc.dram_tensor("b1", [8, 4096], F32, kind="ExternalInput").ap()
    w2i = nc.dram_tensor("w2", [8, 4096, 1024], F32, kind="ExternalInput").ap()
    b2i = nc.dram_tensor("b2", [8, 1024], F32, kind="ExternalInput").ap()
    oto = nc.dram_tensor("out", [2048, 1024], F32, kind="ExternalOutput").ap()
    prog = Prog(nc)
    emit_l1_ffn(nc, prog, emi, psi, x3i, x3fi, gti, w1i, b1i, w2i, b2i, oto)
    with nc.allow_non_contiguous_dma(reason="gate_own column load, 8KB once"):
        prog.run()
    in_maps = []
    for c in cores:
        in_maps.append({"emask_own": emask_own[c], "pos_own": pos_all[2048 * c:2048 * (c + 1)],
                        "x3_own": x3_own[c], "x3fm_own": x3fm_own[c], "gate_own": gate_own[c],
                        "w1": inputs["L1_ffn_w1"], "b1": inputs["L1_ffn_b1"],
                        "w2": inputs["L1_ffn_w2"], "b2": inputs["L1_ffn_b2"]})
    _log("start launch7 l1_ffn")
    res = run_bass_kernel_spmd(nc, in_maps, core_ids=cores)
    out = np.concatenate([res.results[c]["out"] for c in cores], axis=0)
    return out.reshape(1, 16384, 1024).astype(np.float32, copy=False)


# version 04 (final)
# speedup vs baseline: 1.0133x; 1.0133x over previous
import math
import numpy as np
import concourse.bass as bass
import concourse.mybir as mybir

F32 = mybir.dt.float32
BF16 = mybir.dt.bfloat16
AF = mybir.ActivationFunctionType
ALU = mybir.AluOpType
AX = mybir.AxisListType

N_CORES = 8
D = 1024
SEQ = 16384
TPC = SEQ // N_CORES
STRIP = 1024
N_STRIPS = SEQ // STRIP


class Sem:

    def __init__(self, nc, name):
        self.name = name
        self._cm = nc.semaphore(name)
        self.h = self._cm.__enter__()
        self.val = 0

    def inc(self, amount):
        self.val += amount
        return (self, amount)


class Prog:

    ENGINES = ("tensor", "vector", "scalar", "gpsimd", "sync")
    CHAINED = ("vector", "scalar", "gpsimd")

    def __init__(self, nc):
        self.nc = nc
        self.items = {e: [] for e in self.ENGINES}
        self.chain = {e: Sem(nc, f"chain_{e}") for e in self.CHAINED}
        self.last = {e: None for e in self.CHAINED}

    def emit(self, eng, fn, waits=(), inc=None, is_dma=False):
        waits = list(waits)
        if eng in self.CHAINED and not is_dma:
            if self.last[eng] is not None:
                waits.append(self.last[eng])
            if inc is None:
                inc = self.chain[eng].inc(1)
                sem_obj = self.chain[eng]
            else:
                sem_obj = inc[0]
            self.last[eng] = (sem_obj, sem_obj.val)
        self.items[eng].append((fn, waits, inc))

    def dma(self, eng, out, in_, sem, waits=(), **kw):
        inc = sem.inc(16)
        self.emit(eng, lambda e: e.dma_start(out=out, in_=in_, **kw), waits, inc, is_dma=True)
        return sem.val

    def barrier_event(self, eng):
        return self.last.get(eng)

    def run(self):
        nc = self.nc
        with nc.Block() as block:
            for eng_name in self.ENGINES:
                items = self.items[eng_name]
                if not items:
                    continue

                def mk(items):
                    def body(eng):
                        for fn, waits, inc in items:
                            for s, v in waits:
                                eng.wait_ge(s.h if isinstance(s, Sem) else s, v)
                            ins = fn(eng)
                            if inc is not None:
                                sem, amt = inc
                                ins.then_inc(sem.h if isinstance(sem, Sem) else sem, amt)
                    return body

                getattr(block, eng_name)(mk(items))


def ceil_div(a, b):
    return (a + b - 1) // b


class LoadRing:

    def __init__(self, nc, prog, name, k, queue="sync"):
        self.prog = prog
        self.k = k
        self.queue = queue
        self.ld = [Sem(nc, f"{name}_ld{i}") for i in range(k)]
        self.n = 0
        self.uses = [0] * k

    def load(self, out, in_, extra_waits=()):
        slot = self.n % self.k
        self.prog.dma(self.queue, out, in_, self.ld[slot], list(extra_waits))
        self.uses[slot] += 1
        self.n += 1
        return slot, 16 * self.uses[slot]

    def consumer_wait(self, slot_info):
        slot, v = slot_info
        return (self.ld[slot], v)



def l0_qtile_of(core, slot):
    return core + 8 * slot


def l0_core_rows(core):
    rows = []
    for k in range(16):
        t = l0_qtile_of(core, k)
        rows.extend(range(128 * t, 128 * (t + 1)))
    return np.array(rows)


def l0_masks(core):
    f = np.arange(512)[None, :]
    pp = np.arange(128)[:, None]
    if core < 4:
        off = 128 * core
        maskA = (f <= off + pp).astype(np.float32)
        maskB = np.zeros((128, 512), np.float32)
    else:
        off = 128 * (core - 4)
        maskA = np.ones((128, 512), np.float32)
        maskB = (f <= off + pp).astype(np.float32)
    import ml_dtypes
    return np.concatenate([maskA, maskB], axis=1).astype(ml_dtypes.bfloat16)



def emit_l0_proj(nc, prog, x_scat, x_cont, w_aps, b_aps, out_q, out_k, out_v, name="l0p"):
    st = nc.sbuf_tensor
    pt = nc.psum_tensor
    allocs = [
        st(name + "_wld", [128, 1024 * 2], F32),
        st(name + "_wbf", [128, 1024 * 2], BF16),
        st(name + "_wt", [128, 3 * 8192], BF16),
        st(name + "_xq", [128, 16384], BF16),
        st(name + "_xc", [128, 16384], BF16),
        st(name + "_idf", [128, 128], F32),
        st(name + "_idb", [128, 128], BF16),
        st(name + "_iota", [128, 256], F32),
        st(name + "_bias", [1, 3 * 1024], BF16),
        st(name + "_biasf", [1, 3 * 1024], F32),
        st(name + "_ones", [1, 512], BF16),
        st(name + "_ev", [128, 512 * 3], BF16),
        pt(name + "_ptrA", [128, 512], BF16),
        pt(name + "_ptrB", [128, 512], BF16),
        pt(name + "_pmm", [128, 512 * 3], F32),
    ]
    hs = [a.__enter__() for a in allocs]
    (wld, wbf, wt, xq, xc, idf, idb, iota, bias, biasf, ones, ev, ptrA, ptrB, pmm) = hs
    ptr = (ptrA, ptrB)

    s_misc = Sem(nc, name + "_misc")
    s_miscv = Sem(nc, name + "_miscv")
    s_bias = Sem(nc, name + "_bias")
    s_bld = Sem(nc, name + "_bld")
    s_cast = Sem(nc, name + "_cast")
    s_tr = Sem(nc, name + "_tr")
    s_trev = Sem(nc, name + "_trev")
    s_mm = Sem(nc, name + "_mm")
    s_evb = [Sem(nc, name + f"_evb{b}") for b in range(3)]
    s_outb = [Sem(nc, name + f"_outb{b}") for b in range(3)]
    ring = LoadRing(nc, prog, name + "_in", 2, "sync")

    prog.emit("gpsimd", lambda g: g.iota(iota[:, 0:128], pattern=[[1, 128]], base=0,
                                         channel_multiplier=0, allow_small_or_imprecise_dtypes=True),
              (), s_misc.inc(1))
    prog.emit("gpsimd", lambda g: g.iota(iota[:, 128:256], pattern=[[0, 128]], base=0,
                                         channel_multiplier=1, allow_small_or_imprecise_dtypes=True),
              (), s_misc.inc(1))
    v0 = s_misc.val
    prog.emit("vector", lambda v: v.tensor_tensor(idf[:], iota[:, 0:128], iota[:, 128:256], ALU.is_equal),
              [(s_misc, v0)], s_miscv.inc(1))
    prog.emit("vector", lambda v: v.tensor_copy(idb[:], idf[:]), (), s_miscv.inc(1))
    prog.emit("vector", lambda v: v.memset(ones[:], 1.0), (), s_miscv.inc(1))
    id_ready = 2
    ones_ready = 3

    for bi, b_ap in enumerate(b_aps):
        prog.dma("sync", biasf[0:1, 1024 * bi:1024 * (bi + 1)],
                 b_ap.rearrange("(o k) -> o k", o=1), s_bld)
    prog.emit("scalar", lambda s: s.copy(bias[:], biasf[:]), [(s_bld, s_bld.val)], s_bias.inc(1))

    tr_groups = 0

    def chunk_pipeline(in_dram, dst_fn):
        nonlocal tr_groups
        n = ring.n
        buf = n % 2
        lw = [(s_cast, n - 1)] if n >= 2 else []
        sl = ring.load(wld[:, 1024 * buf:1024 * (buf + 1)], in_dram, lw)
        cw = [ring.consumer_wait(sl)]
        if n >= 2:
            cw.append((s_tr, (n - 1) * 2))
        prog.emit("scalar", lambda s, buf=buf: s.copy(wbf[:, 1024 * buf:1024 * (buf + 1)],
                                                      wld[:, 1024 * buf:1024 * (buf + 1)]),
                  cw, s_cast.inc(1))
        castv = s_cast.val
        for grp in range(2):
            bank = tr_groups % 2
            waits = [(s_cast, castv), (s_miscv, id_ready)]
            if tr_groups >= 2:
                waits.append((s_trev, tr_groups - 1))
            for jj in range(4):
                j = grp * 4 + jj
                inc = s_tr.inc(1) if jj == 3 else None
                prog.emit("tensor",
                          lambda t, buf=buf, j=j, bank=bank, jj=jj:
                          t.transpose(ptr[bank][:, 128 * jj:128 * (jj + 1)],
                                      wbf[:, 1024 * buf + 128 * j:1024 * buf + 128 * (j + 1)],
                                      idb[:]),
                          waits if jj == 0 else (), inc)
            trv = s_tr.val
            prog.emit("vector",
                      lambda v, dst_fn=dst_fn, grp=grp, bank=bank:
                      v.tensor_copy(dst_fn(grp),
                                    ptr[bank][:].rearrange("p (j t) -> p j t", j=4)),
                      [(s_tr, trv)], s_trev.inc(1))
            tr_groups += 1

    wt3 = wt[:].rearrange("p (w j t) -> p w j t", w=3, j=8)
    xq3 = xq[:].rearrange("p (j t) -> p j t", j=8)
    xc3 = xc[:].rearrange("p (j t) -> p j t", j=8)
    for wi, w_ap in enumerate(w_aps):
        for i in range(8):
            chunk_pipeline(w_ap[128 * i:128 * (i + 1), :],
                           lambda grp, wi=wi, i=i: wt3[:, wi, grp * 4:(grp + 1) * 4, 128 * i:128 * (i + 1)])
    for ti in range(16):
        chunk_pipeline(x_scat[128 * ti:128 * (ti + 1), :],
                       lambda grp, ti=ti: xq3[:, grp * 4:(grp + 1) * 4, 128 * ti:128 * (ti + 1)])
    for ti in range(16):
        chunk_pipeline(x_cont[128 * ti:128 * (ti + 1), :],
                       lambda grp, ti=ti: xc3[:, grp * 4:(grp + 1) * 4, 128 * ti:128 * (ti + 1)])

    all_trev = tr_groups
    nout = 0

    def proj_group(mm_list, evac_dst_dram):
        nonlocal nout
        bank = nout % 3
        use = nout // 3
        waits = [(s_trev, all_trev), (s_bias, 1), (s_miscv, ones_ready)]
        if use >= 1:
            waits.append((s_evb[bank], use))
        n_mm = len(mm_list)
        for mi, mmfn in enumerate(mm_list):
            inc = s_mm.inc(1) if mi == n_mm - 1 else None
            prog.emit("tensor", lambda t, mmfn=mmfn, bank=bank: mmfn(t, bank),
                      waits if mi == 0 else (), inc)
        mmv = s_mm.val
        ew = [(s_mm, mmv)]
        if use >= 1:
            ew.append((s_outb[bank], 16 * use))
        eng = ("scalar", "vector", "scalar")[bank]
        fn = (lambda e, bank=bank: e.copy(ev[:, 512 * bank:512 * (bank + 1)],
                                          pmm[:, 512 * bank:512 * (bank + 1)])) if eng == "scalar" else \
             (lambda e, bank=bank: e.tensor_copy(ev[:, 512 * bank:512 * (bank + 1)],
                                                 pmm[:, 512 * bank:512 * (bank + 1)]))
        prog.emit(eng, fn, ew, s_evb[bank].inc(1))
        prog.dma("gpsimd", evac_dst_dram, ev[:, 512 * bank:512 * (bank + 1)], s_outb[bank],
                 [(s_evb[bank], use + 1)])
        nout += 1

    for wi, xsrc, out_dram in ((0, xq3, out_q), (1, xc3, out_k)):
        for i in range(8):
            for t4 in range(4):
                mms = [lambda t, bank, wi=wi, i=i:
                       t.matmul(pmm[:, 512 * bank:512 * (bank + 1)],
                                bias[0:1, 1024 * wi + 128 * i:1024 * wi + 128 * (i + 1)],
                                ones[0:1, :], start=True, stop=False)]
                for j in range(8):
                    mms.append(lambda t, bank, wi=wi, i=i, j=j, t4=t4, xsrc=xsrc:
                               t.matmul(pmm[:, 512 * bank:512 * (bank + 1)],
                                        wt3[:, wi, j, 128 * i:128 * (i + 1)],
                                        xsrc[:, j, 512 * t4:512 * (t4 + 1)],
                                        start=False, stop=(j == 7)))
                proj_group(mms, out_dram[128 * i:128 * (i + 1), 512 * t4:512 * (t4 + 1)])
    for si in range(16):
        for h in range(2):
            mms = [lambda t, bank, h=h:
                   t.matmul(pmm[:, 512 * bank:512 * (bank + 1)],
                            ones[0:1, 0:128],
                            bias[0:1, 2048 + 512 * h:2048 + 512 * (h + 1)],
                            start=True, stop=False)]
            for j in range(8):
                mms.append(lambda t, bank, si=si, h=h, j=j:
                           t.matmul(pmm[:, 512 * bank:512 * (bank + 1)],
                                    xc3[:, j, 128 * si:128 * (si + 1)],
                                    wt3[:, 2, j, 512 * h:512 * (h + 1)],
                                    start=False, stop=(j == 7)))
            proj_group(mms, out_v[128 * si:128 * (si + 1), 512 * h:512 * (h + 1)])

    for b in range(3):
        cnt = (nout + 2 - b) // 3
        prog.emit("gpsimd", lambda g: g.engine_nop(), [(s_outb[b], 16 * cnt)], None)
    return dict(allocs=allocs)



def emit_l0_attn(nc, prog, q_own, k_fm_g, v_tm_g, msk_in, x_res, out_x1, name="l0a", dbg_phase=99):
    SC = 1.0 / 32.0
    st = nc.sbuf_tensor
    pt = nc.psum_tensor
    allocs = [
        st(name + "_q", [128, 8 * 2048], BF16),
        st(name + "_k", [128, 2 * 4096], BF16),
        st(name + "_v", [128, 2 * 4096], BF16),
        st(name + "_acc", [128, 16 * 1024], F32),
        st(name + "_m", [128, 16 * 4], F32),
        st(name + "_P", [128, 2 * 512], BF16),
        st(name + "_PT", [128, 2 * 512], BF16),
        st(name + "_msk", [128, 1024], BF16),
        st(name + "_idb", [128, 128], BF16),
        st(name + "_iota", [128, 256], F32),
        st(name + "_idf", [128, 128], F32),
        st(name + "_xr", [128, 2 * 1024], F32),
        st(name + "_fin", [128, 4 * 1024], F32),
        st(name + "_o1", [128, 2 * 1024], F32),
        st(name + "_fs", [128, 16], F32),
        pt(name + "_pS0", [128, 512], F32),
        pt(name + "_pS1", [128, 512], F32),
        pt(name + "_pT0", [128, 512], BF16),
        pt(name + "_pT1", [128, 512], BF16),
        pt(name + "_pV0", [128, 512], F32),
        pt(name + "_pV1", [128, 512], F32),
    ]
    hs = [a.__enter__() for a in allocs]
    (q_sb, k_sb, v_sb, acc, msb, P_sb, PT_sb, msk, idb, iota, idf,
     xr, fin, o1, fs, pS0, pS1, pT0, pT1, pV0, pV1) = hs
    pS = (pS0, pS1)
    pT = (pT0, pT1)

    s_qld = Sem(nc, name + "_qld")
    s_mld = Sem(nc, name + "_mld")
    s_kld = [Sem(nc, name + f"_kld{i}") for i in range(2)]
    s_vld = [Sem(nc, name + f"_vld{i}") for i in range(2)]
    s_xld = [Sem(nc, name + f"_xld{i}") for i in range(2)]
    s_pair = Sem(nc, name + "_pair")
    s_smm = Sem(nc, name + "_smm")
    s_tr = Sem(nc, name + "_tr")
    s_exp = Sem(nc, name + "_exp")
    s_ptev = Sem(nc, name + "_ptev")
    s_stat = Sem(nc, name + "_stat")
    s_lup = Sem(nc, name + "_lup")
    s_accm = Sem(nc, name + "_accm")
    s_out = [Sem(nc, name + f"_out{i}") for i in range(2)]
    s_gmisc = Sem(nc, name + "_gmisc")
    s_vmisc = Sem(nc, name + "_vmisc")

    prog.emit("gpsimd", lambda g: g.iota(iota[:, 0:128], pattern=[[1, 128]], base=0,
                                         channel_multiplier=0, allow_small_or_imprecise_dtypes=True),
              (), s_gmisc.inc(1))
    prog.emit("gpsimd", lambda g: g.iota(iota[:, 128:256], pattern=[[0, 128]], base=0,
                                         channel_multiplier=1, allow_small_or_imprecise_dtypes=True),
              (), s_gmisc.inc(1))
    prog.emit("vector", lambda v: v.tensor_tensor(idf[:], iota[:, 0:128], iota[:, 128:256], ALU.is_equal),
              [(s_gmisc, 2)], s_vmisc.inc(1))
    prog.emit("vector", lambda v: v.tensor_copy(idb[:], idf[:]), (), s_vmisc.inc(1))
    id_ready = s_vmisc.val
    prog.emit("vector", lambda v: v.memset(msb[:], 0.0), (), s_vmisc.inc(1))
    prog.emit("vector", lambda v: v.tensor_scalar(msb[:].rearrange("p (q c) -> p q c", c=4)[:, :, 0:2],
                                                  msb[:].rearrange("p (q c) -> p q c", c=4)[:, :, 0:2],
                                                  0.0, -1e30, ALU.mult, ALU.add), (), s_vmisc.inc(1))
    prog.emit("vector", lambda v: v.memset(acc[:], 0.0), (), s_vmisc.inc(1))
    state_ready = s_vmisc.val

    prog.dma("sync", msk[:], msk_in[:], s_mld)
    for j in range(8):
        prog.dma("sync", q_sb[:, 2048 * j:2048 * (j + 1)],
                 q_own[128 * j:128 * (j + 1), :], s_qld)
    q_ready = s_qld.val

    np_pair = 0
    kv_last_pair = []
    qcount = [0] * 16
    total_pairs = sum(16 - (t // 2) for t in range(32))

    for t in range(32):
        slot = t % 2
        kw = []
        if t >= 2 and dbg_phase >= 5:
            kw.append((s_pair, kv_last_pair[t - 2]))
        r, off = t // 4, (t % 4) * 512
        prog.dma("sync",
                 k_sb[:, 4096 * slot:4096 * (slot + 1)].rearrange("p (j c) -> p j c", j=8),
                 k_fm_g[1024 * r:1024 * (r + 1), off:off + 512].rearrange("(j p) c -> p j c", p=128),
                 s_kld[slot], kw)
        kldv = s_kld[slot].val
        prog.dma("sync",
                 v_sb[:, 4096 * slot:4096 * (slot + 1)].rearrange("p (s d) -> p s d", s=4),
                 v_tm_g[512 * t:512 * (t + 1), :].rearrange("(s p) d -> p s d", p=128),
                 s_vld[slot], kw)
        vldv = s_vld[slot].val

        for k in range(t // 2, 16):
            if dbg_phase < 1:
                continue
            n = np_pair
            masked = (k == t // 2)
            mask_sel = t % 2
            k_ = qcount[k]
            qcount[k] += 1
            mold_c, mnew_c = 4 * k + (k_ % 2), 4 * k + ((k_ + 1) % 2)
            mc_c, l_c = 4 * k + 2, 4 * k + 3
            sbank = n % 2
            tbank = n % 2
            vpar = n % 2
            rs_col = 2 + (n % 2)

            w = [(s_qld, q_ready), (s_kld[slot], kldv)]
            if n >= 2:
                w.append((s_exp, n - 1))
            for j in range(8):
                prog.emit("tensor",
                          lambda tt, j=j, k=k, sbank=sbank, slot=slot:
                          tt.matmul(pS[sbank][:],
                                    q_sb[:, 2048 * j + 128 * k:2048 * j + 128 * (k + 1)],
                                    k_sb[:, 4096 * slot + 512 * j:4096 * slot + 512 * (j + 1)],
                                    start=(j == 0), stop=(j == 7)),
                          w if j == 0 else (), s_smm.inc(1) if j == 7 else None)
            smmv = s_smm.val

            if dbg_phase < 2:
                np_pair += 1
                continue
            rmw = [(s_smm, smmv), (s_vmisc, state_ready)]
            if n >= 1:
                rmw.append((s_exp, n))
            prog.emit("vector", lambda v, sbank=sbank, mc_c=mc_c:
                      v.reduce_max(msb[:, mc_c:mc_c + 1], pS[sbank][:], axis=AX.X),
                      rmw, None)
            prog.emit("vector", lambda v, mc_c=mc_c: v.tensor_scalar(
                          msb[:, mc_c:mc_c + 1], msb[:, mc_c:mc_c + 1], SC, None, ALU.mult),
                      (), None)
            prog.emit("vector", lambda v, mc_c=mc_c, mold_c=mold_c, mnew_c=mnew_c:
                      v.tensor_tensor(msb[:, mnew_c:mnew_c + 1], msb[:, mold_c:mold_c + 1],
                                      msb[:, mc_c:mc_c + 1], ALU.max),
                      (), None)
            prog.emit("vector", lambda v, mc_c=mc_c, mnew_c=mnew_c: v.tensor_scalar(
                          msb[:, mc_c:mc_c + 1], msb[:, mnew_c:mnew_c + 1], -1.0, None, ALU.mult),
                      (), s_stat.inc(1))
            statv = s_stat.val
            prog.emit("scalar", lambda s, mold_c=mold_c, mc_c=mc_c:
                      s.activation(msb[:, mold_c:mold_c + 1], msb[:, mold_c:mold_c + 1],
                                   AF.Exp, bias=msb[:, mc_c:mc_c + 1], scale=1.0),
                      [(s_stat, statv)], None)

            if dbg_phase < 3:
                np_pair += 1
                continue
            pw = []
            if n >= 2:
                pw.append((s_tr, n - 1))
                pw.append((s_lup, n - 1))
            if masked:
                prog.emit("scalar", lambda s, sbank=sbank, vpar=vpar, mc_c=mc_c:
                          s.activation(P_sb[:, 512 * vpar:512 * (vpar + 1)], pS[sbank][:],
                                       AF.Exp, bias=msb[:, mc_c:mc_c + 1], scale=SC),
                          pw, s_exp.inc(1))
                expv = s_exp.val
                prog.emit("vector", lambda v, vpar=vpar, mask_sel=mask_sel:
                          v.tensor_tensor(P_sb[:, 512 * vpar:512 * (vpar + 1)],
                                          P_sb[:, 512 * vpar:512 * (vpar + 1)],
                                          msk[:, 512 * mask_sel:512 * (mask_sel + 1)], ALU.mult),
                          [(s_exp, expv), (s_mld, 16)], None)
                prog.emit("vector", lambda v, vpar=vpar, rs_col=rs_col:
                          v.reduce_sum(fs[:, rs_col:rs_col + 1],
                                       P_sb[:, 512 * vpar:512 * (vpar + 1)], axis=AX.X),
                          (), s_stat.inc(1))
            else:
                prog.emit("scalar", lambda s, sbank=sbank, vpar=vpar, mc_c=mc_c, rs_col=rs_col:
                          s.activation(P_sb[:, 512 * vpar:512 * (vpar + 1)], pS[sbank][:],
                                       AF.Exp, bias=msb[:, mc_c:mc_c + 1], scale=SC,
                                       accum_out=fs[:, rs_col:rs_col + 1]),
                          pw, s_exp.inc(1))
                prog.emit("vector", lambda v: v.engine_nop(), [(s_exp, s_exp.val)], s_stat.inc(1))
            rsv = s_stat.val

            prog.emit("vector", lambda v, l_c=l_c, mold_c=mold_c, rs_col=rs_col:
                      v.scalar_tensor_tensor(msb[:, l_c:l_c + 1], msb[:, l_c:l_c + 1],
                                             msb[:, mold_c:mold_c + 1], fs[:, rs_col:rs_col + 1],
                                             ALU.mult, ALU.add),
                      [(s_exp, n + 1)], s_lup.inc(1))

            if dbg_phase < 4:
                np_pair += 1
                continue
            tw = [(s_stat, rsv), (s_vmisc, id_ready)]
            if n >= 2:
                tw.append((s_ptev, n - 1))
            for k4 in range(4):
                prog.emit("tensor",
                          lambda tt, k4=k4, vpar=vpar, tbank=tbank:
                          tt.transpose(pT[tbank][:, 128 * k4:128 * (k4 + 1)],
                                       P_sb[:, 512 * vpar + 128 * k4:512 * vpar + 128 * (k4 + 1)],
                                       idb[:]),
                          tw if k4 == 0 else (), s_tr.inc(1) if k4 == 3 else None)
            trv = s_tr.val
            prog.emit("scalar", lambda s, tbank=tbank, vpar=vpar:
                      s.copy(PT_sb[:, 512 * vpar:512 * (vpar + 1)], pT[tbank][:]),
                      [(s_tr, trv)], s_ptev.inc(1))
            ptevv = s_ptev.val

            if dbg_phase < 5:
                np_pair += 1
                continue
            vw = [(s_ptev, ptevv), (s_vld[slot], vldv)]
            if n >= 1:
                vw.append((s_accm, 2 * n))
            for h in range(2):
                for k4 in range(4):
                    pv = (pV0, pV1)[h]
                    prog.emit("tensor",
                              lambda tt, h=h, k4=k4, vpar=vpar, slot=slot, pv=pv:
                              tt.matmul(pv[:],
                                        PT_sb[:, 512 * vpar + 128 * k4:512 * vpar + 128 * (k4 + 1)],
                                        v_sb[:, 4096 * slot + 1024 * k4 + 512 * h:4096 * slot + 1024 * k4 + 512 * (h + 1)],
                                        start=(k4 == 0), stop=(k4 == 3)),
                              vw if (h == 0 and k4 == 0) else (),
                              s_pair.inc(1) if (h == 1 and k4 == 3) else None)
            pairv = s_pair.val

            for h in range(2):
                pv = (pV0, pV1)[h]
                prog.emit("vector",
                          lambda v, k=k, h=h, mold_c=mold_c, pv=pv:
                          v.scalar_tensor_tensor(acc[:, 1024 * k + 512 * h:1024 * k + 512 * (h + 1)],
                                                 acc[:, 1024 * k + 512 * h:1024 * k + 512 * (h + 1)],
                                                 msb[:, mold_c:mold_c + 1], pv[:],
                                                 ALU.mult, ALU.add),
                          [(s_pair, pairv)] if h == 0 else (), s_accm.inc(1))
            np_pair += 1
        kv_last_pair.append(np_pair)

    if dbg_phase < 6:
        sdone = Sem(nc, name + "_dbgdone")
        for k in range(16):
            prog.dma("sync", xr[:, 0:1024], x_res[128 * k:128 * (k + 1), :], s_xld[0],
                     [(s_out[0], 16 * k)] if k >= 1 else [])
            prog.dma("gpsimd", out_x1[128 * k:128 * (k + 1), :], xr[:, 0:1024], s_out[0],
                     [(s_xld[0], 16 * (k + 1))])
        if dbg_phase >= 1:
            prog.emit("gpsimd", lambda g: g.engine_nop(), [(s_pair, np_pair)] if dbg_phase >= 5 else [], None)
        prog.emit("gpsimd", lambda g: g.engine_nop(), [(s_out[0], 16 * 16)], None)
        return dict(allocs=allocs)

    fin_ev = []
    n_fin = 0
    for k in range(16):
        l_c = 4 * k + 3
        par = n_fin % 2
        row0 = 128 * k
        xw = []
        if n_fin >= 2:
            xw.append(fin_ev[n_fin - 2])
        prog.dma("sync", xr[:, 1024 * par:1024 * (par + 1)], x_res[row0:row0 + 128, :],
                 s_xld[par], xw)
        xldv = s_xld[par].val
        prog.emit("vector", lambda v, l_c=l_c: v.reciprocal(fs[:, 6:7], msb[:, l_c:l_c + 1]),
                  [(s_accm, 2 * total_pairs)], None)
        prog.emit("vector", lambda v, k=k: v.tensor_scalar(
                      acc[:, 1024 * k:1024 * (k + 1)], acc[:, 1024 * k:1024 * (k + 1)],
                      fs[:, 6:7], None, ALU.mult), (), None)
        prog.emit("vector", lambda v, k=k, par=par: v.tensor_tensor(
                      fin[:, 2048 * par:2048 * par + 1024], acc[:, 1024 * k:1024 * (k + 1)],
                      xr[:, 1024 * par:1024 * (par + 1)], ALU.add),
                  [(s_xld[par], xldv)], None)
        prog.emit("vector", lambda v, par=par: v.reduce_sum(
                      fs[:, 0:1], fin[:, 2048 * par:2048 * par + 1024], axis=AX.X), (), None)
        prog.emit("vector", lambda v: v.tensor_scalar(fs[:, 0:1], fs[:, 0:1], 1.0 / 1024, None, ALU.mult),
                  (), None)
        prog.emit("vector", lambda v, par=par: v.tensor_tensor(
                      fin[:, 2048 * par + 1024:2048 * par + 2048],
                      fin[:, 2048 * par:2048 * par + 1024],
                      fin[:, 2048 * par:2048 * par + 1024], ALU.mult), (), None)
        prog.emit("vector", lambda v, par=par: v.reduce_sum(
                      fs[:, 1:2], fin[:, 2048 * par + 1024:2048 * par + 2048], axis=AX.X), (), None)
        prog.emit("vector", lambda v: v.tensor_tensor(fs[:, 2:3], fs[:, 0:1], fs[:, 0:1], ALU.mult), (), None)
        prog.emit("vector", lambda v: v.scalar_tensor_tensor(fs[:, 1:2], fs[:, 1:2], 1.0 / 1024,
                                                             fs[:, 2:3], ALU.mult, ALU.subtract),
                  (), None)
        prog.emit("vector", lambda v: v.tensor_scalar(fs[:, 1:2], fs[:, 1:2], 1e-5, None, ALU.add),
                  (), s_stat.inc(1))
        vstat = s_stat.val
        prog.emit("scalar", lambda s: s.activation(fs[:, 3:4], fs[:, 1:2], AF.Ln),
                  [(s_stat, vstat)], None)
        prog.emit("scalar", lambda s: s.activation(fs[:, 4:5], fs[:, 3:4], AF.Exp, scale=-0.5),
                  (), s_exp.inc(1))
        rstdv = s_exp.val
        ow = [(s_exp, rstdv)]
        if n_fin >= 2:
            ow.append((s_out[par], 16 * (n_fin // 2)))
        prog.emit("vector", lambda v, par=par: v.tensor_scalar(
                      o1[:, 1024 * par:1024 * (par + 1)], fin[:, 2048 * par:2048 * par + 1024],
                      fs[:, 0:1], fs[:, 4:5], ALU.subtract, ALU.mult),
                  ow, s_accm.inc(1))
        fin_ev.append((s_accm, s_accm.val))
        prog.dma("gpsimd", out_x1[row0:row0 + 128, :], o1[:, 1024 * par:1024 * (par + 1)],
                 s_out[par], [fin_ev[-1]])
        n_fin += 1

    prog.emit("gpsimd", lambda g: g.engine_nop(), [(s_out[0], 16 * 8), (s_out[1], 16 * 8)], None)
    return dict(allocs=allocs)



def emit_l0_ffn(nc, prog, x1_ap, wg1, bg1, wg2, w_in, w_out, out_x2, out_x2fm, name="l0f"):
    st = nc.sbuf_tensor
    pt = nc.psum_tensor
    allocs = [
        st(name + "_x1fm", [128, 16384], BF16),
        st(name + "_yacc", [128, 16 * 1024], F32),
        st(name + "_hh", [128, 8 * 2048], BF16),
        st(name + "_wi", [128, 8 * 2048], BF16),
        st(name + "_wo", [128, 16 * 1024], BF16),
        st(name + "_stg", [128, 3072], F32),
        st(name + "_gate", [128, 16 * 8], F32),
        st(name + "_ones", [1, 512], BF16),
        st(name + "_idb", [128, 128], BF16),
        st(name + "_idf", [128, 128], F32),
        st(name + "_iota", [128, 256], F32),
        st(name + "_fs", [128, 16], F32),
        pt(name + "_pA", [128, 512 * 8], F32),
    ]
    hs = [a.__enter__() for a in allocs]
    (x1fm, yacc, hh, wi, wo, stg, gate, ones,
     idb, idf, iota, fs, pA) = hs
    ptrA = pA[:, 512 * 4:512 * 4 + 256].bitcast(BF16)
    ptrB = pA[:, 512 * 5:512 * 5 + 256].bitcast(BF16)
    smf = stg
    gfm = wo[:, 0:4096]
    wg1s = wo[:, 4096:6144]
    wg2s = wo[:, 6144:6160]
    bg = wo[0:1, 6160:6416]
    gsc = yacc[:, 0:512]
    ptr = (ptrA, ptrB)

    s_ld = [Sem(nc, name + f"_ld{i}") for i in range(2)]
    s_cast = Sem(nc, name + "_cast")
    s_gcast = Sem(nc, name + "_gcast")
    s_tr = Sem(nc, name + "_tr")
    s_trev = Sem(nc, name + "_trev")
    s_mm = Sem(nc, name + "_mm")
    s_hev = Sem(nc, name + "_hev")
    s_mrg = Sem(nc, name + "_mrg")
    s_gst = Sem(nc, name + "_gst")
    s_sexp = Sem(nc, name + "_sexp")
    s_bld = Sem(nc, name + "_bld")
    s_xld = [Sem(nc, name + f"_xld{i}") for i in range(2)]
    s_out = [Sem(nc, name + f"_out{i}") for i in range(2)]
    s_outf = [Sem(nc, name + f"_outf{i}") for i in range(2)]
    s_gmisc = Sem(nc, name + "_gmisc")
    s_vmisc = Sem(nc, name + "_vmisc")

    prog.emit("gpsimd", lambda g: g.iota(iota[:, 0:128], pattern=[[1, 128]], base=0,
                                         channel_multiplier=0, allow_small_or_imprecise_dtypes=True),
              (), s_gmisc.inc(1))
    prog.emit("gpsimd", lambda g: g.iota(iota[:, 128:256], pattern=[[0, 128]], base=0,
                                         channel_multiplier=1, allow_small_or_imprecise_dtypes=True),
              (), s_gmisc.inc(1))
    prog.emit("vector", lambda v: v.tensor_tensor(idf[:], iota[:, 0:128], iota[:, 128:256], ALU.is_equal),
              [(s_gmisc, 2)], s_vmisc.inc(1))
    prog.emit("vector", lambda v: v.tensor_copy(idb[:], idf[:]), (), s_vmisc.inc(1))
    id_ready = s_vmisc.val
    prog.emit("vector", lambda v: v.memset(ones[:], 1.0), (), s_vmisc.inc(1))
    ones_ready = s_vmisc.val

    for j in range(8):
        prog.dma("sync", smf[:, 256 * j:256 * (j + 1)], wg1[128 * j:128 * (j + 1), :], s_bld)
    for j2 in range(2):
        prog.dma("sync", smf[:, 2048 + 8 * j2:2048 + 8 * (j2 + 1)],
                 wg2[128 * j2:128 * (j2 + 1), :], s_bld)
    prog.dma("sync", smf[0:1, 2064:2064 + 256], bg1.rearrange("(o k) -> o k", o=1), s_bld)
    bldv = s_bld.val
    prog.emit("scalar", lambda s: s.copy(wg1s[:], smf[:, 0:2048]), [(s_bld, bldv)], s_cast.inc(1))
    prog.emit("scalar", lambda s: s.copy(wg2s[:], smf[:, 2048:2064]), (), s_cast.inc(1))
    prog.emit("scalar", lambda s: s.copy(bg[:], smf[0:1, 2064:2064 + 256]), (), s_cast.inc(1))
    gatew_ready = s_cast.val

    tr_groups = 0
    x1fm3 = x1fm[:].rearrange("p (j t) -> p j t", j=8)
    for ti in range(16):
        par = ti % 2
        lw = [(s_gcast, ti - 1)] if ti >= 2 else [(s_cast, gatew_ready)]
        prog.dma("sync", stg[:, 1024 * par:1024 * (par + 1)], x1_ap[128 * ti:128 * (ti + 1), :],
                 s_ld[par], lw)
        ldv = s_ld[par].val
        cw = [(s_ld[par], ldv)]
        if ti >= 2:
            cw.append((s_tr, (ti - 1) * 2))
        prog.emit("gpsimd", lambda g, par=par: g.tensor_copy(hh[:, 1024 * par:1024 * (par + 1)],
                                                             stg[:, 1024 * par:1024 * (par + 1)]),
                  cw, s_gcast.inc(1))
        castv = s_gcast.val
        for grp in range(2):
            bank = tr_groups % 2
            waits = [(s_gcast, castv), (s_vmisc, id_ready)]
            if tr_groups >= 2:
                waits.append((s_trev, tr_groups - 1))
            for jj in range(4):
                j = grp * 4 + jj
                prog.emit("tensor",
                          lambda t, par=par, j=j, bank=bank, jj=jj:
                          t.transpose(ptr[bank][:, 128 * jj:128 * (jj + 1)],
                                      hh[:, 1024 * par + 128 * j:1024 * par + 128 * (j + 1)],
                                      idb[:]),
                          waits if jj == 0 else (), s_tr.inc(1) if jj == 3 else None)
            prog.emit("vector",
                      lambda v, ti=ti, grp=grp, bank=bank:
                      v.tensor_copy(x1fm3[:, grp * 4:(grp + 1) * 4, 128 * ti:128 * (ti + 1)],
                                    ptr[bank][:].rearrange("p (j t) -> p j t", j=4)),
                      [(s_tr, s_tr.val)], s_trev.inc(1))
            tr_groups += 1
    x1fm_ready = tr_groups

    nmm = 0
    wg1s3 = wg1s[:].rearrange("p (j c) -> p j c", j=8)
    for i2 in range(2):
        for t4 in range(4):
            bank = nmm % 2
            w0 = [(s_trev, x1fm_ready), (s_cast, gatew_ready), (s_vmisc, ones_ready)]
            if nmm >= 2:
                w0.append((s_hev, nmm - 1))
            prog.emit("tensor", lambda t, i2=i2, bank=bank:
                      t.matmul(pA[:, 512 * bank:512 * (bank + 1)],
                               bg[0:1, 128 * i2:128 * (i2 + 1)], ones[0:1, :],
                               start=True, stop=False), w0, None)
            for j in range(8):
                prog.emit("tensor", lambda t, i2=i2, j=j, t4=t4, bank=bank:
                          t.matmul(pA[:, 512 * bank:512 * (bank + 1)],
                                   wg1s3[:, j, 128 * i2:128 * (i2 + 1)],
                                   x1fm3[:, j, 512 * t4:512 * (t4 + 1)],
                                   start=False, stop=(j == 7)),
                          (), s_mm.inc(1) if j == 7 else None)
            prog.emit("scalar", lambda s, i2=i2, t4=t4, bank=bank:
                      s.activation(gfm[:, 2048 * i2 + 512 * t4:2048 * i2 + 512 * (t4 + 1)],
                                   pA[:, 512 * bank:512 * (bank + 1)], AF.Gelu),
                      [(s_mm, s_mm.val)], s_hev.inc(1))
            nmm += 1
    gfm_ready = s_hev.val

    nlg = 0
    for sub in range(16):
        bank = 2 + (nlg % 2)
        G = gsc[:, 32 * sub:32 * (sub + 1)]
        w0 = [(s_hev, gfm_ready)]
        if nlg >= 2:
            w0.append((s_sexp, nlg - 1))
        for j2 in range(2):
            prog.emit("tensor", lambda t, sub=sub, j2=j2, bank=bank:
                      t.matmul(pA[:, 512 * bank:512 * bank + 8],
                               gfm[:, 2048 * j2 + 128 * sub:2048 * j2 + 128 * (sub + 1)],
                               wg2s[:, 8 * j2:8 * (j2 + 1)],
                               start=(j2 == 0), stop=(j2 == 1)),
                      w0 if j2 == 0 else (), s_mm.inc(1) if j2 == 1 else None)
        mmv = s_mm.val
        pslice = pA[:, 512 * bank:512 * bank + 8]
        prog.emit("vector", lambda v, pslice=pslice, G=G: v.reduce_max(G[:, 0:1], pslice, axis=AX.X),
                  [(s_mm, mmv)], None)
        prog.emit("vector", lambda v, G=G: v.tensor_scalar(G[:, 1:2], G[:, 0:1], -1.0, None, ALU.mult),
                  (), s_gst.inc(1))
        prog.emit("scalar", lambda s, pslice=pslice, G=G:
                  s.activation(G[:, 6:14], pslice, AF.Exp, bias=G[:, 1:2], accum_out=G[:, 2:3]),
                  [(s_gst, s_gst.val)], s_sexp.inc(1))
        prog.emit("vector", lambda v, G=G: v.reciprocal(G[:, 3:4], G[:, 2:3]),
                  [(s_sexp, s_sexp.val)], None)
        prog.emit("vector", lambda v, G=G: v.tensor_scalar(G[:, 6:14], G[:, 6:14], G[:, 3:4], None, ALU.mult),
                  (), None)
        prog.emit("vector", lambda v, G=G: v.reduce_max(G[:, 4:5], G[:, 6:14], axis=AX.X), (), None)
        prog.emit("vector", lambda v, G=G: v.tensor_scalar(G[:, 14:22], G[:, 6:14], G[:, 4:5], None, ALU.is_ge),
                  (), None)
        prog.emit("vector", lambda v, G=G: v.scalar_tensor_tensor(G[:, 22:30], G[:, 14:22], -1e9,
                                                                  G[:, 6:14], ALU.mult, ALU.add), (), None)
        prog.emit("vector", lambda v, G=G: v.reduce_max(G[:, 5:6], G[:, 22:30], axis=AX.X), (), None)
        prog.emit("vector", lambda v, G=G: v.tensor_scalar(G[:, 22:30], G[:, 22:30], G[:, 5:6], None, ALU.is_ge),
                  (), None)
        prog.emit("vector", lambda v, G=G: v.tensor_add(G[:, 14:22], G[:, 14:22], G[:, 22:30]), (), None)
        prog.emit("vector", lambda v, G=G, sub=sub: v.tensor_mul(gate[:, 8 * sub:8 * (sub + 1)],
                                                                 G[:, 6:14], G[:, 14:22]),
                  (), s_gst.inc(1))
        nlg += 1
    gates_ready = s_gst.val

    s_wild = [Sem(nc, name + f"_wild{i}") for i in range(2)]
    s_wold = [Sem(nc, name + f"_wold{i}") for i in range(2)]
    s_wic = Sem(nc, name + "_wic")
    s_woc = Sem(nc, name + "_woc")
    n_h = 0
    n_o = 0
    wi_casts = 0
    wo_casts = 0
    hh3 = hh[:].rearrange("p (io t) -> p io t", io=8)

    def load_wi(e):
        nonlocal wi_casts
        for jh in range(16):
            j, hhf = jh // 2, jh % 2
            par = jh % 2
            lw = []
            if e == 0 and jh < 2:
                lw.append((s_gcast, 16))
            if e * 16 + jh >= 2:
                lw.append((s_wic, wi_casts - 1))
            if e > 0:
                lw.append((s_mm, hmm_done[e - 1]))
            prog.dma("sync", stg[:, 1024 * par:1024 * (par + 1)],
                     w_in[e].rearrange("(c p) h -> c p h", p=128)[j][:, 1024 * hhf:1024 * (hhf + 1)],
                     s_wild[par], lw)
            prog.emit("scalar", lambda s, par=par, j=j, hhf=hhf:
                      s.copy(wi[:, 2048 * j + 1024 * hhf:2048 * j + 1024 * (hhf + 1)],
                             stg[:, 1024 * par:1024 * (par + 1)]),
                      [(s_wild[par], 16 * ((e * 16 + jh) // 2 + 1))], s_wic.inc(1))
            wi_casts += 1

    def load_wo(e, logits_done):
        nonlocal wo_casts
        for i in range(16):
            lw = []
            if e * 16 + i >= 1:
                lw.append((s_woc, wo_casts))
            if e > 0:
                lw.append((s_mm, omm_done[e - 1]))
            prog.dma("gpsimd", stg[:, 2048:3072],
                     w_out[e].rearrange("(c p) h -> c p h", p=128)[i], s_wold[0], lw)
            cw = [(s_wold[0], 16 * (e * 16 + i + 1))]
            if e == 0 and i <= 6:
                cw.append((s_mm, logits_done))
            prog.emit("gpsimd", lambda g, i=i:
                      g.tensor_copy(wo[:, 1024 * i:1024 * (i + 1)], stg[:, 2048:3072]),
                      cw, s_woc.inc(1))
            wo_casts += 1

    hmm_done = {}
    omm_done = {}
    hev_count = s_hev.val
    mrg_count = s_mrg.val

    logits_done = s_mm.val
    for e in range(8):
        load_wi(e)
        load_wo(e, logits_done)
        wic_ready = wi_casts
        woc_ready = wo_casts
        for hf in range(2):
            for io in range(8):
                i_abs = 8 * hf + io
                for t4 in range(4):
                    bank = 6 + (n_h % 2)
                    w0 = [(s_wic, wic_ready), (s_trev, x1fm_ready)]
                    if n_h >= 2:
                        w0.append((s_hev, hev_count + n_h - 1))
                    if n_o >= 1:
                        w0.append((s_mrg, mrg_count + 8 * n_o))
                    for j in range(8):
                        prog.emit("tensor",
                                  lambda t, j=j, i_abs=i_abs, t4=t4, bank=bank:
                                  t.matmul(pA[:, 512 * bank:512 * (bank + 1)],
                                           wi[:, 2048 * j + 128 * i_abs:2048 * j + 128 * (i_abs + 1)],
                                           x1fm3[:, j, 512 * t4:512 * (t4 + 1)],
                                           start=(j == 0), stop=(j == 7)),
                                  w0 if j == 0 else (), s_mm.inc(1) if j == 7 else None)
                    prog.emit("scalar", lambda s, io=io, t4=t4, bank=bank:
                              s.activation(hh3[:, io, 512 * t4:512 * (t4 + 1)],
                                           pA[:, 512 * bank:512 * (bank + 1)], AF.Gelu),
                              [(s_mm, s_mm.val)], s_hev.inc(1))
                    n_h += 1
            hphase_done = s_hev.val
            for t4 in range(4):
                grp_banks = []
                w0 = [(s_hev, hphase_done), (s_woc, woc_ready), (s_gst, gates_ready)]
                if n_o >= 1:
                    w0.append((s_mrg, mrg_count + 8 * n_o))
                first = True
                for io in range(8):
                    i_abs = 8 * hf + io
                    for sub in range(4):
                        for h2 in range(2):
                            bank = sub * 2 + h2
                            prog.emit("tensor",
                                      lambda t, io=io, i_abs=i_abs, t4=t4, sub=sub, h2=h2, bank=bank:
                                      t.matmul(pA[:, 512 * bank:512 * (bank + 1)],
                                               hh3[:, io, 512 * t4 + 128 * sub:512 * t4 + 128 * (sub + 1)],
                                               wo[:, 1024 * i_abs + 512 * h2:1024 * i_abs + 512 * (h2 + 1)],
                                               start=(io == 0), stop=(io == 7)),
                                      w0 if first else (),
                                      s_mm.inc(1) if (io == 7 and sub == 3 and h2 == 1) else None)
                            first = False
                mmv = s_mm.val
                for sub in range(4):
                    for h2 in range(2):
                        bank = sub * 2 + h2
                        s16 = 4 * t4 + sub
                        dst = yacc[:, 1024 * s16 + 512 * h2:1024 * s16 + 512 * (h2 + 1)]
                        gap = gate[:, 8 * s16 + e:8 * s16 + e + 1]
                        if e == 0 and hf == 0:
                            prog.emit("vector", lambda v, dst=dst, gap=gap, bank=bank:
                                      v.tensor_scalar(dst, pA[:, 512 * bank:512 * (bank + 1)],
                                                      gap, None, ALU.mult),
                                      [(s_mm, mmv)] if (sub == 0 and h2 == 0) else (), s_mrg.inc(1))
                        else:
                            prog.emit("vector", lambda v, dst=dst, gap=gap, bank=bank:
                                      v.scalar_tensor_tensor(dst, pA[:, 512 * bank:512 * (bank + 1)],
                                                             gap, dst, ALU.mult, ALU.add),
                                      [(s_mm, mmv)] if (sub == 0 and h2 == 0) else (), s_mrg.inc(1))
                n_o += 1
        hmm_done[e] = s_mm.val
        omm_done[e] = s_mm.val
    experts_done_mrg = s_mrg.val

    s_fin = Sem(nc, name + "_fin")
    fin_ev = []
    n_fin = 0
    ntr = tr_groups
    wic_all = wi_casts
    woc_all = wo_casts
    for s16 in range(16):
        par = n_fin % 2
        row0 = 128 * s16
        xw = [(s_wic, wic_all)]
        if n_fin >= 2:
            xw.append((s_fin, n_fin - 1))
        prog.dma("sync", stg[:, 1024 * par:1024 * (par + 1)], x1_ap[row0:row0 + 128, :],
                 s_xld[par], xw)
        xldv = s_xld[par].val
        ys = yacc[:, 1024 * s16:1024 * (s16 + 1)]
        prog.emit("vector", lambda v, ys=ys, par=par: v.tensor_tensor(
                      ys, ys, stg[:, 1024 * par:1024 * (par + 1)], ALU.add),
                  [(s_xld[par], xldv), (s_mrg, experts_done_mrg)], s_fin.inc(1))
        prog.emit("vector", lambda v, ys=ys: v.reduce_sum(fs[:, 0:1], ys, axis=AX.X), (), None)
        prog.emit("vector", lambda v: v.tensor_scalar(fs[:, 0:1], fs[:, 0:1], 1.0 / 1024, None, ALU.mult),
                  (), s_gst.inc(1))
        mean_v = s_gst.val
        prog.emit("scalar", lambda s, ys=ys: s.activation(stg[:, 2048:3072], ys, AF.Square,
                                                          accum_out=fs[:, 1:2]),
                  [(s_gst, mean_v), (s_woc, woc_all)], s_sexp.inc(1))
        sqv = s_sexp.val
        prog.emit("vector", lambda v: v.tensor_tensor(fs[:, 2:3], fs[:, 0:1], fs[:, 0:1], ALU.mult),
                  [(s_sexp, sqv)], None)
        prog.emit("vector", lambda v: v.scalar_tensor_tensor(fs[:, 1:2], fs[:, 1:2], 1.0 / 1024,
                                                             fs[:, 2:3], ALU.mult, ALU.subtract), (), None)
        prog.emit("vector", lambda v: v.tensor_scalar(fs[:, 1:2], fs[:, 1:2], 1e-5, None, ALU.add),
                  (), s_gst.inc(1))
        vstat = s_gst.val
        prog.emit("scalar", lambda s: s.activation(fs[:, 3:4], fs[:, 1:2], AF.Ln),
                  [(s_gst, vstat)], None)
        prog.emit("scalar", lambda s: s.activation(fs[:, 4:5], fs[:, 3:4], AF.Exp, scale=-0.5),
                  (), s_sexp.inc(1))
        rstdv = s_sexp.val
        ow = [(s_sexp, rstdv)]
        if n_fin >= 2:
            ow.append((s_out[par], 16 * (n_fin // 2)))
        prog.emit("vector", lambda v, ys=ys: v.tensor_scalar(
                      ys, ys, fs[:, 0:1], fs[:, 4:5], ALU.subtract, ALU.mult),
                  ow, s_mrg.inc(1))
        fin_ev.append((s_mrg, s_mrg.val))
        prog.dma("gpsimd", out_x2[row0:row0 + 128, :], ys, s_out[par], [fin_ev[-1]])
        cw = [fin_ev[-1], (s_mm, omm_done[7])]
        if n_fin >= 2:
            cw.append((s_tr, ntr))
        prog.emit("vector", lambda v, ys=ys, par=par: v.tensor_copy(
                      hh[:, 1024 * par:1024 * (par + 1)], ys), cw, s_mrg.inc(1))
        castev = (s_mrg, s_mrg.val)
        for grp in range(2):
            bank = ntr % 2
            tw2 = [castev] if grp == 0 else []
            if ntr >= 2:
                tw2.append((s_trev, ntr - 1))
            for jj in range(4):
                j = grp * 4 + jj
                prog.emit("tensor",
                          lambda t, par=par, j=j, bank=bank, jj=jj:
                          t.transpose(ptr[bank][:, 128 * jj:128 * (jj + 1)],
                                      hh[:, 1024 * par + 128 * j:1024 * par + 128 * (j + 1)],
                                      idb[:]),
                          tw2 if jj == 0 else (), s_tr.inc(1) if jj == 3 else None)
            ew = [(s_tr, s_tr.val)]
            if n_fin >= 2:
                ew.append((s_outf[par], 16 * (n_fin // 2)))
            prog.emit("vector",
                      lambda v, par=par, grp=grp, bank=bank:
                      v.tensor_copy(hh[:, 2048 + 1024 * par + 512 * grp:2048 + 1024 * par + 512 * (grp + 1)],
                                    ptr[bank][:].rearrange("p (j t) -> p j t", j=4)),
                      ew, s_trev.inc(1))
            ntr += 1
        prog.dma("gpsimd",
                 out_x2fm[:, 128 * s16:128 * (s16 + 1)].rearrange("(j p) c -> p j c", p=128),
                 hh[:, 2048 + 1024 * par:2048 + 1024 * (par + 1)].rearrange("p (j c) -> p j c", j=8),
                 s_outf[par], [(s_trev, ntr)])
        n_fin += 1

    prog.emit("gpsimd", lambda g: g.engine_nop(),
              [(s_out[0], 16 * 8), (s_out[1], 16 * 8), (s_outf[0], 16 * 8), (s_outf[1], 16 * 8)], None)
    return dict(allocs=allocs)



def l1_const_block():
    pos = np.arange(128) % 64
    inv_freq = 1.0 / (10000.0 ** (np.arange(0, 64, 2, dtype=np.float64) / 64))
    emb = pos[:, None] * inv_freq[None, :]
    cos = np.cos(np.concatenate([emb, emb], 1))
    sin = np.sin(np.concatenate([emb, emb], 1))
    cos2 = np.tile(cos, (1, 4)).astype(np.float32)
    sin_half = np.concatenate([-sin[:, 32:64], sin[:, 0:32]], 1)
    sinS = np.tile(np.concatenate([-sin[:, 0:32], sin[:, 32:64]], 1), (1, 4)).astype(np.float32)
    maskT = (np.arange(128)[:, None] % 64 <= np.arange(64)[None, :]).astype(np.float32)
    return cos2, sinS, maskT


def l1_make_inputs(core, Wq, Wk, Wv, lr_w, lr_b, g4, b4, W1, b1, W2, b2):
    h0, h1 = 2 * core, 2 * core + 1
    projT = np.concatenate([Wq[128 * core:128 * (core + 1)].T,
                            Wk[128 * core:128 * (core + 1)].T,
                            Wv[128 * core:128 * (core + 1)].T], axis=1)
    lr_exp = np.zeros((1024, 128), np.float32)
    lr_exp[:, 0] = lr_w[h0]
    lr_exp[:, 64] = lr_w[h1]
    bias_row = np.zeros((1, 128), np.float32)
    bias_row[0, 0] = lr_b[h0]
    bias_row[0, 64] = lr_b[h1]
    cos2, sinS, maskT = l1_const_block()
    g4t = np.concatenate([np.tile(g4[h0], (128, 1)), np.tile(g4[h1], (128, 1))], 1)
    b4t = np.concatenate([np.tile(b4[h0], (128, 1)), np.tile(b4[h1], (128, 1))], 1)
    g4s = np.concatenate([np.tile(g4[h0], (64, 1)), np.tile(g4[h1], (64, 1))], 0)
    b4s = np.concatenate([np.tile(b4[h0], (64, 1)), np.tile(b4[h1], (64, 1))], 0)
    l1c = np.concatenate([cos2, sinS, g4t, b4t, g4s, b4s, maskT], axis=1).astype(np.float32)
    W1_0 = np.concatenate([W1[h0], W1[h1]], 0).astype(np.float32)
    b1_0 = np.concatenate([b1[h0, 0], b1[h1, 0]])[None, :].astype(np.float32)
    W2_0 = np.zeros((128, 256), np.float32)
    W2_0[:, 0:64] = W2[h0][0:128]
    W2_0[:, 64:128] = W2[h0][128:256]
    W2_0[:, 128:192] = W2[h1][0:128]
    W2_0[:, 192:256] = W2[h1][128:256]
    W2T_0 = np.concatenate([W2[h0].T, W2[h1].T], 1).astype(np.float32)
    b2_0 = np.concatenate([b2[h0, 0], b2[h1, 0]])[None, :].astype(np.float32)
    return dict(projT=np.ascontiguousarray(projT), lr_exp=lr_exp, bias_row=bias_row,
                l1c=l1c, W1_0=W1_0, b1_0=b1_0, W2_0=np.ascontiguousarray(W2_0),
                W2T_0=np.ascontiguousarray(W2T_0), b2_0=b2_0)


def emit_l1_attn(nc, prog, x2fm_g, projT_d, lr_exp_d, bias_row_d, l1c_d,
                 W1_d, b1_d, W2_d, W2T_d, b2_d, out_yfm, out_stats, name="l1a",
                 n_tiles=128, n_chunks=256, dbg="full", dbg_cut=9999):
    GC = math.sqrt(2.0 / math.pi)
    A_ = 0.044715
    st_ = nc.sbuf_tensor
    pt = nc.psum_tensor
    allocs = [
        st_(name + "_xqf", [128, 16384], BF16),
        st_(name + "_xkf", [128, 16384], BF16),
        st_(name + "_tgt", [128, 16384], BF16),
        st_(name + "_a1m", [128, 16384], BF16),
        st_(name + "_yfm", [128, 16384], BF16),
        st_(name + "_eta", [128, 256], F32),
        st_(name + "_sts", [128, 512], F32),
        st_(name + "_l1c", [128, 960], BF16),
        st_(name + "_pjT", [128, 8 * 384], BF16),
        st_(name + "_lre", [128, 8 * 128], BF16),
        st_(name + "_brw", [1, 128], BF16),
        st_(name + "_x2t", [128, 2 * 1024], BF16),
        st_(name + "_stg", [128, 960], F32),
        st_(name + "_prj", [128, 2 * 384], BF16),
        st_(name + "_qkn", [128, 256], F32),
        st_(name + "_qkr", [128, 256], F32),
        st_(name + "_qkb", [128, 256], BF16),
        st_(name + "_tgtb", [128, 128], BF16),
        st_(name + "_sc", [128, 1024], F32),
        st_(name + "_sc2", [128, 384], F32),
        st_(name + "_lns", [128, 32], F32),
        st_(name + "_etf", [128, 256], BF16),
        st_(name + "_idb", [128, 128], BF16),
        st_(name + "_id2", [128, 64], BF16),
        st_(name + "_W1f", [128, 256], F32),
        st_(name + "_W1zb", [128, 512], BF16),
        st_(name + "_W2b", [128, 256], BF16),
        st_(name + "_WTf", [64, 512], F32),
        st_(name + "_WTb", [64, 512], BF16),
        st_(name + "_b1f", [1, 512], F32),
        st_(name + "_b2f", [1, 128], F32),
        st_(name + "_b1b2", [1, 512], BF16),
        st_(name + "_b2b2", [1, 128], BF16),
        st_(name + "_gz1z", [128, 512], BF16),
        st_(name + "_gz2z", [128, 128], BF16),
        st_(name + "_o1z", [128, 2], BF16),
        st_(name + "_xqfz", [128, 128], BF16),
        st_(name + "_X2t", [128, 256], BF16),
        st_(name + "_X2f", [128, 256], BF16),
        st_(name + "_X2bt", [128, 256], BF16),
        st_(name + "_X2bf", [128, 256], BF16),
        st_(name + "_gb", [128, 256], F32),
        st_(name + "_gz2", [128, 64], BF16),
        st_(name + "_gz2f", [64, 128], BF16),
        st_(name + "_gz1", [128, 256], BF16),
        st_(name + "_a2m", [128, 64], BF16),
        st_(name + "_xqt", [128, 64], BF16),
        st_(name + "_xkt", [128, 64], BF16),
        st_(name + "_yc", [128, 64], F32),
        st_(name + "_ons", [128, 128], BF16),
        st_(name + "_o11", [128, 1], BF16),
        pt(name + "_p0", [128, 512], F32),
        pt(name + "_p1", [128, 512], F32),
        pt(name + "_p2", [128, 512], F32),
        pt(name + "_p3", [128, 512], F32),
        pt(name + "_p4", [128, 512], F32),
        pt(name + "_p5", [128, 512], F32),
        pt(name + "_p6", [128, 512], F32),
        pt(name + "_p7", [128, 512], F32),
    ]
    hs = [a.__enter__() for a in allocs]
    (xqf, xkf, tgt, a1m, yfm, eta, sts, l1c, pjT, lre, brw, x2t, stg, prj, qkn, qkr, qkb, tgtb,
     sc, sc2, lns, etf, idb, id2, W1f, W1zb, W2b, WTf, WTb, b1f, b2f, b1b2, b2b2, gz1z, gz2z, o1z, xqfz,
     X2t, X2f, X2bt, X2bf, gb, gz2, gz2f, gz1, a2m, xqt, xkt, yc, ons, o11,
     p0, p1, p2, p3, p4, p5, p6, p7) = hs
    pT0b = p6[:, 0:256].bitcast(BF16)
    pT1b = p7[:, 0:256].bitcast(BF16)
    pTb = (pT0b, pT1b)

    cos2 = l1c[:, 0:256]
    sinS = l1c[:, 256:512]
    g4t = l1c[:, 512:640]
    b4t = l1c[:, 640:768]
    g4s = l1c[:, 768:832]
    b4s = l1c[:, 832:896]
    mskT = l1c[:, 896:960]

    s_ld = [Sem(nc, name + f"_ld{i}") for i in range(2)]
    s_wld = Sem(nc, name + "_wld")
    s_cast = Sem(nc, name + "_cast")
    s_mm = Sem(nc, name + "_mm")
    s_vp = Sem(nc, name + "_vp")
    s_sp = Sem(nc, name + "_sp")
    s_gp = Sem(nc, name + "_gp")
    s_tr = Sem(nc, name + "_tr")
    s_out = Sem(nc, name + "_out")

    prog.emit("gpsimd", lambda g: g.iota(sc[:, 0:128], pattern=[[1, 128]], base=0,
                                         channel_multiplier=0, allow_small_or_imprecise_dtypes=True),
              (), s_gp.inc(1))
    prog.emit("gpsimd", lambda g: g.iota(sc[:, 128:256], pattern=[[0, 128]], base=0,
                                         channel_multiplier=1, allow_small_or_imprecise_dtypes=True),
              (), s_gp.inc(1))
    prog.emit("vector", lambda v: v.tensor_tensor(sc[:, 256:384], sc[:, 0:128], sc[:, 128:256], ALU.is_equal),
              [(s_gp, 2)], s_vp.inc(1))
    prog.emit("vector", lambda v: v.tensor_copy(idb[:], sc[:, 256:384]), (), s_vp.inc(1))
    prog.emit("vector", lambda v: v.tensor_copy(id2[0:64, :], idb[0:64, 0:64]), (), s_vp.inc(1))
    prog.emit("vector", lambda v: v.tensor_copy(id2[64:128, :], idb[64:128, 64:128]), (), s_vp.inc(1))

    init_loads = []
    init_loads.append((l1c_d[:], 960, "vector", lambda: l1c[:]))
    for j in range(8):
        init_loads.append((projT_d[128 * j:128 * (j + 1), :], 384, "scalar",
                           lambda j=j: pjT[:, 384 * j:384 * (j + 1)]))
        init_loads.append((lr_exp_d[128 * j:128 * (j + 1), :], 128, "scalar",
                           lambda j=j: lre[:, 128 * j:128 * (j + 1)]))
    init_loads.append((W1_d[:], 256, "vector", lambda: W1f[:]))
    init_loads.append((b1_d[:], 512, "vector", lambda: b1f[:]))
    init_loads.append((W2T_d[:], 512, "vector", lambda: WTf[:]))
    init_loads.append((b2_d[:], 128, "vector", lambda: b2f[:]))
    prev_ev = None
    for (src_ap, wid, eng, dstfn) in init_loads:
        psz = src_ap.shape[0]
        prog.dma("sync", stg[0:psz, 0:wid], src_ap, s_wld, [prev_ev] if prev_ev else [])
        wv = s_wld.val
        if eng == "vector":
            prog.emit("vector", lambda v, dstfn=dstfn, wid=wid, psz=psz:
                      v.tensor_copy(dstfn(), stg[0:psz, 0:wid]), [(s_wld, wv)], s_vp.inc(1))
            prev_ev = (s_vp, s_vp.val)
        else:
            prog.emit("scalar", lambda s, dstfn=dstfn, wid=wid, psz=psz:
                      s.copy(dstfn(), stg[0:psz, 0:wid]), [(s_wld, wv)], s_cast.inc(1))
            prev_ev = (s_cast, s_cast.val)
    prog.dma("sync", stg[0:1, 0:128], bias_row_d[:], s_wld, [prev_ev])
    prog.emit("scalar", lambda s: s.copy(brw[:], stg[0:1, 0:128]),
              [(s_wld, s_wld.val)], s_cast.inc(1))
    prog.emit("vector", lambda v: v.memset(W1zb[:], 0.0), (), s_vp.inc(1))
    prog.emit("vector", lambda v: v.tensor_copy(W1zb[0:64, 0:256], W1f[0:64, :]), (), s_vp.inc(1))
    prog.emit("vector", lambda v: v.tensor_copy(W1zb[64:128, 256:512], W1f[64:128, :]), (), s_vp.inc(1))
    prog.emit("vector", lambda v: v.tensor_copy(WTb[:], WTf[:]), (), s_vp.inc(1))
    prog.emit("vector", lambda v: v.tensor_copy(b1b2[:], b1f[:]), (), s_vp.inc(1))
    prog.emit("vector", lambda v: v.tensor_copy(b2b2[:], b2f[:]), (), s_vp.inc(1))
    prog.emit("vector", lambda v: v.memset(gz1z[:], 0.0), (), s_vp.inc(1))
    prog.emit("vector", lambda v: v.memset(gz2z[:], 0.0), (), s_vp.inc(1))
    prog.emit("vector", lambda v: v.memset(xqfz[:], 0.0), (), s_vp.inc(1))
    prog.emit("vector", lambda v: v.memset(o1z[:], 0.0), (), s_vp.inc(1))
    prog.emit("vector", lambda v: v.tensor_scalar(o1z[0:64, 0:1], o1z[0:64, 0:1], 0.0, 1.0,
                                                  ALU.mult, ALU.add), (), s_vp.inc(1))
    prog.emit("vector", lambda v: v.tensor_scalar(o1z[64:128, 1:2], o1z[64:128, 1:2], 0.0, 1.0,
                                                  ALU.mult, ALU.add), (), s_vp.inc(1))
    prog.emit("vector", lambda v: v.memset(ons[:], 1.0), (), s_vp.inc(1))
    prog.emit("vector", lambda v: v.memset(o11[:], 1.0), (), s_vp.inc(1))
    prog.emit("vector", lambda v: v.memset(sts[:], 0.0), (), s_vp.inc(1))
    prog.emit("vector", lambda v: v.memset(yfm[:], 0.0), (), s_vp.inc(1))
    init_v = s_vp.val
    cast_ok = s_cast.val

    pjT3 = pjT[:].rearrange("p (j c) -> p j c", j=8)
    lre3 = lre[:].rearrange("p (j c) -> p j c", j=8)
    x2t3 = [x2t[:, 1024 * par:1024 * (par + 1)].rearrange("p (j c) -> p j c", j=8) for par in range(2)]
    n_tr = 0
    tile_mm_done = []
    tile_prj_free = []
    tile_psum_free = []
    tile_trev = []
    trev_vals = []
    eta_mm_evac = []

    for T in range(n_tiles):
        par = T % 2
        r, coff = T % 8, 128 * (T // 8)
        lw = [(s_cast, cast_ok)]
        if T >= 2:
            lw.append((s_mm, tile_mm_done[T - 2]))
        prog.dma("sync", x2t3[par],
                 x2fm_g[1024 * r:1024 * (r + 1), coff:coff + 128].rearrange("(j p) c -> p j c", p=128),
                 s_ld[par], lw)
        ldv = s_ld[par].val
        pP = (p0, p1)[par]
        pL = (p2, p3)[par]
        w0 = [(s_ld[par], ldv), (s_cast, cast_ok), (s_vp, init_v)]
        if T >= 2:
            w0.append((s_sp, tile_psum_free[T - 2]))
        for j in range(8):
            prog.emit("tensor", lambda t, j=j, par=par, pP=pP:
                      t.matmul(pP[:, 0:384], x2t3[par][:, j, :], pjT3[:, j, :],
                               start=(j == 0), stop=(j == 7)),
                      w0 if j == 0 else (), None)
        for j in range(8):
            prog.emit("tensor", lambda t, j=j, par=par, pL=pL:
                      t.matmul(pL[:, 0:128], lre3[:, j, :], x2t3[par][:, j, :],
                               start=(j == 0), stop=False), (), None)
        prog.emit("tensor", lambda t, pL=pL: t.matmul(pL[:, 0:128], brw[0:1, :], ons[0:1, 0:128],
                                                      start=False, stop=True), (), s_mm.inc(1))
        tile_mm_done.append(s_mm.val)
        ew = [(s_mm, s_mm.val)]
        if T >= 2:
            ew.append((s_vp, tile_prj_free[T - 2]))
        prog.emit("scalar", lambda s, par=par, pP=pP: s.copy(prj[:, 384 * par:384 * (par + 1)], pP[:, 0:384]),
                  ew, s_sp.inc(1))
        sgw = [eta_mm_evac[-1]] if eta_mm_evac else []
        prog.emit("scalar", lambda s, pL=pL:
                  s.activation(etf[:, 0:128], pL[:, 0:128], AF.Exp, scale=-1.0),
                  sgw, s_sp.inc(1))
        tile_psum_free.append(s_sp.val)
        spv = s_sp.val
        P = prj[:, 384 * par:384 * (par + 1)]
        EF = etf[:]
        QKB = qkb[:]
        vw = [(s_sp, spv)]
        prog.emit("vector", lambda v, P=P: v.tensor_tensor(sc[:, 0:256], P[:, 0:256], P[:, 0:256], ALU.mult),
                  vw, None)
        prog.emit("vector", lambda v: v.reduce_sum(lns[:, 0:4].rearrange("p (g o) -> p g o", o=1),
                                                   sc[:, 0:256].rearrange("p (g f) -> p g f", g=4),
                                                   axis=AX.X), (), s_vp.inc(1))
        ssv = s_vp.val
        prog.emit("scalar", lambda s: s.activation(lns[:, 4:8], lns[:, 0:4], AF.Ln),
                  [(s_vp, ssv)], None)
        prog.emit("scalar", lambda s: s.activation(lns[:, 8:12], lns[:, 4:8], AF.Exp, scale=-0.5),
                  (), s_sp.inc(1))
        rnv = s_sp.val
        prog.emit("vector", lambda v: v.tensor_scalar(lns[:, 8:12], lns[:, 8:12], 1e12, None, ALU.min),
                  [(s_sp, rnv)], None)
        for g_ in range(4):
            prog.emit("vector", lambda v, P=P, g_=g_:
                      v.tensor_scalar(qkn[:, 64 * g_:64 * (g_ + 1)], P[:, 64 * g_:64 * (g_ + 1)],
                                      lns[:, 8 + g_:9 + g_], None, ALU.mult), (), None)
        qkn4 = qkn[:].rearrange("p (g h f) -> p g h f", g=4, f=32)
        prog.emit("vector", lambda v: v.tensor_tensor(qkr[:], qkn[:], cos2, ALU.mult), (), None)
        prog.emit("vector", lambda v, qkn4=qkn4: v.tensor_tensor(
                      sc[:, 256:512].rearrange("p (g h f) -> p g h f", g=4, f=32),
                      qkn4[:, :, ::-1, :],
                      sinS.rearrange("p (g h f) -> p g h f", g=4, f=32), ALU.mult), (), None)
        prog.emit("vector", lambda v: v.tensor_tensor(qkr[:], qkr[:], sc[:, 256:512], ALU.add),
                  (), None)
        cw = []
        if T >= 1:
            cw.append((s_tr, 3 * T))
        prog.emit("vector", lambda v, QKB=QKB: v.tensor_copy(QKB, qkr[:]), cw, s_vp.inc(1))
        qkbv = s_vp.val
        prog.emit("vector", lambda v, P=P: v.tensor_tensor(sc2[:, 0:128], P[:, 256:384],
                                                           qkr[:, 128:256], ALU.subtract), (), None)
        prog.emit("vector", lambda v: v.reduce_sum(lns[:, 12:14].rearrange("p (g o) -> p g o", o=1),
                                                   sc2[:, 0:128].rearrange("p (g f) -> p g f", g=2),
                                                   axis=AX.X), (), None)
        prog.emit("vector", lambda v: v.tensor_scalar(lns[:, 12:14], lns[:, 12:14], 1.0 / 64, None, ALU.mult),
                  (), None)
        for h in range(2):
            prog.emit("vector", lambda v, h=h: v.tensor_scalar(
                          sc2[:, 64 * h:64 * (h + 1)], sc2[:, 64 * h:64 * (h + 1)],
                          lns[:, 12 + h:13 + h], None, ALU.subtract), (), None)
        prog.emit("vector", lambda v: v.tensor_tensor(sc2[:, 128:256], sc2[:, 0:128], sc2[:, 0:128], ALU.mult),
                  (), None)
        prog.emit("vector", lambda v: v.reduce_sum(lns[:, 14:16].rearrange("p (g o) -> p g o", o=1),
                                                   sc2[:, 128:256].rearrange("p (g f) -> p g f", g=2),
                                                   axis=AX.X), (), None)
        prog.emit("vector", lambda v: v.tensor_scalar(lns[:, 14:16], lns[:, 14:16], 1.0 / 63, None, ALU.mult),
                  (), s_vp.inc(1))
        varv = s_vp.val
        prog.emit("scalar", lambda s: s.activation(lns[:, 16:18], lns[:, 14:16], AF.Ln),
                  [(s_vp, varv)], None)
        prog.emit("scalar", lambda s: s.activation(lns[:, 18:20], lns[:, 16:18], AF.Exp, scale=0.5),
                  (), s_sp.inc(1))
        sdv = s_sp.val
        prog.emit("vector", lambda v: v.tensor_scalar(lns[:, 18:20], lns[:, 18:20], 1e-8, None, ALU.add),
                  [(s_sp, sdv)], None)
        prog.emit("vector", lambda v: v.reciprocal(lns[:, 20:22], lns[:, 18:20]), (), None)
        for h in range(2):
            prog.emit("vector", lambda v, h=h: v.tensor_scalar(
                          sc2[:, 64 * h:64 * (h + 1)], sc2[:, 64 * h:64 * (h + 1)],
                          lns[:, 20 + h:21 + h], None, ALU.mult), (), None)
        prog.emit("vector", lambda v: v.tensor_tensor(sc2[:, 0:128], sc2[:, 0:128], g4t, ALU.mult), (), None)
        prog.emit("vector", lambda v: v.tensor_tensor(sc2[:, 0:128], sc2[:, 0:128], b4t, ALU.add), (), None)
        tgw = []
        if T >= 1:
            tgw.append((s_tr, 3 * T))
        prog.emit("vector", lambda v: v.tensor_copy(tgtb[:], sc2[:, 0:128]), tgw, s_vp.inc(1))
        tile_prj_free.append(s_vp.val)
        prog.emit("vector", lambda v, EF=EF: v.tensor_scalar(sc2[:, 0:128], EF[:, 0:128], 1.0, None, ALU.add),
                  (), None)
        prog.emit("vector", lambda v: v.reciprocal(sc2[:, 128:256], sc2[:, 0:128]), (), None)
        prog.emit("vector", lambda v, EF=EF: v.tensor_scalar(EF[:, 128:256], sc2[:, 128:256], 1.0 / 64,
                                                             None, ALU.mult), (), s_vp.inc(1))
        etav = s_vp.val
        tile_tgtv = s_vp.val
        for dst, srcap, gate in ((xqf, QKB[:, 0:128], qkbv), (xkf, QKB[:, 128:256], qkbv),
                                 (tgt, tgtb[:], tile_tgtv)):
            bank = n_tr % 2
            tw = [(s_vp, gate)]
            if n_tr >= 2:
                tw.append((s_vp, trev_vals[n_tr - 2]))
            prog.emit("tensor", lambda t, srcap=srcap, bank=bank:
                      t.transpose(pTb[bank][:, 0:128], srcap, idb[:]),
                      tw, s_tr.inc(1))
            prog.emit("vector", lambda v, dst=dst, T=T, bank=bank:
                      v.tensor_copy(dst[:, 128 * T:128 * (T + 1)], pTb[bank][:, 0:128]),
                      [(s_tr, s_tr.val)], s_vp.inc(1))
            trev_vals.append(s_vp.val)
            n_tr += 1
        tile_trev.append(s_vp.val)
        for ci in range(2):
            i_ = 2 * T + ci
            emw = [(s_vp, etav)]
            if eta_mm_evac:
                emw.append(eta_mm_evac[-1])
            for h in range(2):
                prog.emit("tensor", lambda t, EF=EF, ci=ci, h=h:
                          t.matmul(p4[64 * h:64 * (h + 1), 0:1],
                                   EF[64 * h:64 * h + 1, 128 + 64 * ci:128 + 64 * (ci + 1)],
                                   o11[64 * h:64 * h + 1, 0:1],
                                   start=True, stop=True),
                          emw if h == 0 else (), s_mm.inc(1) if h == 1 else None)
            prog.emit("vector", lambda v, i_=i_: v.tensor_copy(eta[:, i_:i_ + 1], p4[:, 0:1]),
                      [(s_mm, s_mm.val)], s_vp.inc(1))
            eta_mm_evac.append((s_vp, s_vp.val))
    pre_v = s_vp.val

    if dbg == "pre":
        prog.dma("gpsimd", out_yfm[:], yfm[:], s_out, [(s_vp, s_vp.val)])
        prog.dma("gpsimd", out_stats[:], sts[:], s_out, [(s_vp, s_vp.val)])
        prog.emit("gpsimd", lambda g: g.engine_nop(), [(s_out, 32)], None)
        return dict(allocs=allocs)

    a1_evac = []
    for i in range(n_chunks):
        bank = (p4, p5)[i % 2]
        cpw = [(s_vp, pre_v)] if i == 0 else [(s_vp, a1_evac[i - 1])]
        prog.emit("vector", lambda v, i=i: v.tensor_copy(xqfz[0:64, 0:64],
                                                         xqf[0:64, 64 * i:64 * (i + 1)]), cpw, None)
        prog.emit("vector", lambda v, i=i: v.tensor_copy(xqfz[64:128, 64:128],
                                                         xqf[64:128, 64 * i:64 * (i + 1)]),
                  (), s_vp.inc(1))
        xqzv = s_vp.val
        w0 = [(s_vp, xqzv)]
        if i >= 2:
            w0.append((s_vp, a1_evac[i - 2]))
        for h in range(2):
            prog.emit("tensor", lambda t, i=i, h=h, bank=bank:
                      t.matmul(bank[64 * h:64 * (h + 1), 0:64],
                               xkf[:, 64 * i:64 * (i + 1)],
                               xqfz[:, 64 * h:64 * (h + 1)],
                               start=True, stop=True),
                      w0 if h == 0 else (), s_mm.inc(1) if h == 1 else None)
        prog.emit("vector", lambda v, i=i, bank=bank:
                  v.scalar_tensor_tensor(a1m[:, 64 * i:64 * (i + 1)], bank[:, 0:64], 1.0,
                                         mskT, ALU.add, ALU.mult),
                  [(s_mm, s_mm.val)], s_vp.inc(1))
        a1_evac.append(s_vp.val)
    a1_done = s_vp.val

    if dbg == "a1":
        prog.dma("gpsimd", out_yfm[:], a1m[:], s_out, [(s_vp, a1_done)])
        prog.dma("gpsimd", out_stats[:], sts[:], s_out, [(s_vp, a1_done)])
        prog.emit("gpsimd", lambda g: g.engine_nop(), [(s_out, 32)], None)
        return dict(allocs=allocs)

    ybf = nc.sbuf_tensor(name + "_ybf", [128, 64], BF16).__enter__()
    tgc = nc.sbuf_tensor(name + "_tgc", [128, 64], BF16).__enter__()
    s_sv = Sem(nc, name + "_sv")
    s_ss = Sem(nc, name + "_ss")
    for k4, (h, c) in enumerate(((0, 0), (0, 1), (1, 0), (1, 1))):
        prog.emit("tensor", lambda t, h=h, c=c, k4=k4:
                  t.transpose(pTb[0][:, 64 * k4:64 * (k4 + 1)],
                              WTb[0:64, 256 * h + 128 * c:256 * h + 128 * (c + 1)],
                              idb[0:64, 0:64]),
                  [(s_vp, a1_done)] if k4 == 0 else (), s_tr.inc(1) if k4 == 3 else None)
    prog.emit("vector", lambda v: v.tensor_copy(W2b[:], pTb[0][:, 0:256]),
              [(s_tr, s_tr.val)], s_sv.inc(1))
    carry_ok0 = (s_sv, s_sv.val)

    n_str = 0
    str_evac = []

    def tr_unit(tr_ops, evac_fn, eng, waits):
        nonlocal n_str
        bank = n_str % 2
        for oi, mk in enumerate(tr_ops):
            out_sl, in_ap, ident = mk(pTb[bank])
            tw = list(waits) if oi == 0 else []
            if oi == 0 and n_str >= 2:
                tw.append(str_evac[n_str - 2])
            prog.emit("tensor", lambda t, out_sl=out_sl, in_ap=in_ap, ident=ident:
                      t.transpose(out_sl, in_ap, ident),
                      tw, s_tr.inc(1) if oi == len(tr_ops) - 1 else None)
        trv = s_tr.val
        if eng == "scalar":
            prog.emit("scalar", lambda s, bank=bank: evac_fn(s, pTb[bank]), [(s_tr, trv)], s_ss.inc(1))
            ev = (s_ss, s_ss.val)
        else:
            prog.emit("vector", lambda v, bank=bank: evac_fn(v, pTb[bank]), [(s_tr, trv)], s_sv.inc(1))
            ev = (s_sv, s_sv.val)
        str_evac.append(ev)
        n_str += 1
        return ev

    x2gelu_v = {}
    x2bgelu_v = {}
    gm_v = {}
    gz2v_v = {}
    z1ev_v = {}
    updmm_v = {}
    carry_v = {}
    z2evac_v = {}
    a2mp_v = {}
    ycopy_v = {}

    for i in range(n_chunks):
        ii = 64 * i
        w0 = [(s_vp, a1_done), carry_ok0] if i == 0 else []

        def mk_tr2(srcfm, h):
            return lambda PB: (PB[64 * h:64 * (h + 1), 0:64],
                               srcfm[64 * h:64 * (h + 1), ii:ii + 64],
                               id2[64 * h:64 * (h + 1), :])
        uwq = list(w0)
        if i >= 1:
            uwq.append((s_sv, ycopy_v[i - 1]))
        ev_xqt = tr_unit([mk_tr2(xqf, 0), mk_tr2(xqf, 1)],
                         lambda e, PB: e.copy(xqt[:], PB[:, 0:64]), "scalar", uwq)
        uwk = []
        if i >= 1:
            uwk.append((s_mm, updmm_v[i - 1]))
        ev_xkt = tr_unit([mk_tr2(xkf, 0), mk_tr2(xkf, 1)],
                         lambda e, PB: e.copy(xkt[:], PB[:, 0:64]), "scalar", uwk)
        uwt = []
        if i >= 1:
            uwt.append((s_sv, gz2v_v[i - 1]))
        ev_tgc = tr_unit([mk_tr2(tgt, 0), mk_tr2(tgt, 1)],
                         lambda e, PB: e.copy(tgc[:], PB[:, 0:64]), "scalar", uwt)

        zw = list(w0)
        if i >= 1:
            zw.append((s_ss, z1ev_v[i - 1]))
            zw.append(carry_v[i - 1])
        for h in range(2):
            prog.emit("tensor", lambda t, h=h, ii=ii:
                      t.matmul(p0[64 * h:64 * (h + 1), 0:256],
                               xkf[:, ii:ii + 64],
                               W1zb[:, 256 * h:256 * (h + 1)], start=True, stop=False),
                      zw if h == 0 else (), None)
            prog.emit("tensor", lambda t, h=h:
                      t.matmul(p0[64 * h:64 * (h + 1), 0:256],
                               ons[0:1, 0:64],
                               b1b2[0:1, 256 * h:256 * (h + 1)], start=False, stop=True),
                      (), s_mm.inc(1) if h == 1 else None)
        z1v = s_mm.val

        gw = [(s_mm, z1v)]
        if i >= 1:
            gw.append((s_mm, updmm_v[i - 1]))
            gw.append((s_gp, gm_v[i - 1]))
        prog.emit("scalar", lambda s: s.activation(X2t[:], p0[:, 0:256], AF.Gelu_apprx_tanh),
                  gw, s_ss.inc(1))
        x2gelu_v[i] = s_ss.val
        prog.emit("scalar", lambda s: s.activation(sc[:, 0:256], p0[:, 0:256], AF.Square), (), None)
        prog.emit("scalar", lambda s: s.copy(sc[:, 256:512], p0[:, 0:256]), (), s_ss.inc(1))
        z1ev_v[i] = s_ss.val

        prog.emit("vector", lambda v: v.tensor_scalar(sc[:, 512:768], sc[:, 0:256], A_, 1.0,
                                                      ALU.mult, ALU.add), [(s_ss, z1ev_v[i])], None)
        prog.emit("vector", lambda v: v.tensor_tensor(sc[:, 512:768], sc[:, 512:768],
                                                      sc[:, 256:512], ALU.mult), (), s_sv.inc(1))
        uv = s_sv.val
        prog.emit("scalar", lambda s: s.activation(sc[:, 512:768], sc[:, 512:768], AF.Tanh, scale=GC),
                  [(s_sv, uv)], s_ss.inc(1))
        thv = s_ss.val
        prog.emit("vector", lambda v: v.tensor_scalar(sc[:, 0:256], sc[:, 0:256], 3 * A_ * GC, GC,
                                                      ALU.mult, ALU.add), [(s_ss, thv)], s_sv.inc(1))
        bcoefv = s_sv.val
        prog.emit("vector", lambda v: v.tensor_scalar(sc[:, 768:1024], sc[:, 512:768], -1.0, 1.0,
                                                      ALU.mult, ALU.add), (), s_sv.inc(1))
        prog.emit("gpsimd", lambda g: g.tensor_tensor(sc[:, 768:1024], sc[:, 768:1024], X2t[:], ALU.mult),
                  [(s_ss, x2gelu_v[i]), (s_sv, s_sv.val)], s_gp.inc(1))
        prog.emit("gpsimd", lambda g: g.tensor_tensor(sc[:, 768:1024], sc[:, 768:1024], sc[:, 0:256],
                                                      ALU.mult), [(s_sv, bcoefv)], s_gp.inc(1))
        gm2 = s_gp.val
        gm_v[i] = gm2
        prog.emit("vector", lambda v: v.tensor_scalar(gb[:], sc[:, 512:768], 0.5, 0.5, ALU.mult, ALU.add),
                  (), None)
        prog.emit("vector", lambda v: v.tensor_tensor(gb[:], gb[:], sc[:, 768:1024], ALU.add),
                  [(s_gp, gm2)], s_sv.inc(1))
        gbv = s_sv.val

        ev_x2f = None
        for c in range(2):
            ev_x2f = tr_unit([lambda PB, c=c: (PB[:, 0:128], X2t[:, 128 * c:128 * (c + 1)], idb[:])],
                             (lambda e, PB, c=c: e.copy(X2f[:, 128 * c:128 * (c + 1)], PB[:, 0:128])),
                             "scalar", [(s_ss, x2gelu_v[i])] if c == 0 else [])

        zw2 = [ev_x2f]
        if i >= 1:
            zw2.append(carry_v[i - 1])
        first = True
        for h in range(2):
            for c in range(2):
                prog.emit("tensor", lambda t, h=h, c=c:
                          t.matmul(p1[64 * h:64 * (h + 1), 0:64],
                                   X2f[:, 128 * c + 64 * h:128 * c + 64 * (h + 1)],
                                   W2b[:, 64 * (2 * h + c):64 * (2 * h + c + 1)],
                                   start=(c == 0), stop=False),
                          zw2 if first else (), None)
                first = False
            prog.emit("tensor", lambda t, h=h:
                      t.matmul(p1[64 * h:64 * (h + 1), 0:64],
                               ons[0:1, 0:64],
                               b2b2[0:1, 64 * h:64 * (h + 1)], start=False, stop=True),
                      (), s_mm.inc(1) if h == 1 else None)
        z2v = s_mm.val

        prog.emit("vector", lambda v: v.tensor_copy(sc2[:, 0:64], p1[:, 0:64]),
                  [(s_mm, z2v)], s_sv.inc(1))
        z2evac_v[i] = s_sv.val
        prog.emit("vector", lambda v: v.reduce_sum(lns[:, 0:1], sc2[:, 0:64], axis=AX.X), (), None)
        prog.emit("vector", lambda v: v.tensor_scalar(lns[:, 0:1], lns[:, 0:1], 1.0 / 64, None, ALU.mult),
                  (), None)
        prog.emit("vector", lambda v: v.tensor_scalar(sc2[:, 0:64], sc2[:, 0:64], lns[:, 0:1], None,
                                                      ALU.subtract), (), None)
        prog.emit("vector", lambda v: v.tensor_tensor(sc2[:, 64:128], sc2[:, 0:64], sc2[:, 0:64], ALU.mult),
                  (), None)
        prog.emit("vector", lambda v: v.reduce_sum(lns[:, 1:2], sc2[:, 64:128], axis=AX.X), (), None)
        prog.emit("vector", lambda v: v.tensor_scalar(lns[:, 1:2], lns[:, 1:2], 1.0 / 64, 1e-6,
                                                      ALU.mult, ALU.add), (), s_sv.inc(1))
        varv = s_sv.val
        lni = lns[:, 2:3].bitcast(mybir.dt.int32)
        lnv = lns[:, 1:2].bitcast(mybir.dt.int32)
        prog.emit("vector", lambda v: v.tensor_scalar(lni, lnv, 1, None, ALU.arith_shift_right),
                  [(s_sv, varv)], None)
        prog.emit("vector", lambda v: v.tensor_scalar(lni, lni, -1, 0x5f3759df, ALU.mult, ALU.add),
                  (), None)
        for _ in range(2):
            prog.emit("vector", lambda v: v.tensor_tensor(lns[:, 3:4], lns[:, 2:3], lns[:, 2:3], ALU.mult),
                      (), None)
            prog.emit("vector", lambda v: v.tensor_tensor(lns[:, 3:4], lns[:, 3:4], lns[:, 1:2], ALU.mult),
                      (), None)
            prog.emit("vector", lambda v: v.tensor_scalar(lns[:, 3:4], lns[:, 3:4], -0.5, 1.5,
                                                          ALU.mult, ALU.add), (), None)
            prog.emit("vector", lambda v: v.tensor_tensor(lns[:, 2:3], lns[:, 2:3], lns[:, 3:4], ALU.mult),
                      (), None)
        prog.emit("vector", lambda v: v.tensor_scalar(sc2[:, 0:64], sc2[:, 0:64], lns[:, 2:3], None,
                                                      ALU.mult), (), None)
        prog.emit("vector", lambda v: v.tensor_tensor(sc2[:, 64:128], sc2[:, 0:64], g4s, ALU.mult), (), None)
        prog.emit("vector", lambda v: v.tensor_tensor(sc2[:, 128:192], b4s, tgc[:], ALU.subtract),
                  [ev_tgc], None)
        prog.emit("vector", lambda v: v.tensor_tensor(sc2[:, 64:128], sc2[:, 64:128], sc2[:, 128:192],
                                                      ALU.add), (), None)
        prog.emit("vector", lambda v: v.tensor_tensor(sc2[:, 64:128], sc2[:, 64:128], g4s, ALU.mult),
                  (), None)
        prog.emit("vector", lambda v: v.reduce_sum(lns[:, 4:5], sc2[:, 64:128], axis=AX.X), (), None)
        prog.emit("vector", lambda v: v.tensor_tensor(sc2[:, 128:192], sc2[:, 64:128], sc2[:, 0:64],
                                                      ALU.mult), (), None)
        prog.emit("vector", lambda v: v.reduce_sum(lns[:, 5:6], sc2[:, 128:192], axis=AX.X), (), None)
        prog.emit("vector", lambda v: v.tensor_scalar(lns[:, 4:6], lns[:, 4:6], 1.0 / 64, None, ALU.mult),
                  (), None)
        prog.emit("vector", lambda v: v.tensor_scalar(sc2[:, 128:192], sc2[:, 0:64], lns[:, 5:6], None,
                                                      ALU.mult), (), None)
        prog.emit("vector", lambda v: v.tensor_tensor(sc2[:, 128:192], sc2[:, 64:128], sc2[:, 128:192],
                                                      ALU.subtract), (), None)
        prog.emit("vector", lambda v, i=i: v.tensor_tensor(lns[:, 6:7], lns[:, 2:3], eta[:, i:i + 1],
                                                           ALU.mult), (), None)
        prog.emit("vector", lambda v: v.tensor_scalar(lns[:, 6:7], lns[:, 6:7], -1.0, None, ALU.mult),
                  (), None)
        prog.emit("vector", lambda v: v.tensor_scalar(sc2[:, 128:192], sc2[:, 128:192], lns[:, 4:5], None,
                                                      ALU.subtract), (), None)
        gz2w = [(s_mm, updmm_v[i - 1])] if i >= 1 else []
        prog.emit("vector", lambda v: v.tensor_scalar(gz2[:], sc2[:, 128:192], lns[:, 6:7], None,
                                                      ALU.mult), gz2w, s_sv.inc(1))
        gz2v_v[i] = s_sv.val
        prog.emit("vector", lambda v: v.tensor_copy(gz2z[0:64, 0:64], gz2[0:64, :]), (), None)
        prog.emit("vector", lambda v: v.tensor_copy(gz2z[64:128, 64:128], gz2[64:128, :]),
                  (), s_sv.inc(1))
        gz2zv = s_sv.val

        ev_gz2f = tr_unit([lambda PB: (PB[0:64, 0:128], gz2[:], idb[:])],
                          (lambda e, PB: e.copy(gz2f[0:64, 0:128], PB[0:64, 0:128])),
                          "scalar", [(s_sv, gz2v_v[i])])

        g1w = [ev_gz2f]
        if i >= 1:
            g1w.append(carry_v[i - 1])
        for h in range(2):
            prog.emit("tensor", lambda t, h=h:
                      t.matmul(p2[64 * h:64 * (h + 1), 0:256],
                               gz2f[0:64, 64 * h:64 * (h + 1)],
                               WTb[0:64, 256 * h:256 * (h + 1)], start=True, stop=True),
                      g1w if h == 0 else (), s_mm.inc(1) if h == 1 else None)
        gz1mmv = s_mm.val
        gz1w = [(s_mm, gz1mmv), (s_sv, gbv)]
        if i >= 1:
            gz1w.append((s_mm, updmm_v[i - 1]))
        prog.emit("vector", lambda v: v.tensor_tensor(gz1[:], p2[:, 0:256], gb[:], ALU.mult),
                  gz1w, s_sv.inc(1))
        gz1v = s_sv.val
        prog.emit("vector", lambda v: v.tensor_copy(gz1z[0:64, 0:256], gz1[0:64, :]), (), None)
        prog.emit("vector", lambda v: v.tensor_copy(gz1z[64:128, 256:512], gz1[64:128, :]),
                  (), s_sv.inc(1))
        gz1zv = s_sv.val

        zbw = [(s_sv, gz1zv)]
        if i >= 1:
            zbw.append(carry_v[i - 1])
        for h in range(2):
            prog.emit("tensor", lambda t, h=h, ii=ii:
                      t.matmul(p3[64 * h:64 * (h + 1), 0:256],
                               xqf[:, ii:ii + 64],
                               W1zb[:, 256 * h:256 * (h + 1)], start=True, stop=False),
                      zbw if h == 0 else (), None)
            prog.emit("tensor", lambda t, h=h, ii=ii:
                      t.matmul(p3[64 * h:64 * (h + 1), 0:256],
                               a1m[:, ii:ii + 64],
                               gz1z[:, 256 * h:256 * (h + 1)], start=False, stop=False),
                      (), None)
            prog.emit("tensor", lambda t, h=h:
                      t.matmul(p3[64 * h:64 * (h + 1), 0:256],
                               ons[0:1, 0:64],
                               b1b2[0:1, 256 * h:256 * (h + 1)], start=False, stop=True),
                      (), s_mm.inc(1) if h == 1 else None)
        z1bv = s_mm.val
        x2bw = [(s_mm, z1bv)]
        if i >= 1:
            x2bw.append((s_mm, updmm_v[i - 1]))
        prog.emit("scalar", lambda s: s.activation(X2bt[:], p3[:, 0:256], AF.Gelu_apprx_tanh),
                  x2bw, s_ss.inc(1))
        x2bgelu_v[i] = s_ss.val

        ev_x2bf = None
        for c in range(2):
            ev_x2bf = tr_unit([lambda PB, c=c: (PB[:, 0:128], X2bt[:, 128 * c:128 * (c + 1)], idb[:])],
                              (lambda e, PB, c=c: e.copy(X2bf[:, 128 * c:128 * (c + 1)], PB[:, 0:128])),
                              "scalar", [(s_ss, x2bgelu_v[i])] if c == 0 else [])

        a2w = [ev_x2bf]
        if i >= 1:
            a2w.append(carry_v[i - 1])
        first = True
        for h in range(2):
            for c in range(2):
                prog.emit("tensor", lambda t, h=h, c=c:
                          t.matmul(p5[64 * h:64 * (h + 1), 0:64],
                                   X2f[:, 128 * c + 64 * h:128 * c + 64 * (h + 1)],
                                   X2bf[:, 128 * c + 64 * h:128 * c + 64 * (h + 1)],
                                   start=(c == 0), stop=(c == 1)),
                          a2w if first else (), s_mm.inc(1) if (h == 1 and c == 1) else None)
                first = False
        a2v = s_mm.val
        prog.emit("vector", lambda v: v.scalar_tensor_tensor(a2m[:], p5[:, 0:64], 1.0, mskT,
                                                             ALU.add, ALU.mult),
                  [(s_mm, a2v)], s_sv.inc(1))
        a2mp_v[i] = s_sv.val

        zb2w = [(s_sv, a2mp_v[i]), (s_sv, z2evac_v[i]), (s_sv, gz2zv)]
        first = True
        for h in range(2):
            for c in range(2):
                prog.emit("tensor", lambda t, h=h, c=c:
                          t.matmul(p1[64 * h:64 * (h + 1), 0:64],
                                   X2bf[:, 128 * c + 64 * h:128 * c + 64 * (h + 1)],
                                   W2b[:, 64 * (2 * h + c):64 * (2 * h + c + 1)],
                                   start=(c == 0), stop=False),
                          zb2w if first else (), None)
                first = False
            prog.emit("tensor", lambda t, h=h:
                      t.matmul(p1[64 * h:64 * (h + 1), 0:64],
                               a2m[:, 0:64] if False else a2m[:],
                               gz2z[:, 64 * h:64 * (h + 1)], start=False, stop=False),
                      (), None)
            prog.emit("tensor", lambda t, h=h:
                      t.matmul(p1[64 * h:64 * (h + 1), 0:64],
                               ons[0:1, 0:64],
                               b2b2[0:1, 64 * h:64 * (h + 1)], start=False, stop=True),
                      (), s_mm.inc(1) if h == 1 else None)
        z2bv = s_mm.val

        prog.emit("vector", lambda v: v.tensor_copy(sc2[:, 192:256], p1[:, 0:64]),
                  [(s_mm, z2bv)], s_sv.inc(1))
        prog.emit("vector", lambda v: v.reduce_sum(lns[:, 8:9], sc2[:, 192:256], axis=AX.X), (), None)
        prog.emit("vector", lambda v: v.tensor_scalar(lns[:, 8:9], lns[:, 8:9], 1.0 / 64, None, ALU.mult),
                  (), None)
        prog.emit("vector", lambda v: v.tensor_scalar(sc2[:, 192:256], sc2[:, 192:256], lns[:, 8:9], None,
                                                      ALU.subtract), (), None)
        prog.emit("vector", lambda v: v.tensor_tensor(sc2[:, 320:384], sc2[:, 192:256], sc2[:, 192:256],
                                                      ALU.mult), (), None)
        prog.emit("vector", lambda v: v.reduce_sum(lns[:, 9:10], sc2[:, 320:384], axis=AX.X), (), None)
        prog.emit("vector", lambda v: v.tensor_scalar(lns[:, 9:10], lns[:, 9:10], 1.0 / 64, 1e-6,
                                                      ALU.mult, ALU.add), (), s_sv.inc(1))
        var2v = s_sv.val
        ln2i = lns[:, 10:11].bitcast(mybir.dt.int32)
        ln2v = lns[:, 9:10].bitcast(mybir.dt.int32)
        prog.emit("vector", lambda v: v.tensor_scalar(ln2i, ln2v, 1, None, ALU.arith_shift_right),
                  [(s_sv, var2v)], None)
        prog.emit("vector", lambda v: v.tensor_scalar(ln2i, ln2i, -1, 0x5f3759df, ALU.mult, ALU.add),
                  (), None)
        for _ in range(2):
            prog.emit("vector", lambda v: v.tensor_tensor(lns[:, 11:12], lns[:, 10:11], lns[:, 10:11],
                                                          ALU.mult), (), None)
            prog.emit("vector", lambda v: v.tensor_tensor(lns[:, 11:12], lns[:, 11:12], lns[:, 9:10],
                                                          ALU.mult), (), None)
            prog.emit("vector", lambda v: v.tensor_scalar(lns[:, 11:12], lns[:, 11:12], -0.5, 1.5,
                                                          ALU.mult, ALU.add), (), None)
            prog.emit("vector", lambda v: v.tensor_tensor(lns[:, 10:11], lns[:, 10:11], lns[:, 11:12],
                                                          ALU.mult), (), None)
        prog.emit("vector", lambda v: v.tensor_scalar(sc2[:, 192:256], sc2[:, 192:256], lns[:, 10:11],
                                                      None, ALU.mult), (), None)
        prog.emit("vector", lambda v: v.tensor_tensor(sc2[:, 192:256], sc2[:, 192:256], g4s, ALU.mult),
                  (), None)
        prog.emit("vector", lambda v: v.tensor_tensor(sc2[:, 192:256], sc2[:, 192:256], b4s, ALU.add),
                  (), None)
        prog.emit("vector", lambda v: v.tensor_copy(sc2[:, 256:320], xqt[:]), [ev_xqt], s_sv.inc(1))
        ycopy_v[i] = s_sv.val
        prog.emit("vector", lambda v: v.tensor_tensor(yc[:], sc2[:, 192:256], sc2[:, 256:320], ALU.add),
                  (), None)
        prog.emit("vector", lambda v, i=i: v.reduce_sum(sts[:, 2 * i:2 * i + 1], yc[:], axis=AX.X),
                  (), None)
        prog.emit("vector", lambda v: v.tensor_tensor(sc2[:, 320:384], yc[:], yc[:], ALU.mult), (), None)
        prog.emit("vector", lambda v, i=i: v.reduce_sum(sts[:, 2 * i + 1:2 * i + 2], sc2[:, 320:384],
                                                        axis=AX.X), (), None)
        prog.emit("vector", lambda v: v.tensor_copy(ybf[:], yc[:]), (), s_sv.inc(1))
        ybfv = s_sv.val
        tr_unit([(lambda PB, h_=h_: (PB[64 * h_:64 * (h_ + 1), 0:64], ybf[64 * h_:64 * (h_ + 1), :],
                                     id2[64 * h_:64 * (h_ + 1), :]))
                 for h_ in range(2)],
                (lambda e, PB, ii=ii: e.tensor_copy(yfm[:, ii:ii + 64], PB[:, 0:64])),
                "vector", [(s_sv, ybfv)])

        uw1 = [(s_sv, gz1zv), ev_xkt]
        if i >= 1:
            uw1.append(carry_v[i - 1])
        for h in range(2):
            prog.emit("tensor", lambda t, h=h:
                      t.matmul(p4[64 * h:64 * (h + 1), 0:256],
                               xkt[:, 0:64],
                               gz1z[:, 256 * h:256 * (h + 1)], start=True, stop=True),
                      uw1 if h == 0 else (), None)
        prog.emit("tensor", lambda t: t.matmul(p2[0:1, 256:512], o1z[:, 0:1], gz1[:, :],
                                               start=True, stop=True), (), None)
        prog.emit("tensor", lambda t: t.matmul(p3[0:1, 256:512], o1z[:, 1:2], gz1[:, :],
                                               start=True, stop=True),
                  [(s_ss, x2bgelu_v[i])], None)
        wtdw = [(s_ss, z1ev_v[i]), (s_sv, a2mp_v[i]), (s_ss, x2gelu_v[i])]
        for h in range(2):
            prog.emit("tensor", lambda t, h=h:
                      t.matmul(p5[0:64, 256 * h:256 * (h + 1)],
                               gz2z[:, 64 * h:64 * (h + 1)],
                               X2t[:, 0:256], start=True, stop=True),
                      wtdw if h == 0 else (), None)
        for h in range(2):
            prog.emit("tensor", lambda t, h=h:
                      t.matmul(p4[0:1, 256 + 64 * h:320 + 64 * h],
                               o1z[:, h:h + 1],
                               gz2[:, :], start=True, stop=True),
                      (), s_mm.inc(1) if h == 1 else None)
        updmm_v[i] = s_mm.val

        cw = [(s_mm, updmm_v[i])]
        prog.emit("vector", lambda v: v.tensor_tensor(W1f[:], W1f[:], p4[:, 0:256], ALU.add), cw, None)
        prog.emit("vector", lambda v: v.tensor_copy(W1zb[0:64, 0:256], W1f[0:64, :]), (), None)
        prog.emit("vector", lambda v: v.tensor_copy(W1zb[64:128, 256:512], W1f[64:128, :]), (), None)
        prog.emit("vector", lambda v: v.tensor_tensor(WTf[:], WTf[:], p5[0:64, 0:512], ALU.add), (), None)
        prog.emit("vector", lambda v: v.tensor_copy(WTb[:], WTf[:]), (), s_sv.inc(1))
        wtb_ev = (s_sv, s_sv.val)
        prog.emit("vector", lambda v: v.tensor_tensor(b1f[0:1, 0:256], b1f[0:1, 0:256],
                                                      p2[0:1, 256:512], ALU.add), (), None)
        prog.emit("vector", lambda v: v.tensor_tensor(b1f[0:1, 256:512], b1f[0:1, 256:512],
                                                      p3[0:1, 256:512], ALU.add), (), None)
        prog.emit("vector", lambda v: v.tensor_tensor(b2f[0:1, :], b2f[0:1, :],
                                                      p4[0:1, 256:384], ALU.add), (), None)
        prog.emit("vector", lambda v: v.tensor_copy(b1b2[:], b1f[:]), (), None)
        prog.emit("vector", lambda v: v.tensor_copy(b2b2[:], b2f[:]), (), s_sv.inc(1))
        ev_w2b = tr_unit([(lambda PB, hc=hc: (PB[:, 64 * hc[2]:64 * (hc[2] + 1)],
                                              WTb[0:64, 256 * hc[0] + 128 * hc[1]:
                                                  256 * hc[0] + 128 * (hc[1] + 1)],
                                              idb[0:64, 0:64]))
                          for hc in ((0, 0, 0), (0, 1, 1), (1, 0, 2), (1, 1, 3))],
                         (lambda e, PB: e.tensor_copy(W2b[:], PB[:, 0:256])),
                         "vector", [wtb_ev])
        carry_v[i] = ev_w2b

    scan_done_sv = s_sv.val
    prog.dma("gpsimd", out_yfm[:], yfm[:], s_out, [(s_sv, scan_done_sv)])
    prog.dma("gpsimd", out_stats[:], sts[:], s_out, [(s_sv, scan_done_sv)])
    prog.emit("gpsimd", lambda g: g.engine_nop(), [(s_out, 32)], None)
    return dict(allocs=allocs)



def emit_l1_post(nc, prog, y_tm, pnw_rep, pnb_rep, WoT, x2_own, rw,
                 out_x3, out_x3fm, out_emask, out_gate, name="l1p"):
    st = nc.sbuf_tensor
    pt = nc.psum_tensor
    allocs = [
        st(name + "_ylnf", [128, 16384], BF16),
        st(name + "_wot", [128, 8 * 1024], BF16),
        st(name + "_pn", [128, 2048], F32),
        st(name + "_rw", [128, 64], F32),
        st(name + "_stg", [128, 2 * 1024], F32),
        st(name + "_yld", [128, 2 * 1024], BF16),
        st(name + "_yln", [128, 2 * 1024], F32),
        st(name + "_ylb", [128, 2 * 1024], BF16),
        st(name + "_xr", [128, 2 * 1024], F32),
        st(name + "_o3", [128, 2 * 1024], F32),
        st(name + "_o3b", [128, 2 * 1024], BF16),
        st(name + "_o3f", [128, 2 * 1024], BF16),
        st(name + "_o3ff", [128, 2 * 1024], F32),
        st(name + "_sq", [128, 1024], F32),
        st(name + "_eout", [128, 32], F32),
        st(name + "_gsc", [128, 32], F32),
        st(name + "_fs", [128, 16], F32),
        st(name + "_idb", [128, 128], BF16),
        st(name + "_iota", [128, 384], F32),
        pt(name + "_pA", [128, 512 * 6], F32),
        pt(name + "_pT0", [128, 512], F32),
        pt(name + "_pT1", [128, 512], F32),
    ]
    hs = [a.__enter__() for a in allocs]
    (ylnf, wot, pn, rwsb, stg, yld, yln, ylb, xr, o3, o3b, o3f, o3ff, sq, eout, gsc, fs, idb, iota,
     pA, pT0, pT1) = hs
    pTb = (pT0[:, 0:256].bitcast(BF16), pT1[:, 0:256].bitcast(BF16))
    rw3 = rwsb[:].rearrange("p (j c) -> p j c", j=8)
    ylnf3 = ylnf[:].rearrange("p (j t) -> p j t", j=8)
    o3f3 = [o3f[:, 1024 * par:1024 * (par + 1)].rearrange("p (j c) -> p j c", j=8) for par in range(2)]
    o3ff3 = [o3ff[:, 1024 * par:1024 * (par + 1)].rearrange("p (j c) -> p j c", j=8) for par in range(2)]
    pTf = (pT0, pT1)

    s_ld = [Sem(nc, name + f"_ld{i}") for i in range(2)]
    s_wld = Sem(nc, name + "_wld")
    s_cast = Sem(nc, name + "_cast")
    s_v = Sem(nc, name + "_v")
    s_s = Sem(nc, name + "_s")
    s_g = Sem(nc, name + "_g")
    s_tr = Sem(nc, name + "_tr")
    s_mm = Sem(nc, name + "_mm")
    s_out = [Sem(nc, name + f"_o{i}") for i in range(4)]

    prog.emit("gpsimd", lambda g: g.iota(iota[:, 0:128], pattern=[[1, 128]], base=0,
                                         channel_multiplier=0, allow_small_or_imprecise_dtypes=True),
              (), s_g.inc(1))
    prog.emit("gpsimd", lambda g: g.iota(iota[:, 128:256], pattern=[[0, 128]], base=0,
                                         channel_multiplier=1, allow_small_or_imprecise_dtypes=True),
              (), s_g.inc(1))
    prog.emit("vector", lambda v: v.tensor_tensor(iota[:, 256:384], iota[:, 0:128], iota[:, 128:256],
                                                  ALU.is_equal), [(s_g, 2)], s_v.inc(1))
    prog.emit("vector", lambda v: v.tensor_copy(idb[:], iota[:, 256:384]), (), s_v.inc(1))
    id_ok = s_v.val
    prog.dma("sync", pn[:, 0:1024], pnw_rep[:], s_wld)
    prog.dma("sync", pn[:, 1024:2048], pnb_rep[:], s_wld)
    for j in range(8):
        prog.dma("sync", stg[:, 0:1024], WoT[128 * j:128 * (j + 1), :], s_wld,
                 [(s_cast, j)] if j >= 1 else [])
        prog.emit("scalar", lambda s, j=j: s.copy(wot[:, 1024 * j:1024 * (j + 1)], stg[:, 0:1024]),
                  [(s_wld, s_wld.val)], s_cast.inc(1))
    prog.dma("sync", stg[:, 1024:1088].rearrange("p (j c) -> p j c", j=8),
             rw.rearrange("(j p) c -> p j c", p=128), s_wld)
    prog.emit("scalar", lambda s: s.copy(rwsb[:], stg[:, 1024:1088]),
              [(s_wld, s_wld.val)], s_cast.inc(1))
    w_ok = s_cast.val

    ntr = 0
    trev = []
    lnv_t = []
    for T in range(16):
        par = T % 2
        lw = [(s_cast, w_ok)] if T < 2 else [(s_v, lnv_t[T - 2])]
        prog.dma("sync", yld[:, 1024 * par:1024 * (par + 1)], y_tm[128 * T:128 * (T + 1), :],
                 s_ld[par], lw)
        ldv = s_ld[par].val
        Y = yld[:, 1024 * par:1024 * (par + 1)]
        L = yln[:, 1024 * par:1024 * (par + 1)]
        B = ylb[:, 1024 * par:1024 * (par + 1)]
        prog.emit("vector", lambda v, Y=Y, L=L: v.tensor_copy(L, Y), [(s_ld[par], ldv)], None)
        prog.emit("vector", lambda v, L=L: v.reduce_sum(fs[:, 0:1], L, axis=AX.X), (), None)
        prog.emit("vector", lambda v: v.tensor_scalar(fs[:, 0:1], fs[:, 0:1], 1.0 / 1024, None, ALU.mult),
                  (), None)
        prog.emit("vector", lambda v, L=L: v.tensor_scalar(L, L, fs[:, 0:1], None, ALU.subtract), (), None)
        prog.emit("vector", lambda v, L=L: v.tensor_tensor(sq[:], L, L, ALU.mult), (), None)
        prog.emit("vector", lambda v: v.reduce_sum(fs[:, 1:2], sq[:], axis=AX.X), (), None)
        prog.emit("vector", lambda v: v.tensor_scalar(fs[:, 1:2], fs[:, 1:2], 1.0 / 1024, 1e-6,
                                                      ALU.mult, ALU.add), (), s_v.inc(1))
        vv = s_v.val
        prog.emit("scalar", lambda s: s.activation(fs[:, 2:3], fs[:, 1:2], AF.Ln), [(s_v, vv)], None)
        prog.emit("scalar", lambda s: s.activation(fs[:, 3:4], fs[:, 2:3], AF.Exp, scale=-0.5),
                  (), s_s.inc(1))
        sv = s_s.val
        prog.emit("vector", lambda v, L=L: v.tensor_scalar(L, L, fs[:, 3:4], None, ALU.mult),
                  [(s_s, sv)], None)
        prog.emit("vector", lambda v, L=L: v.tensor_tensor(L, L, pn[:, 0:1024], ALU.mult), (), None)
        prog.emit("vector", lambda v, L=L: v.tensor_tensor(L, L, pn[:, 1024:2048], ALU.add), (), None)
        prog.emit("vector", lambda v, L=L, B=B: v.tensor_copy(B, L), (), s_v.inc(1))
        bv = s_v.val
        for grp in range(2):
            bank = ntr % 2
            tw = [(s_v, bv), (s_v, id_ok)]
            if ntr >= 2:
                tw.append((s_v, trev[ntr - 2]))
            for jj in range(4):
                j = grp * 4 + jj
                prog.emit("tensor", lambda t, B=B, j=j, bank=bank, jj=jj:
                          t.transpose(pTb[bank][:, 128 * jj:128 * (jj + 1)],
                                      B[:, 128 * j:128 * (j + 1)], idb[:]),
                          tw if jj == 0 else (), s_tr.inc(1) if jj == 3 else None)
            prog.emit("vector", lambda v, T=T, grp=grp, bank=bank:
                      v.tensor_copy(ylnf3[:, grp * 4:(grp + 1) * 4, 128 * T:128 * (T + 1)],
                                    pTb[bank][:].rearrange("p (j t) -> p j t", j=4)),
                      [(s_tr, s_tr.val)], s_v.inc(1))
            trev.append(s_v.val)
            ntr += 1
        lnv_t.append(s_v.val)
    ylnf_ok = s_v.val

    fin_ev = []
    evac_prev = []
    emask_done = []
    for T in range(16):
        par = T % 2
        xw = [(s_v, ylnf_ok)]
        if T >= 2:
            xw.append(fin_ev[T - 2])
        prog.dma("sync", xr[:, 1024 * par:1024 * (par + 1)], x2_own[128 * T:128 * (T + 1), :],
                 s_ld[par], xw)
        xldv = s_ld[par].val
        po = 1024 + 1024 * par
        mw = [(s_cast, w_ok), (s_v, ylnf_ok)]
        if T >= 2:
            mw.append((s_v, evac_prev[T - 2]))
        for hh in range(2):
            for j in range(8):
                prog.emit("tensor", lambda t, T=T, hh=hh, j=j, po=po:
                          t.matmul(pA[:, po + 512 * hh:po + 512 * (hh + 1)],
                                   ylnf3[:, j, 128 * T:128 * (T + 1)],
                                   wot[:, 1024 * j + 512 * hh:1024 * j + 512 * (hh + 1)],
                                   start=(j == 0), stop=(j == 7)),
                          mw if (hh == 0 and j == 0) else (),
                          s_mm.inc(1) if (hh == 1 and j == 7) else None)
        mmv = s_mm.val
        O = o3[:, 1024 * par:1024 * (par + 1)]
        prog.emit("vector", lambda v, O=O, po=po: v.tensor_copy(O, pA[:, po:po + 1024]),
                  [(s_mm, mmv)], s_v.inc(1))
        evac_prev.append(s_v.val)
        prog.emit("vector", lambda v, O=O, par=par: v.tensor_tensor(
                      O, O, xr[:, 1024 * par:1024 * (par + 1)], ALU.add),
                  [(s_ld[par], xldv)], None)
        prog.emit("vector", lambda v, O=O: v.reduce_sum(fs[:, 4:5], O, axis=AX.X), (), None)
        prog.emit("vector", lambda v: v.tensor_scalar(fs[:, 4:5], fs[:, 4:5], 1.0 / 1024, None, ALU.mult),
                  (), None)
        prog.emit("vector", lambda v, O=O: v.tensor_scalar(O, O, fs[:, 4:5], None, ALU.subtract), (), None)
        prog.emit("vector", lambda v, O=O: v.tensor_tensor(sq[:], O, O, ALU.mult), (), None)
        prog.emit("vector", lambda v: v.reduce_sum(fs[:, 5:6], sq[:], axis=AX.X), (), None)
        prog.emit("vector", lambda v: v.tensor_scalar(fs[:, 5:6], fs[:, 5:6], 1.0 / 1024, 1e-5,
                                                      ALU.mult, ALU.add), (), s_v.inc(1))
        v2 = s_v.val
        prog.emit("scalar", lambda s: s.activation(fs[:, 6:7], fs[:, 5:6], AF.Ln), [(s_v, v2)], None)
        prog.emit("scalar", lambda s: s.activation(fs[:, 7:8], fs[:, 6:7], AF.Exp, scale=-0.5),
                  (), s_s.inc(1))
        r2 = s_s.val
        ow = [(s_s, r2)]
        if T >= 2:
            ow.append((s_out[0], 16 * (T - 1)))
        prog.emit("vector", lambda v, O=O: v.tensor_scalar(O, O, fs[:, 7:8], None, ALU.mult), ow, None)
        prog.emit("vector", lambda v, O=O, par=par: v.tensor_copy(o3b[:, 1024 * par:1024 * (par + 1)], O),
                  (), s_v.inc(1))
        bv2 = s_v.val
        fin_ev.append((s_v, bv2))
        prog.dma("gpsimd", out_x3[128 * T:128 * (T + 1), :], O, s_out[0], [(s_v, bv2)])
        for grp in range(2):
            bank = ntr % 2
            tw2 = [(s_v, bv2)]
            if ntr >= 2:
                tw2.append((s_v, trev[ntr - 2]))
            for jj in range(4):
                j = grp * 4 + jj
                prog.emit("tensor", lambda t, par=par, j=j, bank=bank, jj=jj:
                          t.transpose(pTb[bank][:, 128 * jj:128 * (jj + 1)],
                                      o3b[:, 1024 * par + 128 * j:1024 * par + 128 * (j + 1)], idb[:]),
                          tw2 if jj == 0 else (), s_tr.inc(1) if jj == 3 else None)
            prog.emit("vector", lambda v, par=par, grp=grp, bank=bank:
                      v.tensor_copy(o3f3[par][:, grp * 4:(grp + 1) * 4, :].rearrange("p j c -> p (j c)")
                                    if False else
                                    o3f[:, 1024 * par + 512 * grp:1024 * par + 512 * (grp + 1)],
                                    pTb[bank][:]),
                      [(s_tr, s_tr.val)], s_v.inc(1))
            trev.append(s_v.val)
            ntr += 1
        o3f_ok = (s_v, trev[-1])
        prog.dma("gpsimd",
                 out_x3fm[:, 128 * T:128 * (T + 1)].rearrange("(j p) c -> p j c", p=128),
                 o3f[:, 1024 * par:1024 * (par + 1)].rearrange("p (j c) -> p j c", j=8),
                 s_out[1], [o3f_ok])
        for grp in range(2):
            bank = ntr % 2
            tw3 = [(s_v, bv2)]
            if ntr >= 2:
                tw3.append((s_v, trev[ntr - 2]))
            for jj in range(4):
                j = grp * 4 + jj
                prog.emit("tensor", lambda t, O=O, j=j, bank=bank, jj=jj:
                          t.transpose(pTf[bank][:, 128 * jj:128 * (jj + 1)],
                                      O[:, 128 * j:128 * (j + 1)], iota[:, 256:384]),
                          tw3 if jj == 0 else (), s_tr.inc(1) if jj == 3 else None)
            prog.emit("vector", lambda v, par=par, grp=grp, bank=bank:
                      v.tensor_copy(o3ff[:, 1024 * par + 512 * grp:1024 * par + 512 * (grp + 1)],
                                    pTf[bank][:]),
                      [(s_tr, s_tr.val)], s_v.inc(1))
            trev.append(s_v.val)
            ntr += 1
        o3ff_ok = (s_v, trev[-1])
        rmw = [o3ff_ok]
        if T >= 2:
            rmw.append((s_v, emask_done[T - 2]))
        for j in range(8):
            prog.emit("tensor", lambda t, par=par, j=j:
                      t.matmul(pA[:, 512 * par:512 * par + 8],
                               o3ff3[par][:, j, :], rw3[:, j, :],
                               start=(j == 0), stop=(j == 7)),
                      rmw if j == 0 else (), s_mm.inc(1) if j == 7 else None)
        rmmv = s_mm.val
        G = gsc
        prog.emit("vector", lambda v, par=par: v.reduce_max(G[:, 0:1], pA[:, 512 * par:512 * par + 8],
                                                            axis=AX.X), [(s_mm, rmmv)], None)
        prog.emit("vector", lambda v: v.tensor_scalar(G[:, 1:2], G[:, 0:1], -1.0, None, ALU.mult),
                  (), None)
        eow = [(s_out[2], 16 * (T - 1)), (s_out[3], 16 * (T - 1))] if T >= 2 else []
        prog.emit("vector", lambda v, par=par: v.tensor_scalar(eout[:, 16 * par:16 * par + 8],
                                                               pA[:, 512 * par:512 * par + 8],
                                                               G[:, 0:1], None, ALU.is_ge),
                  eow, s_v.inc(1))
        gsv = s_v.val
        prog.emit("scalar", lambda s, par=par: s.activation(G[:, 8:16], pA[:, 512 * par:512 * par + 8],
                                                            AF.Exp, bias=G[:, 1:2], accum_out=G[:, 2:3]),
                  [(s_v, gsv)], s_s.inc(1))
        esv = s_s.val
        prog.emit("vector", lambda v, par=par: v.reciprocal(eout[:, 16 * par + 8:16 * par + 9],
                                                            G[:, 2:3]), [(s_s, esv)], s_v.inc(1))
        emv = s_v.val
        emask_done.append(emv)
        prog.dma("gpsimd", out_emask[128 * T:128 * (T + 1), :], eout[:, 16 * par:16 * par + 8],
                 s_out[2], [(s_v, emv)])
        prog.dma("gpsimd", out_gate[128 * T:128 * (T + 1), :], eout[:, 16 * par + 8:16 * par + 9],
                 s_out[3], [(s_v, emv)])
    prog.emit("gpsimd", lambda g: g.engine_nop(),
              [(s_out[0], 16 * 16), (s_out[1], 16 * 16), (s_out[2], 16 * 16), (s_out[3], 16 * 16)], None)
    return dict(allocs=allocs)



def emit_cumsum(nc, prog, emask_all, out_pos, name="csx"):
    st = nc.sbuf_tensor
    pt = nc.psum_tensor
    allocs = [
        st(name + "_em", [128, 2 * 8], BF16),
        st(name + "_emf", [128, 2 * 8], F32),
        st(name + "_offf", [1, 8], F32),
        st(name + "_pos", [128, 2 * 8], F32),
        st(name + "_tri", [128, 128], BF16),
        st(name + "_ons", [128, 128], BF16),
        st(name + "_onf", [1, 128], F32),
        pt(name + "_pA", [128, 512 * 6], F32),
    ]
    hs = [a.__enter__() for a in allocs]
    (em, emf, offf, pos, tri, ons, onf, pA) = hs
    s_eld = [Sem(nc, name + f"_eld{i}") for i in range(2)]
    s_v = Sem(nc, name + "_v")
    s_g = Sem(nc, name + "_g")
    s_mm = Sem(nc, name + "_mm")
    s_out = [Sem(nc, name + f"_o{i}") for i in range(2)]

    prog.emit("vector", lambda v: v.memset(ons[:], 1.0), (), s_v.inc(1))
    prog.emit("gpsimd", lambda g: g.affine_select(tri[:], ons[:], pattern=[[1, 128]],
                                                  compare_op=ALU.is_ge, fill=0.0, base=0,
                                                  channel_multiplier=-1),
              [(s_v, 1)], s_g.inc(1))
    tri_ok = s_g.val
    prog.emit("vector", lambda v: v.memset(offf[:], 0.0), (), s_v.inc(1))
    prog.emit("vector", lambda v: v.memset(onf[:], 1.0), (), s_v.inc(1))
    init_ok = s_v.val
    csum_done = {}
    off_upd = {}
    mcvs = {}
    for T in range(128):
        par = T % 2
        lw = [(s_v, init_ok)]
        if T >= 2:
            lw.append((s_v, csum_done[T - 2]))
        prog.dma("sync", emf[:, 8 * par:8 * (par + 1)], emask_all[128 * T:128 * (T + 1), :],
                 s_eld[par], lw)
        eldv = s_eld[par].val
        cpw = [(s_eld[par], eldv)]
        if T >= 2:
            cpw.append((s_mm, mcvs[T - 2]))
        prog.emit("vector", lambda v, par=par: v.tensor_copy(em[:, 8 * par:8 * (par + 1)],
                                                             emf[:, 8 * par:8 * (par + 1)]),
                  cpw, s_v.inc(1))
        cpv = s_v.val
        mmw = [(s_v, cpv), (s_g, tri_ok), (s_v, init_ok)]
        if T >= 2:
            mmw.append((s_v, csum_done[T - 2]))
        prog.emit("tensor", lambda t, par=par: t.matmul(pA[:, 512 * par:512 * par + 8], tri[:],
                                                        em[:, 8 * par:8 * (par + 1)],
                                                        start=True, stop=True), mmw, s_mm.inc(1))
        mav = s_mm.val
        bw = []
        if T >= 1:
            bw.append((s_v, off_upd[T - 1]))
        prog.emit("tensor", lambda t, par=par: t.matmul(pA[:, 1024 + 512 * par:1024 + 512 * par + 8],
                                                        onf[0:1, :], offf[0:1, 0:8],
                                                        start=True, stop=True), bw, s_mm.inc(1))
        mbv = s_mm.val
        prog.emit("tensor", lambda t, par=par: t.matmul(pA[0:1, 2048 + 512 * par:2048 + 512 * par + 8],
                                                        ons[:, 0:1], em[:, 8 * par:8 * (par + 1)],
                                                        start=True, stop=True), (), s_mm.inc(1))
        mcv = s_mm.val
        mcvs[T] = mcv
        pw = [(s_mm, mav)]
        if T >= 2:
            pw.append((s_out[par], 16 * (T // 2)))
        prog.emit("vector", lambda v, par=par:
                  v.scalar_tensor_tensor(pos[:, 8 * par:8 * (par + 1)],
                                         emf[:, 8 * par:8 * (par + 1)], -1.0,
                                         pA[:, 512 * par:512 * par + 8], ALU.mult, ALU.add),
                  pw, None)
        prog.emit("vector", lambda v, par=par: v.tensor_tensor(
                      pos[:, 8 * par:8 * (par + 1)], pos[:, 8 * par:8 * (par + 1)],
                      pA[:, 1024 + 512 * par:1024 + 512 * par + 8], ALU.add),
                  [(s_mm, mbv)], s_v.inc(1))
        csum_done[T] = s_v.val
        prog.dma("gpsimd", out_pos[128 * T:128 * (T + 1), :], pos[:, 8 * par:8 * (par + 1)],
                 s_out[par], [(s_v, csum_done[T])])
        prog.emit("vector", lambda v, par=par: v.tensor_tensor(
                      offf[0:1, :], offf[0:1, :],
                      pA[0:1, 2048 + 512 * par:2048 + 512 * par + 8], ALU.add),
                  [(s_mm, mcv)], s_v.inc(1))
        off_upd[T] = s_v.val
    prog.emit("gpsimd", lambda g: g.engine_nop(), [(s_out[0], 16 * 64), (s_out[1], 16 * 64)], None)
    return dict(allocs=allocs)


def emit_l1_ffn(nc, prog, emask_own, pos_own, x3_own, x3fm_own, gate_own, w1, b1, w2, b2,
                out, name="l1f"):
    st = nc.sbuf_tensor
    pt = nc.psum_tensor
    allocs = [
        st(name + "_x3fm", [128, 16384], BF16),
        st(name + "_yacc", [128, 16 * 1024], F32),
        st(name + "_hq", [128, 8 * 2048], BF16),
        st(name + "_w1q", [128, 8 * 1024], BF16),
        st(name + "_w2q", [128, 8 * 1024], BF16),
        st(name + "_stg", [128, 2 * 2048], F32),
        st(name + "_b1r", [1, 4096], BF16),
        st(name + "_b2r", [1, 1024], BF16),
        st(name + "_gate", [128, 16 * 8], F32),
        st(name + "_gv", [128, 16], F32),
        st(name + "_em", [128, 2 * 8], F32),
        st(name + "_pos", [128, 16 * 8], F32),
        st(name + "_ons", [128, 512], BF16),
        st(name + "_xr", [128, 2 * 1024], F32),
        st(name + "_sq", [128, 1024], F32),
        st(name + "_fs", [128, 16], F32),
        pt(name + "_pA", [128, 512 * 8], F32),
    ]
    hs = [a.__enter__() for a in allocs]
    (x3fm, yacc, hq, w1q, w2q, stg, b1r, b2r, gate, gv, em, pos, ons,
     xr, sq, fs, pA) = hs
    x3fm3 = x3fm[:].rearrange("p (j c) -> p j c", j=8)
    hq3 = hq[:].rearrange("p (i c) -> p i c", i=8)
    w1q3 = w1q[:].rearrange("p (j c) -> p j c", j=8)
    w2q3 = w2q[:].rearrange("p (i c) -> p i c", i=8)

    s_wld = [Sem(nc, name + f"_wld{i}") for i in range(2)]
    s_eld = [Sem(nc, name + f"_eld{i}") for i in range(2)]
    s_xld = [Sem(nc, name + f"_xld{i}") for i in range(2)]
    s_cast = Sem(nc, name + "_cast")
    s_gcast = Sem(nc, name + "_gcast")
    s_v = Sem(nc, name + "_v")
    s_s = Sem(nc, name + "_s")
    s_g = Sem(nc, name + "_g")
    s_mm = Sem(nc, name + "_mm")
    s_hev = Sem(nc, name + "_hev")
    s_mrg = Sem(nc, name + "_mrg")
    s_out = [Sem(nc, name + f"_o{i}") for i in range(2)]

    prog.emit("vector", lambda v: v.memset(ons[:], 1.0), (), s_v.inc(1))
    prog.dma("sync", x3fm[:].rearrange("p (j c) -> p j c", j=8),
             x3fm_own.rearrange("(j p) c -> p j c", p=128), s_wld[0])
    prog.dma("sync", gv[:], gate_own.rearrange("(t p) c -> p (t c)", p=128), s_wld[0])
    base_ld = s_wld[0].val

    gdone = {}
    for ti in range(16):
        par = ti % 2
        gw = [(s_wld[0], base_ld)]
        if ti >= 2:
            gw.append((s_v, gdone[ti - 2]))
        prog.dma("sync", em[:, 8 * par:8 * (par + 1)],
                 emask_own[128 * ti:128 * (ti + 1), :], s_eld[par], gw)
        prog.dma("sync", pos[:, 8 * ti:8 * (ti + 1)],
                 pos_own[128 * ti:128 * (ti + 1), :], s_eld[par])
        eldv = s_eld[par].val
        prog.emit("vector", lambda v, ti=ti: v.tensor_scalar(gate[:, 8 * ti:8 * (ti + 1)],
                                                             pos[:, 8 * ti:8 * (ti + 1)], 2048.0,
                                                             None, ALU.is_lt),
                  [(s_eld[par], eldv), (s_wld[0], base_ld)], None)
        prog.emit("vector", lambda v, ti=ti, par=par: v.tensor_tensor(
                      gate[:, 8 * ti:8 * (ti + 1)], gate[:, 8 * ti:8 * (ti + 1)],
                      em[:, 8 * par:8 * (par + 1)], ALU.mult), (), None)
        prog.emit("vector", lambda v, ti=ti: v.tensor_scalar(
                      gate[:, 8 * ti:8 * (ti + 1)], gate[:, 8 * ti:8 * (ti + 1)],
                      gv[:, ti:ti + 1], None, ALU.mult), (), s_v.inc(1))
        gdone[ti] = s_v.val
    gates_ok = s_v.val

    s_w2c = Sem(nc, name + "_w2c")
    n_h = 0
    n_mrg = 0
    w1c_n = 0
    w2c_n = 0
    prev_h_done = 0
    prev_o_done = 0
    prev_b_free = 0
    for e in range(8):
        bw = [(s_wld[0], base_ld)] if e == 0 else [(s_mm, prev_o_done)]
        prog.dma("sync", stg[0:1, 0:2048], b1[e].rearrange("(a c) -> a c", a=1)[:, 0:2048],
                 s_wld[1], bw)
        prog.dma("sync", stg[0:1, 2048:4096], b1[e].rearrange("(a c) -> a c", a=1)[:, 2048:4096],
                 s_wld[1])
        bldv = s_wld[1].val
        prog.emit("scalar", lambda s: s.copy(b1r[0:1, :], stg[0:1, 0:4096]),
                  [(s_wld[1], bldv)], s_cast.inc(1))
        b1cv = s_cast.val
        prog.dma("sync", stg[0:1, 0:1024], b2[e].rearrange("(a c) -> a c", a=1), s_wld[1],
                 [(s_cast, b1cv)])
        b2ldv = s_wld[1].val
        prog.emit("scalar", lambda s: s.copy(b2r[0:1, :], stg[0:1, 0:1024]),
                  [(s_wld[1], b2ldv)], s_cast.inc(1))
        b_ok = s_cast.val
        for q in range(4):
            for j in range(8):
                par = j % 2
                lw = [(s_cast, b_ok)]
                if w1c_n >= 2:
                    lw.append((s_gcast, w1c_n - 1))
                prog.dma("sync", stg[:, 2048 * par:2048 * par + 1024],
                         w1[e].rearrange("(c p) f -> c p f", p=128)[j][:, 1024 * q:1024 * (q + 1)],
                         s_wld[0], lw)
                wldv = s_wld[0].val
                cw = [(s_wld[0], wldv), (s_mm, prev_h_done)]
                prog.emit("gpsimd", lambda g, j=j:
                          g.tensor_copy(w1q3[:, j, :], stg[:, 2048 * (j % 2):2048 * (j % 2) + 1024]),
                          cw, s_gcast.inc(1))
                w1c_n += 1
            w1q_ok = s_gcast.val
            for i8 in range(8):
                for t4 in range(4):
                    bank = 6 + (n_h % 2)
                    hw = [(s_gcast, w1q_ok), (s_wld[0], base_ld)]
                    if n_h >= 2:
                        hw.append((s_hev, n_h - 1))
                    if i8 == 0 and t4 == 0 and n_mrg > 0:
                        hw.append((s_mrg, n_mrg))
                    Habs = 1024 * q + 128 * i8
                    prog.emit("tensor", lambda t, Habs=Habs, bank=bank:
                              t.matmul(pA[:, 512 * bank:512 * (bank + 1)],
                                       b1r[0:1, Habs:Habs + 128], ons[0:1, 0:512],
                                       start=True, stop=False),
                              hw, None)
                    for j in range(8):
                        prog.emit("tensor", lambda t, j=j, i8=i8, t4=t4, bank=bank:
                                  t.matmul(pA[:, 512 * bank:512 * (bank + 1)],
                                           w1q3[:, j, 128 * i8:128 * (i8 + 1)],
                                           x3fm3[:, j, 512 * t4:512 * (t4 + 1)],
                                           start=False, stop=(j == 7)),
                                  (), s_mm.inc(1) if j == 7 else None)
                    prog.emit("scalar", lambda s, i8=i8, t4=t4, bank=bank:
                              s.activation(hq3[:, i8, 512 * t4:512 * (t4 + 1)],
                                           pA[:, 512 * bank:512 * (bank + 1)], AF.Relu),
                              [(s_mm, s_mm.val)], s_hev.inc(1))
                    n_h += 1
            prev_h_done = s_mm.val
            h_ok = s_hev.val
            for i8 in range(8):
                par = i8 % 2
                lw2 = [(s_cast, b_ok)]
                if w2c_n >= 2:
                    lw2.append((s_w2c, w2c_n - 1))
                prog.dma("sync", stg[:, 2048 * par + 1024:2048 * par + 2048],
                         w2[e].rearrange("(c p) f -> c p f", p=128)[8 * q + i8],
                         s_wld[1], lw2)
                w2ldv = s_wld[1].val
                cw2 = [(s_wld[1], w2ldv), (s_mm, prev_o_done)]
                prog.emit("scalar", lambda s, par=par, i8=i8:
                          s.copy(w2q3[:, i8, :], stg[:, 2048 * par + 1024:2048 * par + 2048]),
                          cw2, s_w2c.inc(1))
                w2c_n += 1
            w2q_ok = s_w2c.val
            for t4 in range(4):
                ow = [(s_hev, h_ok), (s_w2c, w2q_ok), (s_v, gates_ok)]
                if n_mrg >= 8:
                    ow.append((s_mrg, n_mrg))
                first = True
                for i8 in range(8):
                    for sub in range(4):
                        for h2 in range(2):
                            bank = sub * 2 + h2
                            if q == 0 and i8 == 0:
                                prog.emit("tensor", lambda t, h2=h2, bank=bank:
                                          t.matmul(pA[:, 512 * bank:512 * (bank + 1)],
                                                   ons[0:1, 0:128], b2r[0:1, 512 * h2:512 * (h2 + 1)],
                                                   start=True, stop=False),
                                          ow if first else (), None)
                                first = False
                            prog.emit("tensor", lambda t, i8=i8, t4=t4, sub=sub, h2=h2, bank=bank, q=q:
                                      t.matmul(pA[:, 512 * bank:512 * (bank + 1)],
                                               hq3[:, i8, 512 * t4 + 128 * sub:512 * t4 + 128 * (sub + 1)],
                                               w2q3[:, i8, 512 * h2:512 * (h2 + 1)],
                                               start=(i8 == 0 and q != 0), stop=(i8 == 7)),
                                      ow if first else (),
                                      s_mm.inc(1) if (i8 == 7 and sub == 3 and h2 == 1) else None)
                            first = False
                omv = s_mm.val
                for sub in range(4):
                    for h2 in range(2):
                        bank = sub * 2 + h2
                        s16 = 4 * t4 + sub
                        dst = yacc[:, 1024 * s16 + 512 * h2:1024 * s16 + 512 * (h2 + 1)]
                        gap = gate[:, 8 * s16 + e:8 * s16 + e + 1]
                        if e == 0 and q == 0:
                            prog.emit("vector", lambda v, dst=dst, gap=gap, bank=bank:
                                      v.tensor_scalar(dst, pA[:, 512 * bank:512 * (bank + 1)],
                                                      gap, None, ALU.mult),
                                      [(s_mm, omv)] if (sub == 0 and h2 == 0) else (), s_mrg.inc(1))
                        else:
                            prog.emit("vector", lambda v, dst=dst, gap=gap, bank=bank:
                                      v.scalar_tensor_tensor(dst, pA[:, 512 * bank:512 * (bank + 1)],
                                                             gap, dst, ALU.mult, ALU.add),
                                      [(s_mm, omv)] if (sub == 0 and h2 == 0) else (), s_mrg.inc(1))
                        n_mrg += 1
            prev_o_done = s_mm.val
    mrg_all = s_mrg.val

    fin_ev = []
    for s16 in range(16):
        par = s16 % 2
        xw = [(s_mrg, mrg_all)]
        if s16 >= 2:
            xw.append(fin_ev[s16 - 2])
        prog.dma("sync", xr[:, 1024 * par:1024 * (par + 1)], x3_own[128 * s16:128 * (s16 + 1), :],
                 s_xld[par], xw)
        xldv = s_xld[par].val
        ys = yacc[:, 1024 * s16:1024 * (s16 + 1)]
        prog.emit("vector", lambda v, ys=ys, par=par: v.tensor_tensor(
                      ys, ys, xr[:, 1024 * par:1024 * (par + 1)], ALU.add),
                  [(s_xld[par], xldv), (s_mrg, mrg_all)], None)
        prog.emit("vector", lambda v, ys=ys: v.reduce_sum(fs[:, 0:1], ys, axis=AX.X), (), None)
        prog.emit("vector", lambda v: v.tensor_scalar(fs[:, 0:1], fs[:, 0:1], 1.0 / 1024, None, ALU.mult),
                  (), None)
        prog.emit("vector", lambda v, ys=ys: v.tensor_scalar(ys, ys, fs[:, 0:1], None, ALU.subtract),
                  (), None)
        prog.emit("vector", lambda v, ys=ys: v.tensor_tensor(sq[:], ys, ys, ALU.mult), (), None)
        prog.emit("vector", lambda v: v.reduce_sum(fs[:, 1:2], sq[:], axis=AX.X), (), None)
        prog.emit("vector", lambda v: v.tensor_scalar(fs[:, 1:2], fs[:, 1:2], 1.0 / 1024, 1e-5,
                                                      ALU.mult, ALU.add), (), s_v.inc(1))
        vv = s_v.val
        prog.emit("scalar", lambda s: s.activation(fs[:, 2:3], fs[:, 1:2], AF.Ln), [(s_v, vv)], None)
        prog.emit("scalar", lambda s: s.activation(fs[:, 3:4], fs[:, 2:3], AF.Exp, scale=-0.5),
                  (), s_s.inc(1))
        rv = s_s.val
        ow2 = [(s_s, rv)]
        if s16 >= 2:
            ow2.append((s_out[0], 16 * (s16 - 1)))
        prog.emit("vector", lambda v, ys=ys: v.tensor_scalar(ys, ys, fs[:, 3:4], None, ALU.mult),
                  ow2, s_v.inc(1))
        fin_ev.append((s_v, s_v.val))
        prog.dma("gpsimd", out[128 * s16:128 * (s16 + 1), :], ys, s_out[0], [fin_ev[-1]])
    prog.emit("gpsimd", lambda g: g.engine_nop(), [(s_out[0], 16 * 16)], None)
    return dict(allocs=allocs)


from concourse.bass_utils import run_bass_kernel_spmd


def kernel(**inputs):
    import sys as _sys, time as _time
    _t0 = _time.time()
    def _log(msg):
        print(f"[kernel +{_time.time()-_t0:7.1f}s] {msg}", file=_sys.stderr, flush=True)

    X = np.ascontiguousarray(inputs["L0_attn_x"].reshape(16384, 1024))
    cores = list(range(8))

    nc = bass.Bass("TRN2", target_bir_lowering=False)
    xs = nc.dram_tensor("x_scat", [2048, 1024], F32, kind="ExternalInput").ap()
    xc = nc.dram_tensor("x_cont", [2048, 1024], F32, kind="ExternalInput").ap()
    w = [nc.dram_tensor(n, [1024, 1024], F32, kind="ExternalInput").ap() for n in ("Wq", "Wk", "Wv")]
    b = [nc.dram_tensor(n, [1024], F32, kind="ExternalInput").ap() for n in ("bq", "bk", "bv")]
    q = nc.dram_tensor("q_fm", [1024, 2048], BF16, kind="ExternalOutput").ap()
    k = nc.dram_tensor("k_fm", [1024, 2048], BF16, kind="ExternalOutput").ap()
    v = nc.dram_tensor("v_tm", [2048, 1024], BF16, kind="ExternalOutput").ap()
    prog = Prog(nc)
    emit_l0_proj(nc, prog, xs, xc, w, b, q, k, v)
    prog.run()
    in_maps = []
    for c in cores:
        rows = l0_core_rows(c)
        in_maps.append({"x_scat": X[rows], "x_cont": X[2048 * c:2048 * (c + 1)],
                        "Wq": inputs["L0_attn_Wq"], "Wk": inputs["L0_attn_Wk"],
                        "Wv": inputs["L0_attn_Wv"], "bq": inputs["L0_attn_bq"],
                        "bk": inputs["L0_attn_bk"], "bv": inputs["L0_attn_bv"]})
    _log("start launch1 l0_proj")
    res = run_bass_kernel_spmd(nc, in_maps, core_ids=cores)
    q_own = [res.results[c]["q_fm"] for c in cores]
    k_g = np.concatenate([res.results[c]["k_fm"] for c in cores], axis=0)
    v_g = np.concatenate([res.results[c]["v_tm"] for c in cores], axis=0)

    nc = bass.Bass("TRN2", target_bir_lowering=False)
    qi = nc.dram_tensor("q_own", [1024, 2048], BF16, kind="ExternalInput").ap()
    ki = nc.dram_tensor("k_fm_g", [8192, 2048], BF16, kind="ExternalInput").ap()
    vi = nc.dram_tensor("v_tm_g", [16384, 1024], BF16, kind="ExternalInput").ap()
    mi = nc.dram_tensor("msk", [128, 1024], BF16, kind="ExternalInput").ap()
    xri = nc.dram_tensor("x_res", [2048, 1024], F32, kind="ExternalInput").ap()
    x1o = nc.dram_tensor("x1", [2048, 1024], F32, kind="ExternalOutput").ap()
    prog = Prog(nc)
    emit_l0_attn(nc, prog, qi, ki, vi, mi, xri, x1o)
    prog.run()
    in_maps = []
    for c in cores:
        rows = l0_core_rows(c)
        in_maps.append({"q_own": q_own[c], "k_fm_g": k_g, "v_tm_g": v_g,
                        "msk": l0_masks(c), "x_res": X[rows]})
    _log("start launch2 l0_attn")
    res = run_bass_kernel_spmd(nc, in_maps, core_ids=cores)
    x1_own = [res.results[c]["x1"] for c in cores]

    nc = bass.Bass("TRN2", target_bir_lowering=False)
    x1i = nc.dram_tensor("x1", [2048, 1024], F32, kind="ExternalInput").ap()
    wg1 = nc.dram_tensor("wg1", [1024, 256], F32, kind="ExternalInput").ap()
    bg1 = nc.dram_tensor("bg1", [256], F32, kind="ExternalInput").ap()
    wg2 = nc.dram_tensor("wg2", [256, 8], F32, kind="ExternalInput").ap()
    win = nc.dram_tensor("w_in", [8, 1024, 2048], F32, kind="ExternalInput").ap()
    wout = nc.dram_tensor("w_out", [8, 2048, 1024], F32, kind="ExternalInput").ap()
    x2o = nc.dram_tensor("x2", [2048, 1024], F32, kind="ExternalOutput").ap()
    x2fo = nc.dram_tensor("x2fm", [1024, 2048], BF16, kind="ExternalOutput").ap()
    prog = Prog(nc)
    emit_l0_ffn(nc, prog, x1i, wg1, bg1, wg2, win, wout, x2o, x2fo)
    prog.run()
    in_maps = []
    for c in cores:
        in_maps.append({"x1": x1_own[c], "wg1": inputs["L0_ffn_wg1"], "bg1": inputs["L0_ffn_bg1"],
                        "wg2": inputs["L0_ffn_wg2"], "w_in": inputs["L0_ffn_w_in"],
                        "w_out": inputs["L0_ffn_w_out"]})
    _log("start launch3 l0_ffn")
    res = run_bass_kernel_spmd(nc, in_maps, core_ids=cores)
    x2fm_g = np.concatenate([res.results[c]["x2fm"] for c in cores], axis=0)
    X2_seq = np.empty((16384, 1024), np.float32)
    for c in cores:
        X2_seq[l0_core_rows(c)] = res.results[c]["x2"]

    nc = bass.Bass("TRN2", target_bir_lowering=False)
    d = {}
    d["x2fm_g"] = nc.dram_tensor("x2fm_g", [8192, 2048], BF16, kind="ExternalInput").ap()
    d["projT"] = nc.dram_tensor("projT", [1024, 384], F32, kind="ExternalInput").ap()
    d["lr_exp"] = nc.dram_tensor("lr_exp", [1024, 128], F32, kind="ExternalInput").ap()
    d["bias_row"] = nc.dram_tensor("bias_row", [1, 128], F32, kind="ExternalInput").ap()
    d["l1c"] = nc.dram_tensor("l1c", [128, 960], F32, kind="ExternalInput").ap()
    d["W1_0"] = nc.dram_tensor("W1_0", [128, 256], F32, kind="ExternalInput").ap()
    d["b1_0"] = nc.dram_tensor("b1_0", [1, 512], F32, kind="ExternalInput").ap()
    d["W2_0"] = nc.dram_tensor("W2_0", [128, 256], F32, kind="ExternalInput").ap()
    d["W2T_0"] = nc.dram_tensor("W2T_0", [64, 512], F32, kind="ExternalInput").ap()
    d["b2_0"] = nc.dram_tensor("b2_0", [1, 128], F32, kind="ExternalInput").ap()
    yfm_o = nc.dram_tensor("yfm", [128, 16384], BF16, kind="ExternalOutput").ap()
    sts_o = nc.dram_tensor("stats", [128, 512], F32, kind="ExternalOutput").ap()
    prog = Prog(nc)
    emit_l1_attn(nc, prog, d["x2fm_g"], d["projT"], d["lr_exp"], d["bias_row"], d["l1c"],
                 d["W1_0"], d["b1_0"], d["W2_0"], d["W2T_0"], d["b2_0"], yfm_o, sts_o)
    prog.run()
    in_maps = []
    for c in cores:
        ins = l1_make_inputs(c, inputs["L1_attn_Wq"], inputs["L1_attn_Wk"], inputs["L1_attn_Wv"],
                             inputs["L1_attn_lr_weight"], inputs["L1_attn_lr_bias"],
                             inputs["L1_attn_ttt_norm_weight"], inputs["L1_attn_ttt_norm_bias"],
                             inputs["L1_attn_W1"], inputs["L1_attn_b1"],
                             inputs["L1_attn_W2"], inputs["L1_attn_b2"])
        in_maps.append({"x2fm_g": x2fm_g, "projT": ins["projT"], "lr_exp": ins["lr_exp"],
                        "bias_row": ins["bias_row"], "l1c": ins["l1c"], "W1_0": ins["W1_0"],
                        "b1_0": ins["b1_0"], "W2_0": ins["W2_0"], "W2T_0": ins["W2T_0"],
                        "b2_0": ins["b2_0"]})
    _log("start launch4 l1_attn")
    res = run_bass_kernel_spmd(nc, in_maps, core_ids=cores)
    y_fm = np.concatenate([res.results[c]["yfm"] for c in cores], axis=0)
    y_tm = np.ascontiguousarray(y_fm.T)

    nc = bass.Bass("TRN2", target_bir_lowering=False)
    yti = nc.dram_tensor("y_tm", [2048, 1024], BF16, kind="ExternalInput").ap()
    pnw = nc.dram_tensor("pnw", [128, 1024], F32, kind="ExternalInput").ap()
    pnb = nc.dram_tensor("pnb", [128, 1024], F32, kind="ExternalInput").ap()
    woi = nc.dram_tensor("WoT", [1024, 1024], F32, kind="ExternalInput").ap()
    x2i = nc.dram_tensor("x2_own", [2048, 1024], F32, kind="ExternalInput").ap()
    rwi = nc.dram_tensor("rw", [1024, 8], F32, kind="ExternalInput").ap()
    x3o = nc.dram_tensor("x3", [2048, 1024], F32, kind="ExternalOutput").ap()
    x3fo = nc.dram_tensor("x3fm", [1024, 2048], BF16, kind="ExternalOutput").ap()
    emo = nc.dram_tensor("emask", [2048, 8], F32, kind="ExternalOutput").ap()
    gto = nc.dram_tensor("gate", [2048, 1], F32, kind="ExternalOutput").ap()
    prog = Prog(nc)
    emit_l1_post(nc, prog, yti, pnw, pnb, woi, x2i, rwi, x3o, x3fo, emo, gto)
    prog.run()
    pnw_rep = np.ascontiguousarray(np.broadcast_to(inputs["L1_attn_post_norm_weight"][None, :], (128, 1024)))
    pnb_rep = np.ascontiguousarray(np.broadcast_to(inputs["L1_attn_post_norm_bias"][None, :], (128, 1024)))
    WoT = np.ascontiguousarray(inputs["L1_attn_Wo"].T)
    in_maps = []
    for c in cores:
        in_maps.append({"y_tm": y_tm[2048 * c:2048 * (c + 1)], "pnw": pnw_rep, "pnb": pnb_rep,
                        "WoT": WoT, "x2_own": X2_seq[2048 * c:2048 * (c + 1)],
                        "rw": inputs["L1_ffn_router_w"]})
    _log("start launch5 l1_post")
    res = run_bass_kernel_spmd(nc, in_maps, core_ids=cores)
    x3_own = [res.results[c]["x3"] for c in cores]
    x3fm_own = [res.results[c]["x3fm"] for c in cores]
    gate_own = [res.results[c]["gate"] for c in cores]
    emask_own = [res.results[c]["emask"] for c in cores]
    emask_all = np.concatenate(emask_own, axis=0)

    nc = bass.Bass("TRN2", target_bir_lowering=False)
    emi = nc.dram_tensor("emask_all", [16384, 8], F32, kind="ExternalInput").ap()
    pso = nc.dram_tensor("pos_all", [16384, 8], F32, kind="ExternalOutput").ap()
    prog = Prog(nc)
    emit_cumsum(nc, prog, emi, pso)
    prog.run()
    _log("start launch6 cumsum")
    res = run_bass_kernel_spmd(nc, [{"emask_all": emask_all}], core_ids=[0])
    pos_all = res.results[0]["pos_all"]

    nc = bass.Bass("TRN2", target_bir_lowering=False)
    emi = nc.dram_tensor("emask_own", [2048, 8], F32, kind="ExternalInput").ap()
    psi = nc.dram_tensor("pos_own", [2048, 8], F32, kind="ExternalInput").ap()
    x3i = nc.dram_tensor("x3_own", [2048, 1024], F32, kind="ExternalInput").ap()
    x3fi = nc.dram_tensor("x3fm_own", [1024, 2048], BF16, kind="ExternalInput").ap()
    gti = nc.dram_tensor("gate_own", [2048, 1], F32, kind="ExternalInput").ap()
    w1i = nc.dram_tensor("w1", [8, 1024, 4096], F32, kind="ExternalInput").ap()
    b1i = nc.dram_tensor("b1", [8, 4096], F32, kind="ExternalInput").ap()
    w2i = nc.dram_tensor("w2", [8, 4096, 1024], F32, kind="ExternalInput").ap()
    b2i = nc.dram_tensor("b2", [8, 1024], F32, kind="ExternalInput").ap()
    oto = nc.dram_tensor("out", [2048, 1024], F32, kind="ExternalOutput").ap()
    prog = Prog(nc)
    emit_l1_ffn(nc, prog, emi, psi, x3i, x3fi, gti, w1i, b1i, w2i, b2i, oto)
    with nc.allow_non_contiguous_dma(reason="gate_own column load, 8KB once"):
        prog.run()
    in_maps = []
    for c in cores:
        in_maps.append({"emask_own": emask_own[c], "pos_own": pos_all[2048 * c:2048 * (c + 1)],
                        "x3_own": x3_own[c], "x3fm_own": x3fm_own[c], "gate_own": gate_own[c],
                        "w1": inputs["L1_ffn_w1"], "b1": inputs["L1_ffn_b1"],
                        "w2": inputs["L1_ffn_w2"], "b2": inputs["L1_ffn_b2"]})
    _log("start launch7 l1_ffn")
    res = run_bass_kernel_spmd(nc, in_maps, core_ids=cores)
    out = np.concatenate([res.results[c]["out"] for c in cores], axis=0)
    return out.reshape(1, 16384, 1024).astype(np.float32, copy=False)
